# version 02
# baseline (speedup 1.0000x reference)
import numpy as np
import contextlib
import concourse.bass as bass
import concourse.mybir as mybir
from concourse.bass_utils import run_bass_kernel_spmd

F32 = mybir.dt.float32
BF16 = mybir.dt.bfloat16
I32 = mybir.dt.int32
ALU = mybir.AluOpType
AF = mybir.ActivationFunctionType
AX = mybir.AxisListType

NSEQ, S, D, H, DH, VD = 4, 2048, 1024, 8, 64, 512
NTT, NIC, NJT, NPAIR = 16, 4, 16, 4
EPS = 1e-5
EPS_S = 2e-5


def _finish(nc, ops, st):
    with nc.Block() as blk:
        @blk.sync
        def _(E):
            for f in ops["sync"]:
                f(E)

        @blk.gpsimd
        def _(E):
            for f in ops["gpsimd"]:
                f(E)

        @blk.vector
        def _(E):
            for f in ops["vector"]:
                f(E)

        @blk.scalar
        def _(E):
            for f in ops["scalar"]:
                f(E)

        @blk.tensor
        def _(E):
            for f in ops["tensor"]:
                f(E)
    return nc, st


def build_program(nseq=NSEQ, phases="ABCD", debug_dump=()):
    main_ph, _, sub_ph = phases.partition(".")
    nc = bass.Bass("TRN2", target_bir_lowering=False)

    x_tok = nc.dram_tensor("x_tok", [NSEQ * S, D], F32, kind="ExternalInput").ap()
    x_feat = nc.dram_tensor("x_feat", [NSEQ, D, S], F32, kind="ExternalInput").ap()
    wqk = nc.dram_tensor("wqk", [D, D], F32, kind="ExternalInput").ap()
    wv = nc.dram_tensor("wv", [D, VD], F32, kind="ExternalInput").ap()
    wout = nc.dram_tensor("wout", [VD, D], F32, kind="ExternalInput").ap()
    wffn = nc.dram_tensor("wffn", [D, D], F32, kind="ExternalInput").ap()
    bqk = nc.dram_tensor("bqk", [1, D], F32, kind="ExternalInput").ap()
    bv = nc.dram_tensor("bv", [1, VD], F32, kind="ExternalInput").ap()
    bout = nc.dram_tensor("bout", [1, D], F32, kind="ExternalInput").ap()
    gam = nc.dram_tensor("gam", [1, D], F32, kind="ExternalInput").ap()
    bet = nc.dram_tensor("bet", [1, D], F32, kind="ExternalInput").ap()
    fbias = nc.dram_tensor("fbias", [1, D], F32, kind="ExternalInput").ap()
    out = nc.dram_tensor("out", [NSEQ * S, D], F32, kind="ExternalOutput").ap()
    zscr = nc.dram_tensor("zscr", [NSEQ, 32 * 512], F32, kind="Internal").ap()
    rzscr = nc.dram_tensor("rzscr", [NSEQ, 32 * 512], F32, kind="Internal").ap()
    dbg = {}
    for name in debug_dump:
        shapes = {"q": [128, 4 * S], "k": [128, 4 * S], "v": [128, NJT * VD],
                  "obuf": [128, 4 * S], "zrow": [1, 32 * 512], "stats": [128, 6 * NTT],
                  "rows": [1, 4 * S], "rb": [128, S], "bnb": [128, NTT * 12], "zq": [32, 512], "rzq": [32, 512], "negm": [128, 4], "obun": [128, 4 * S],
                  "zbuf": [128, NTT * D], "sbuf": [128, NTT * D], "x1": [128, NTT * D],
                  "fbuf": [128, NTT * D], "xd": [128, D], "xdt": [128, D],
                  "wq": [128, 8 * D], "w0": [128, D]}
        dbg[name] = nc.dram_tensor("dbg_" + name, shapes[name], F32, kind="ExternalOutput").ap()

    OUTT0 = out.rearrange("(q p) n -> q p n", p=128)

    st = contextlib.ExitStack()
    e = st.enter_context

    wqk_s = e(nc.sbuf_tensor("wqk_s", [128, 8, D], BF16))
    wv_s = e(nc.sbuf_tensor("wv_s", [128, 8, VD], BF16))
    wout_s = e(nc.sbuf_tensor("wout_s", [128, 4, D], BF16))
    wffnq = e(nc.sbuf_tensor("wffnq", [128, 8, D], BF16))

    U1 = e(nc.sbuf_tensor("U1", [128, 16384], BF16))
    U2 = e(nc.sbuf_tensor("U2", [128, 16384], BF16))
    U3 = e(nc.sbuf_tensor("U3", [128, 16384], BF16))
    q_v = U1[:, 0:8192].rearrange("p (c n) -> p c n", c=4)
    k_v = U1[:, 8192:16384].rearrange("p (c n) -> p c n", c=4)
    zbuf = U1[:].rearrange("p (t n) -> p t n", t=NTT)
    fbuf = zbuf
    v_v = U2[:, 0:8192].rearrange("p (t n) -> p t n", t=NJT)
    obuf = U2[:, 8192:16384].rearrange("p (c n) -> p c n", c=4)
    x1buf = U2[:].rearrange("p (t n) -> p t n", t=NTT)
    xa_v = U3[:, 0:4096].bitcast(F32).rearrange("p (a n) -> p a n", a=2)
    wfst = U3[:, 0:4096].bitcast(F32).rearrange("p (a n) -> p a n", a=2)
    xfb = U3[:, 4096:12288].rearrange("p (a c n) -> p a c n", a=2, c=8)
    eg_v = U3[:, 12288:16384].rearrange("p (a n) -> p a n", a=2)
    sb_v = U3[:].rearrange("p (t n) -> p t n", t=NTT)

    identB = e(nc.sbuf_tensor("identB", [128, 128], BF16))
    identF = e(nc.sbuf_tensor("identF", [128, 128], F32))
    iota_a = U1[:, 0:256].bitcast(I32)
    iota_b = U1[:, 256:512].bitcast(I32)
    gamb = e(nc.sbuf_tensor("gamb", [128, D], BF16))
    betb = e(nc.sbuf_tensor("betb", [128, D], BF16))
    ones_col = e(nc.sbuf_tensor("ones_col", [128, 1], BF16))
    ones_rb = e(nc.sbuf_tensor("ones_rb", [1, 128], BF16))
    ones_rf = e(nc.sbuf_tensor("ones_rf", [1, 128], F32))
    oneblkA = e(nc.sbuf_tensor("oneblkA", [1, 128], F32))
    oneblkB = e(nc.sbuf_tensor("oneblkB", [1, 128], F32))

    bqk_r = e(nc.sbuf_tensor("bqk_r", [1, D], BF16))
    bv_r = e(nc.sbuf_tensor("bv_r", [1, VD], BF16))
    bout_r = e(nc.sbuf_tensor("bout_r", [1, D], BF16))
    fbias_r = e(nc.sbuf_tensor("fbias_r", [1, D], BF16))
    cwqk_r = e(nc.sbuf_tensor("cwqk_r", [1, D], BF16))
    cwv_r = e(nc.sbuf_tensor("cwv_r", [1, VD], BF16))

    murow = e(nc.sbuf_tensor("murow", [1, S], BF16))
    rinvrow = e(nc.sbuf_tensor("rinvrow", [1, S], BF16))
    RBP = e(nc.sbuf_tensor("RBP", [128, 2048], BF16))
    rb_v = RBP[:]
    msb_v = RBP[:, 0:1024].rearrange("p (a n) -> p a n", a=2)

    ZR = e(nc.sbuf_tensor("ZR", [1, 2048], F32))
    r0row = ZR[0:1, :].bitcast(BF16)[:, 0:2048]
    rzrow = e(nc.sbuf_tensor("rzrow", [1, 1024], F32))

    MISC = e(nc.sbuf_tensor("MISC", [128, 2048], F32))
    zgt = MISC[0:32, 0:512]
    rzg = MISC[0:32, 512:1024]
    xtok_v = MISC[:, 0:1024].bitcast(BF16).rearrange("p (a n) -> p a n", a=2)
    row_tmp = MISC[0:1, 1024:2048]
    xdt_v = MISC[:, 1024:1536].bitcast(BF16).rearrange("p (c n) -> p c n", c=8)

    bnb = e(nc.sbuf_tensor("bnb", [128, NTT, 12], F32))
    negmu0 = e(nc.sbuf_tensor("negmu0", [128, NTT], F32))
    r0 = e(nc.sbuf_tensor("r0", [128, NTT], F32))
    rinv0 = e(nc.sbuf_tensor("rinv0", [128, NTT], F32))
    negmuz = e(nc.sbuf_tensor("negmuz", [128, NTT], F32))
    rz_s = e(nc.sbuf_tensor("rz_s", [128, NTT], F32))
    negmus = e(nc.sbuf_tensor("negmus", [128, NTT], F32))
    rs_s = e(nc.sbuf_tensor("rs_s", [128, NTT], F32))
    negmu1 = e(nc.sbuf_tensor("negmu1", [128, NTT], F32))
    r1_s = e(nc.sbuf_tensor("r1_s", [128, NTT], F32))
    negmuf = e(nc.sbuf_tensor("negmuf", [128, NTT], F32))
    rf_s = e(nc.sbuf_tensor("rf_s", [128, NTT], F32))
    stat_t1 = e(nc.sbuf_tensor("stat_t1", [128, NTT], F32))
    stat_t2 = e(nc.sbuf_tensor("stat_t2", [128, NTT], F32))
    stat_t3 = e(nc.sbuf_tensor("stat_t3", [128, NTT], F32))
    stat_t5 = e(nc.sbuf_tensor("stat_t5", [128, NTT], F32))
    stat_t6 = e(nc.sbuf_tensor("stat_t6", [128, NTT], F32))
    stat_t7 = e(nc.sbuf_tensor("stat_t7", [128, NTT], F32))
    stat_t8 = e(nc.sbuf_tensor("stat_t8", [128, NTT], F32))
    stat_t4 = e(nc.sbuf_tensor("stat_t4", [128, NTT, 4], F32))
    cm_s = e(nc.sbuf_tensor("cm_s", [128, 8], F32))
    cmp_s = e(nc.sbuf_tensor("cmp_s", [128, 4], F32))
    gm_s = e(nc.sbuf_tensor("gm_s", [4, 1], F32))
    negmp = e(nc.sbuf_tensor("negmp", [4, 1], F32))
    negmrow = e(nc.sbuf_tensor("negmrow", [1, 4], F32))
    negmb = e(nc.sbuf_tensor("negmb", [128, 4], F32))

    t2_s = e(nc.sbuf_tensor("t2_s", [128, D], BF16))
    uTB_s = RBP[:, 1024:2048]
    rzgam = RBP[:, 0:1024]
    wu_s = e(nc.sbuf_tensor("wu_s", [128, D], F32))
    qi_v = wu_s[:].bitcast(I32)
    au_s = e(nc.sbuf_tensor("au_s", [128, D], BF16))
    mk1_s = e(nc.sbuf_tensor("mk1_s", [128, D], BF16))
    mk2_s = e(nc.sbuf_tensor("mk2_s", [128, D], BF16))
    junk_g = e(nc.sbuf_tensor("junk_g", [1, 8], F32))
    junk_v = e(nc.sbuf_tensor("junk_v", [1, 8], F32))
    junk_s = e(nc.sbuf_tensor("junk_s", [1, 8], F32))
    eps_c = e(nc.sbuf_tensor("eps_c", [128, 1], F32))
    eps2_c = e(nc.sbuf_tensor("eps2_c", [128, 1], F32))
    amax_s = e(nc.sbuf_tensor("amax_s", [128, 32], F32))
    ssc_s = e(nc.sbuf_tensor("ssc_s", [128, 32], F32))
    s8c_s = e(nc.sbuf_tensor("s8c_s", [128, 32], F32))
    rssc_s = e(nc.sbuf_tensor("rssc_s", [128, 32], F32))
    outst = U3[:, 0:4096].bitcast(F32)[:, 0:1024]

    ps_a = e(nc.psum_tensor("ps_a", [128, 2048], F32))
    ps_b = e(nc.psum_tensor("ps_b", [128, 512], F32))
    ps_c = e(nc.psum_tensor("ps_c", [128, 512], F32))
    ps_d = e(nc.psum_tensor("ps_d", [128, 1024], F32))
    ps_a_bf = ps_a[:].bitcast(BF16)

    sem_names = ["sdma", "gdma", "sv", "sa", "stn", "sgc", "ssy", "sout"]
    sems = {n: e(nc.semaphore(n)) for n in sem_names}
    cnt = {n: 0 for n in sem_names}
    ops = {k: [] for k in ["sync", "gpsimd", "vector", "scalar", "tensor"]}

    def inc(name, amt=1):
        cnt[name] += amt
        return cnt[name]

    def op(eng, fn, sem=None, amt=1):
        if sem is None:
            ops[eng].append(lambda E, f=fn: f(E))
            return None
        tgt = inc(sem, amt)
        semh = sems[sem]
        ops[eng].append(lambda E, f=fn, sh=semh, a=amt: f(E).then_inc(sh, a))
        return tgt

    def dma(eng, dst, src, sem=None):
        sem = sem or ("gdma" if eng == "gpsimd" else "sdma")
        return op(eng, lambda E, d=dst, s=src: E.dma_start(out=d, in_=s), sem=sem, amt=16)

    def wait(eng, sem, val):
        semh = sems[sem]
        ops[eng].append(lambda E, sh=semh, v=val: E.wait_ge(sh, v))

    ENG_SEM = {"sync": "ssy", "gpsimd": "sgc", "vector": "sv", "scalar": "sa", "tensor": "stn"}

    def barrier():
        wait("sync", "sdma", cnt["sdma"])
        wait("sync", "sout", cnt["sout"])
        wait("gpsimd", "gdma", cnt["gdma"])
        marks = {}
        marks["sync"] = op("sync", lambda E: E.nop(), sem="ssy")
        marks["gpsimd"] = op("gpsimd", lambda E: E.memset(junk_g[:], 0.0), sem="sgc")
        marks["vector"] = op("vector", lambda E: E.tensor_copy(junk_v[:], junk_g[:]), sem="sv")
        marks["scalar"] = op("scalar", lambda E: E.activation(junk_s[:], junk_v[:], AF.Square), sem="sa")
        marks["tensor"] = op("tensor", lambda E: E.matmul(
            ps_b[0:1, 0:1], ones_rf[0:1, 0:1], ones_rf[0:1, 0:1],
            start=True, stop=True), sem="stn")
        for eng in ENG_SEM:
            for other, sn in ENG_SEM.items():
                if other != eng:
                    wait(eng, sn, marks[other])

    for dst, src in [(bqk_r, bqk), (bv_r, bv), (bout_r, bout), (fbias_r, fbias)]:
        dma("gpsimd", dst[:], src)
    dma("gpsimd", wqk_s[:], wqk.rearrange("(kc p) n -> p kc n", p=128))
    dma("gpsimd", wv_s[:], wv.rearrange("(kc p) n -> p kc n", p=128))
    dma("gpsimd", wout_s[:], wout.rearrange("(kc p) n -> p kc n", p=128))
    op("gpsimd", lambda E: E.iota(iota_a[:], pattern=[[1, 128]], base=0, channel_multiplier=0))
    op("gpsimd", lambda E: E.iota(iota_b[:], pattern=[[0, 128]], base=0, channel_multiplier=1))
    op("gpsimd", lambda E: E.memset(ones_col[:], 1.0))
    op("gpsimd", lambda E: E.memset(ones_rb[:], 1.0))
    op("gpsimd", lambda E: E.memset(ones_rf[:], 1.0))
    op("gpsimd", lambda E: E.memset(oneblkA[:, 0:64], 1.0))
    op("gpsimd", lambda E: E.memset(oneblkA[:, 64:128], 0.0))
    op("gpsimd", lambda E: E.memset(oneblkB[:, 0:64], 0.0))
    op("gpsimd", lambda E: E.memset(oneblkB[:, 64:128], 1.0))
    op("gpsimd", lambda E: E.memset(eps_c[:], EPS))
    op("gpsimd", lambda E: E.memset(eps2_c[:], EPS_S))
    barrier()

    if main_ph == "S0":
        dma("sync", OUTT0[0], MISC[:, 0:1024], sem="sout")
        wait("sync", "sout", cnt["sout"])
        return _finish(nc, ops, st)

    op("vector", lambda E: E.tensor_tensor(identB[:], iota_a[:], iota_b[:], ALU.is_equal))
    op("vector", lambda E: E.tensor_tensor(identF[:], iota_a[:], iota_b[:], ALU.is_equal), sem="sv")
    if main_ph == "Sid":
        dma("sync", OUTT0[0], MISC[:, 0:1024], sem="sout")
        wait("sync", "sout", cnt["sout"])
        return _finish(nc, ops, st)

    for dst, src in [(gamb, gam), (betb, bet)]:
        tl = dma("sync", row_tmp, src)
        wait("tensor", "sdma", tl)
        t = None
        for hh in range(2):
            t = op("tensor", lambda E, hh=hh: E.matmul(
                ps_a[:, hh * 512:(hh + 1) * 512], ones_rf[:],
                row_tmp[:, hh * 512:(hh + 1) * 512], start=True, stop=True),
                sem=("stn" if hh == 1 else None))
        wait("vector", "stn", t)
        tv = op("vector", lambda E, d=dst: E.tensor_copy(d[:], ps_a[:, 0:1024]), sem="sv")
        wait("tensor", "sv", tv)
        wait("sync", "sv", tv)
    if main_ph == "Sgb":
        dma("sync", OUTT0[0], MISC[:, 0:1024], sem="sout")
        wait("sync", "sout", cnt["sout"])
        return _finish(nc, ops, st)

    def colsum(dst_row, w_tile, width):
        nh = width // 512
        t = None
        for hh in range(nh):
            for kc in range(8):
                last = (hh == nh - 1 and kc == 7)
                t = op("tensor", lambda E, hh=hh, kc=kc, w=w_tile: E.matmul(
                    ps_a[0:1, hh * 512:(hh + 1) * 512], ones_col[:],
                    w[:, kc, hh * 512:(hh + 1) * 512], start=(kc == 0), stop=(kc == 7)),
                    sem=("stn" if last else None))
        wait("vector", "stn", t)
        tv = op("vector", lambda E, d=dst_row, n=nh: E.tensor_copy(d[:], ps_a[0:1, 0:512 * n]), sem="sv")
        wait("tensor", "sv", tv)

    colsum(cwqk_r, wqk_s, D)
    colsum(cwv_r, wv_s, VD)
    if main_ph == "S1":
        dma("sync", OUTT0[0], MISC[:, 0:1024], sem="sout")
        wait("sync", "sout", cnt["sout"])
        return _finish(nc, ops, st)

    def emit_quant(win_ap):
        view = lambda t: t.rearrange("p (a b) -> p a b", b=32)
        op("vector", lambda E, w=win_ap: E.tensor_reduce(
            amax_s[:].unsqueeze(2), view(w), AX.X, ALU.max, apply_absolute_value=True))
        op("vector", lambda E: E.drain())
        op("vector", lambda E: E.tensor_scalar(ssc_s[:], amax_s[:], 1.0 / 7.5, 1e-30, ALU.mult, ALU.max))
        op("vector", lambda E: E.drain())
        op("vector", lambda E: E.tensor_scalar(s8c_s[:], ssc_s[:], 0.125, None, ALU.mult))
        op("vector", lambda E: E.reciprocal(rssc_s[:], ssc_s[:]))
        op("vector", lambda E: E.drain())
        rsb = rssc_s[:].unsqueeze(2).broadcast_to([128, 32, 32])
        s8b = s8c_s[:].unsqueeze(2).broadcast_to([128, 32, 32])
        op("vector", lambda E, w=win_ap: E.tensor_tensor(view(wu_s[:]), view(w), rsb, ALU.mult))
        op("vector", lambda E: E.scalar_tensor_tensor(au_s[:], wu_s[:], -1.0, wu_s[:], ALU.mult, ALU.max))
        op("vector", lambda E: E.tensor_scalar(mk1_s[:], au_s[:], 2.0, -4.0, ALU.is_ge, ALU.mult))
        op("vector", lambda E: E.tensor_scalar(mk2_s[:], au_s[:], 4.0, -2.0, ALU.is_ge, ALU.mult))
        op("vector", lambda E: E.scalar_tensor_tensor(mk1_s[:], mk1_s[:], 8.0, mk2_s[:], ALU.add, ALU.add))
        op("vector", lambda E: E.tensor_scalar(mk2_s[:], au_s[:], 4.0, 2.0, ALU.is_ge, ALU.mult))
        op("vector", lambda E: E.tensor_scalar(au_s[:], au_s[:], 2.0, None, ALU.is_ge))
        op("vector", lambda E: E.scalar_tensor_tensor(mk2_s[:], au_s[:], 1.0, mk2_s[:], ALU.add, ALU.add))
        op("vector", lambda E: E.tensor_tensor(qi_v, wu_s[:], mk1_s[:], ALU.mult))
        op("vector", lambda E: E.tensor_tensor(mk1_s[:], qi_v, mk2_s[:], ALU.mult))
        return op("vector", lambda E: E.tensor_tensor(view(au_s[:]), view(mk1_s[:]), s8b, ALU.mult), sem="sv")

    _tq_hist = []
    for jc in range(8):
        if jc >= 2:
            wait("sync", "sv", _tq_hist[jc - 2])
        tl = dma("sync", wfst[:, jc % 2, :], wffn.rearrange("(c p) n -> c p n", p=128)[jc])
        wait("vector", "sdma", tl)
        tq = emit_quant(wfst[:, jc % 2, :])
        _tq_hist.append(tq)
        wait("tensor", "sv", tq)
        t = None
        for kc in range(8):
            t = op("tensor", lambda E, kc=kc: E.transpose(
                ps_a_bf[:, kc * 128:(kc + 1) * 128],
                au_s[:, kc * 128:(kc + 1) * 128], identB[:]),
                sem=("stn" if kc == 7 else None))
        wait("vector", "stn", t)
        tv = op("vector", lambda E, jc=jc: E.tensor_copy(
            wffnq[:, :, jc * 128:(jc + 1) * 128],
            ps_a_bf[:, 0:1024].rearrange("p (kc n) -> p kc n", kc=8)), sem="sv")
        wait("tensor", "sv", tv)
    barrier()
    if "wq" in dbg:
        dma("gpsimd", dbg["wq"], wffnq[:].rearrange("p c n -> p (c n)"))
        wait("vector", "gdma", cnt["gdma"])

    def statcombine(negmu, r, eps, rinv=None):
        g = bnb[:].rearrange("p t (a k) -> p t a k", k=3)
        m4 = g[:, :, :, 1]
        v4 = g[:, :, :, 2]
        op("vector", lambda E: E.tensor_reduce(stat_t1[:].unsqueeze(2), m4, AX.X, ALU.add))
        op("vector", lambda E: E.tensor_reduce(stat_t2[:].unsqueeze(2), v4, AX.X, ALU.add))
        op("vector", lambda E: E.tensor_tensor(stat_t4[:], m4, m4, ALU.mult))
        op("vector", lambda E: E.tensor_reduce(stat_t3[:].unsqueeze(2), stat_t4[:], AX.X, ALU.add))
        op("vector", lambda E, n=negmu: E.tensor_scalar(n[:], stat_t1[:], -0.25, None, ALU.mult))
        op("vector", lambda E: E.tensor_scalar(stat_t5[:], stat_t2[:], 1.0 / 1024.0, None, ALU.mult))
        op("vector", lambda E: E.drain())
        op("vector", lambda E: E.scalar_tensor_tensor(stat_t6[:], stat_t3[:], 0.25, stat_t5[:], ALU.mult, ALU.add))
        op("vector", lambda E, n=negmu: E.tensor_tensor(stat_t7[:], n[:], n[:], ALU.mult))
        op("vector", lambda E: E.drain())
        tv = op("vector", lambda E: E.tensor_tensor(stat_t8[:], stat_t6[:], stat_t7[:], ALU.subtract), sem="sv")
        op("vector", lambda E: E.drain())
        wait("scalar", "sv", tv)
        dst_sd = rinv if rinv is not None else stat_t3
        eps_ap = eps_c[:] if eps == EPS else eps2_c[:]
        ta = op("scalar", lambda E, d=dst_sd, ep=eps_ap: E.activation(
            d[:], stat_t8[:], AF.Sqrt, bias=ep, scale=1.0), sem="sa")
        wait("vector", "sa", ta)
        rv = op("vector", lambda E, d=dst_sd, rr=r: E.reciprocal(rr[:], d[:]), sem="sv")
        op("vector", lambda E: E.drain())
        return rv

    def rowify(cols, row_tile):
        t = None
        for tt in range(NTT):
            t = op("tensor", lambda E, tt=tt, c=cols: E.transpose(
                ps_a[0:1, tt * 128:(tt + 1) * 128], c[:, tt:tt + 1], identF[:]),
                sem=("stn" if tt == NTT - 1 else None))
        wait("vector", "stn", t)
        tv = op("vector", lambda E, rt=row_tile: E.tensor_copy(rt, ps_a[0:1, 0:S]), sem="sv")
        wait("tensor", "sv", tv)

    TOKT = x_tok.rearrange("(q p) n -> q p n", p=128)
    OUTT = out.rearrange("(q p) n -> q p n", p=128)

    def dump(name, src_ap):
        if name in dbg:
            dma("gpsimd", dbg[name], src_ap)

    for sq in range(0 if main_ph == "S" else nseq):
        for tt in range(NTT):
            tl = dma("sync", xa_v[:, tt % 2, :], TOKT[sq * NTT + tt])
            wait("vector", "sdma", tl)
            op("vector", lambda E, tt=tt: E.bn_stats(bnb[:, tt, 0:6], xa_v[:, tt % 2, 0:512]))
            tv = op("vector", lambda E, tt=tt: E.bn_stats(bnb[:, tt, 6:12], xa_v[:, tt % 2, 512:1024]), sem="sv")
            wait("sync", "sv", tv)
        if main_ph == "A1":
            continue
        ret = statcombine(negmu0, r0, EPS, rinv=rinv0)
        wait("tensor", "sv", ret)
        if "bnb" in dbg:
            wait("gpsimd", "sv", ret)
            dma("gpsimd", dbg["bnb"], bnb[:].rearrange("p t n -> p (t n)"))
        if "stats" in dbg:
            wait("gpsimd", "sv", ret)
            dma("gpsimd", dbg["stats"].rearrange("p (a n) -> p a n", n=NTT)[:, 0, :], negmu0[:])
            dma("gpsimd", dbg["stats"].rearrange("p (a n) -> p a n", n=NTT)[:, 1, :], r0[:])
            dma("gpsimd", dbg["stats"].rearrange("p (a n) -> p a n", n=NTT)[:, 2, :], rinv0[:])
            dma("gpsimd", dbg["stats"].rearrange("p (a n) -> p a n", n=NTT)[:, 3, :], stat_t1[:])
            dma("gpsimd", dbg["stats"].rearrange("p (a n) -> p a n", n=NTT)[:, 4, :], stat_t2[:])
            dma("gpsimd", dbg["stats"].rearrange("p (a n) -> p a n", n=NTT)[:, 5, :], stat_t3[:])
        if main_ph == "A2":
            continue
        rowify(negmu0, murow[0:1, :])
        rowify(rinv0, rinvrow[0:1, :])
        rowify(r0, r0row)
        if main_ph == "A3":
            continue
        t = None
        for ic in range(NIC):
            t = op("tensor", lambda E, ic=ic: E.matmul(
                ps_a[:, ic * 512:(ic + 1) * 512], ones_rb[:],
                r0row[:, ic * 512:(ic + 1) * 512], start=True, stop=True),
                sem=("stn" if ic == NIC - 1 else None))
        wait("vector", "stn", t)
        tv = op("vector", lambda E: E.tensor_copy(rb_v, ps_a[:, 0:S]), sem="sv")
        wait("tensor", "sv", tv)
        if "rows" in dbg:
            wait("gpsimd", "sv", cnt["sv"])
            rv = dbg["rows"].rearrange("p (a n) -> p a n", n=S)
            dma("gpsimd", rv[:, 0, :], murow[0:1, :])
            dma("gpsimd", rv[:, 1, :], rinvrow[0:1, :])
            dma("gpsimd", rv[:, 2, :], r0row)
        if "rb" in dbg:
            dma("gpsimd", dbg["rb"], rb_v)
        barrier()
        if "B" not in main_ph:
            continue

        XF = x_feat[sq].rearrange("(kc p) n -> p kc n", p=128)
        for ic in range(NIC):
            if ic >= 1:
                wait("gpsimd", "stn", cnt["stn"])
            tl = dma("gpsimd", xfb[:, ic % 2, :, :], XF[:, :, ic * 512:(ic + 1) * 512])
            wait("tensor", "gdma", tl)
            for ct in range(8):
                pb = ps_b if (ct % 2 == 0) else ps_c
                for kc in range(8):
                    op("tensor", lambda E, kc=kc, ct=ct, ic=ic, pb=pb: E.matmul(
                        pb[:], wqk_s[:, kc, ct * 128:(ct + 1) * 128],
                        xfb[:, ic % 2, kc, :], start=(kc == 0), stop=False))
                op("tensor", lambda E, ct=ct, ic=ic, pb=pb: E.matmul(
                    pb[:], cwqk_r[:, ct * 128:(ct + 1) * 128],
                    murow[:, ic * 512:(ic + 1) * 512], start=False, stop=False))
                tmm = op("tensor", lambda E, ct=ct, ic=ic, pb=pb: E.matmul(
                    pb[:], bqk_r[:, ct * 128:(ct + 1) * 128],
                    rinvrow[:, ic * 512:(ic + 1) * 512], start=False, stop=True), sem="stn")
                wait("vector", "stn", tmm)
                dbuf = q_v if ct < 4 else k_v
                tv = op("vector", lambda E, d=dbuf, dc=ct % 4, ic=ic, pb=pb: E.tensor_tensor(
                    d[:, dc, ic * 512:(ic + 1) * 512], pb[:],
                    rb_v[:, ic * 512:(ic + 1) * 512], ALU.mult), sem="sv")
                wait("tensor", "sv", tv)
            for tl_ in range(4):
                tt = ic * 4 + tl_
                ph = tl_ % 2
                for kc in range(8):
                    op("tensor", lambda E, kc=kc, tl_=tl_, ph=ph, ic=ic: E.matmul(
                        ps_d[:, ph * 512:(ph + 1) * 512],
                        xfb[:, ic % 2, kc, tl_ * 128:(tl_ + 1) * 128],
                        wv_s[:, kc, :], start=(kc == 0), stop=False))
                op("tensor", lambda E, tt=tt, ph=ph: E.matmul(
                    ps_d[:, ph * 512:(ph + 1) * 512], murow[:, tt * 128:(tt + 1) * 128],
                    cwv_r[:], start=False, stop=False))
                tmm = op("tensor", lambda E, tt=tt, ph=ph: E.matmul(
                    ps_d[:, ph * 512:(ph + 1) * 512], rinvrow[:, tt * 128:(tt + 1) * 128],
                    bv_r[:], start=False, stop=True), sem="stn")
                wait("vector", "stn", tmm)
                tv = op("vector", lambda E, tt=tt, ph=ph: E.tensor_scalar(
                    v_v[:, tt, :], ps_d[:, ph * 512:(ph + 1) * 512],
                    r0[:, tt:tt + 1], None, ALU.mult), sem="sv")
                wait("tensor", "sv", tv)
        barrier()
        dump("q", q_v.rearrange("p c n -> p (c n)"))
        dump("k", k_v.rearrange("p c n -> p (c n)"))
        dump("v", v_v.rearrange("p c n -> p (c n)"))
        if "C" not in main_ph:
            continue

        for pr in range(NPAIR):
            tmm = None
            for hl in range(2):
                tmm = op("tensor", lambda E, pr=pr, hl=hl: E.matmul(
                    ps_d[:, hl * 512:(hl + 1) * 512],
                    q_v[hl * 64:(hl + 1) * 64, pr, 0:S:16],
                    k_v[hl * 64:(hl + 1) * 64, pr, 0:S:4],
                    start=True, stop=True, tile_position=(hl * 64, 0)),
                    sem=("stn" if hl == 1 else None))
            wait("vector", "stn", tmm)
            op("vector", lambda E, pr=pr: E.reduce_max(
                cm_s[:, 2 * pr:2 * pr + 1], ps_d[:, 0:512], AX.X))
            tv = op("vector", lambda E, pr=pr: E.reduce_max(
                cm_s[:, 2 * pr + 1:2 * pr + 2], ps_d[:, 512:1024], AX.X), sem="sv")
            wait("tensor", "sv", tv)
        evv = cm_s[:].rearrange("p (a b) -> p a b", b=2)
        op("vector", lambda E: E.drain())
        tv = op("vector", lambda E, ev=evv: E.tensor_tensor(
            cmp_s[:].unsqueeze(2), ev[:, :, 0:1], ev[:, :, 1:2], ALU.max), sem="sv")
        op("vector", lambda E: E.drain())
        wait("tensor", "sv", tv)
        t = op("tensor", lambda E: E.transpose(ps_a[0:4, 0:128], cmp_s[:], identF[:]), sem="stn")
        wait("vector", "stn", t)
        op("vector", lambda E: E.reduce_max(gm_s[:], ps_a[0:4, 0:128], AX.X))
        op("vector", lambda E: E.drain())
        tv = op("vector", lambda E: E.tensor_scalar(negmp[:], gm_s[:], -0.125, -2.0, ALU.mult, ALU.add), sem="sv")
        op("vector", lambda E: E.drain())
        wait("tensor", "sv", tv)
        t = op("tensor", lambda E: E.transpose(ps_a[0:1, 256:260], negmp[:], identF[0:4, 0:4]), sem="stn")
        wait("vector", "stn", t)
        tv = op("vector", lambda E: E.tensor_copy(negmrow[:], ps_a[0:1, 256:260]), sem="sv")
        op("vector", lambda E: E.drain())
        wait("tensor", "sv", tv)
        t = op("tensor", lambda E: E.matmul(ps_a[:, 512:516], ones_rf[:], negmrow[:],
                                            start=True, stop=True), sem="stn")
        wait("vector", "stn", t)
        tv = op("vector", lambda E: E.tensor_copy(negmb[:], ps_a[:, 512:516]), sem="sv")
        op("vector", lambda E: E.drain())
        wait("tensor", "sv", tv)
        wait("scalar", "sv", tv)
        if sub_ph == "C1":
            continue

        _nps = 1 if sub_ph == "C2" else NPAIR
        _nis = 1 if sub_ph == "C2" else NIC
        _lcap = 99
        if sub_ph.startswith("L") and sub_ph[1:].isdigit():
            _lcap = int(sub_ph[1:])
        if sub_ph.startswith("P"):
            _lcap = 8 if sub_ph.startswith(("PD", "PC", "PX", "PO", "PZ")) else 1
        _ldone = 0
        _iters = [(p_, i_) for p_ in range(_nps) for i_ in range(_nis)]
        if sub_ph == "LR":
            _iters = [(0, 0), (0, 0)]
            _lcap = 2
        if sub_ph == "LP2":
            _iters = [(2, 0), (2, 1), (2, 2), (2, 3)]
            _lcap = 4
        if sub_ph == "LP2a":
            _iters = [(2, 0)]
            _lcap = 1
        for pr, ic in _iters:
            if True:
                if _ldone >= _lcap:
                    continue
                if _ldone > 0 and _ldone % 4 == 0:
                    barrier()
                _ldone += 1
                for g in range(8):
                    for hl in range(2):
                        for jl in range(2):
                            jt = g * 2 + jl
                            sl = hl * 2 + jl
                            last = (hl == 1 and jl == 1)
                            op("tensor", lambda E, pr=pr, ic=ic, jt=jt, hl=hl, sl=sl: E.matmul(
                                ps_a[:, sl * 512:(sl + 1) * 512],
                                k_v[hl * 64:(hl + 1) * 64, pr, jt * 128:(jt + 1) * 128],
                                q_v[hl * 64:(hl + 1) * 64, pr, ic * 512:(ic + 1) * 512],
                                start=True, stop=True, tile_position=(hl * 64, 0)),
                                sem=("stn" if last else None))
                    t_s = cnt["stn"]
                    wait("scalar", "stn", t_s)
                    for _sl in range(3):
                        op("scalar", lambda E, pr=pr, g=g, _sl=_sl: E.activation(
                            eg_v[:, g % 2, _sl * 512:(_sl + 1) * 512],
                            ps_a[:, _sl * 512:(_sl + 1) * 512], AF.Exp,
                            bias=negmb[:, pr:pr + 1], scale=0.125))
                    t_e = op("scalar", lambda E, pr=pr, g=g: E.activation(
                        eg_v[:, g % 2, 1536:2048], ps_a[:, 1536:2048], AF.Exp,
                        bias=negmb[:, pr:pr + 1], scale=0.125), sem="sa")
                    wait("tensor", "sa", t_e)
                    first = (g == 0)
                    lastg = (g == 7)
                    t_av = None
                    for hl in range(2):
                        for jl in range(2):
                            jt = g * 2 + jl
                            sl = hl * 2 + jl
                            h = pr * 2 + hl
                            st_ = (first and jl == 0)
                            sp_ = (lastg and jl == 1)
                            last_op = (hl == 1 and jl == 1)
                            ob_ap = ps_b[0:64, :] if hl == 0 else ps_b[64:128, :]
                            zb_ap = ps_c[0:1, :] if hl == 0 else ps_c[64:65, :]
                            op("tensor", lambda E, jt=jt, sl=sl, h=h, hl=hl, g=g,
                               st_=st_, sp_=sp_, ob=ob_ap: E.matmul(
                                ob, v_v[:, jt, h * 64:(h + 1) * 64],
                                eg_v[:, g % 2, sl * 512:(sl + 1) * 512],
                                start=st_, stop=sp_, tile_position=(0, hl * 64)))
                            t_av = op("tensor", lambda E, jt=jt, sl=sl, hl=hl, g=g,
                               st_=st_, sp_=sp_, zb=zb_ap: E.matmul(
                                zb, ones_col[:],
                                eg_v[:, g % 2, sl * 512:(sl + 1) * 512],
                                start=st_, stop=sp_, tile_position=(0, hl * 64)),
                                sem=("stn" if last_op else None))
                    wait("scalar", "stn", t_av)
                idx1 = (pr * 2 + 0) * 4 + ic
                idx2 = (pr * 2 + 1) * 4 + ic
                wait("vector", "stn", cnt["stn"])
                tv = op("vector", lambda E, pr=pr, ic=ic: E.tensor_copy(
                    obuf[:, pr, ic * 512:(ic + 1) * 512], ps_b[:]), sem="sv")
                op("scalar", lambda E: E.copy(ZR[0:1, 0:512], ps_c[0:1, :]))
                ta = op("scalar", lambda E: E.copy(ZR[0:1, 512:1024], ps_c[64:65, :]), sem="sa")
                wait("tensor", "sv", tv)
                wait("tensor", "sa", ta)
                wait("sync", "sa", ta)
                wait("sync", "sv", tv)
                if sub_ph not in ("L2z", "L2n"):
                    wait("gpsimd", "sa", ta)
                    dma("gpsimd", zscr[sq:sq + 1, idx1 * 512:(idx1 + 1) * 512], ZR[0:1, 0:512])
                    tzd = dma("gpsimd", zscr[sq:sq + 1, idx2 * 512:(idx2 + 1) * 512], ZR[0:1, 512:1024])
                    wait("scalar", "gdma", tzd)
        if sub_ph in ("C2", "C3") or sub_ph.startswith("L") or sub_ph.startswith("P"):
            continue
        if "obun" in dbg:
            wait("gpsimd", "sv", cnt["sv"])
            dma("gpsimd", dbg["obun"], obuf.rearrange("p c n -> p (c n)"))
            wait("sync", "gdma", cnt["gdma"])
        wait("sync", "gdma", cnt["gdma"])
        t2_ = dma("sync", zgt, zscr[sq:sq + 1, :].rearrange("p (a b) -> (p a) b", a=32))
        wait("vector", "sdma", t2_)
        tv = op("vector", lambda E: E.reciprocal(rzg, zgt), sem="sv")
        wait("sync", "sv", tv)
        t3_ = dma("sync", rzscr[sq:sq + 1, :].rearrange("p (a b) -> (p a) b", a=32), rzg)
        wait("sync", "sdma", t3_)
        if "zq" in dbg:
            dma("sync", dbg["zq"], zgt)
            dma("sync", dbg["rzq"], rzg)
        if "negm" in dbg:
            dma("sync", dbg["negm"], negmb[:])
        if sub_ph.startswith("PZ"):
            for _pi in range(int(sub_ph[2:])):
                op("tensor", lambda E: E.matmul(
                    ps_c[64:65, :], ones_col[:], eg_v[:, 0, 0:512],
                    start=True, stop=True, tile_position=(0, 64)), sem="stn")
            wait("vector", "stn", cnt["stn"])
        elif sub_ph.startswith("PO"):
            for _pi in range(int(sub_ph[2:])):
                qo = (_pi * 64) % 1536
                op("tensor", lambda E, qo=qo: E.matmul(
                    ps_a[:, 0:512], k_v[0:64, 3, qo:qo + 128],
                    q_v[0:64, 3, qo + 64:qo + 576], start=True, stop=True,
                    tile_position=(0, 0)))
            op("tensor", lambda E: E.matmul(ps_b[0:1, 0:1], ones_rf[0:1, 0:1],
                                            ones_rf[0:1, 0:1], start=True, stop=True), sem="stn")
            wait("vector", "stn", cnt["stn"])
        elif sub_ph.startswith("PX"):
            for _pi in range(int(sub_ph[2:])):
                op("scalar", lambda E: E.activation(
                    eg_v[:, 0, :], ps_a[:, 0:2048], AF.Exp,
                    bias=negmb[:, 0:1], scale=0.125))
            op("scalar", lambda E: E.copy(junk_s[:], junk_v[:]), sem="sa")
        elif sub_ph.startswith("PC"):
            for _pi in range(int(sub_ph[2:])):
                op("scalar", lambda E: E.copy(ZR[0:1, 0:512], ps_c[0:1, :]))
                op("scalar", lambda E: E.copy(ZR[0:1, 512:1024], ps_c[64:65, :]))
            op("scalar", lambda E: E.copy(junk_s[:], junk_v[:]), sem="sa")
        elif sub_ph.startswith("PD"):
            for _pi in range(int(sub_ph[2:])):
                dma("gpsimd", junk_v[:], zscr[0:1, 0:8])
            wait("gpsimd", "gdma", cnt["gdma"])
        elif sub_ph.startswith("PS"):
            for _pi in range(int(sub_ph[2:])):
                op("scalar", lambda E: E.activation(junk_s[:], junk_v[:], AF.Square), sem="sa")
                op("tensor", lambda E: E.matmul(ps_b[0:1, 0:1], ones_rf[0:1, 0:1],
                                                ones_rf[0:1, 0:1], start=True, stop=True), sem="stn")
        elif sub_ph.startswith("PW"):
            for _pi in range(int(sub_ph[2:])):
                wait("tensor", "sv", cnt["sv"])
                wait("scalar", "stn", cnt["stn"])
                wait("vector", "sa", cnt["sa"])
        elif sub_ph.startswith("PT"):
            for _pi in range(int(sub_ph[2:])):
                op("tensor", lambda E: E.matmul(ps_b[0:1, 0:1], ones_rf[0:1, 0:1],
                                                ones_rf[0:1, 0:1], start=True, stop=True))
            op("tensor", lambda E: E.matmul(ps_b[0:1, 0:1], ones_rf[0:1, 0:1],
                                            ones_rf[0:1, 0:1], start=True, stop=True), sem="stn")
        elif sub_ph.startswith("PA"):
            for _pi in range(int(sub_ph[2:])):
                op("scalar", lambda E: E.activation(junk_s[:], junk_v[:], AF.Square))
            op("scalar", lambda E: E.activation(junk_s[:], junk_v[:], AF.Square), sem="sa")
        elif sub_ph.startswith("P"):
            for _pi in range(int(sub_ph[1:])):
                op("vector", lambda E: E.tensor_copy(junk_v[:], junk_g[:]))
            op("vector", lambda E: E.tensor_copy(junk_v[:], junk_g[:]), sem="sv")
        wait("gpsimd", "sdma", cnt["sdma"])
        _cap = 99
        if sub_ph.startswith("Cn"):
            _cap = int(sub_ph[2:])
        _done = 0
        for pr in range(0 if sub_ph == "C4" else NPAIR):
            for ic in range(NIC):
                if _done >= _cap:
                    continue
                _done += 1
                idx1 = (pr * 2 + 0) * 4 + ic
                idx2 = (pr * 2 + 1) * 4 + ic
                ph = (pr * NIC + ic) % 2
                rzv = rzscr[sq:sq + 1, :].rearrange("p (a b) -> (p a) b", a=32)
                dma("gpsimd", msb_v[:, ph, :][0:64, :],
                    rzv[idx1:idx1 + 1, :].broadcast_to([64, 512]))
                tg = dma("gpsimd", msb_v[:, ph, :][64:128, :],
                         rzv[idx2:idx2 + 1, :].broadcast_to([64, 512]))
                wait("vector", "gdma", tg)
                tv = op("vector", lambda E, pr=pr, ic=ic, ph=ph: E.tensor_tensor(
                    obuf[:, pr, ic * 512:(ic + 1) * 512],
                    obuf[:, pr, ic * 512:(ic + 1) * 512],
                    msb_v[:, ph, :], ALU.mult), sem="sv")
                wait("gpsimd", "sv", tv)
        barrier()
        dump("obuf", obuf.rearrange("p c n -> p (c n)"))
        if "D" not in main_ph:
            continue

        for tt in range(NTT):
            for vc in range(4):
                for hh in range(2):
                    op("tensor", lambda E, vc=vc, hh=hh, tt=tt: E.matmul(
                        ps_d[:, hh * 512:(hh + 1) * 512],
                        obuf[:, vc, tt * 128:(tt + 1) * 128],
                        wout_s[:, vc, hh * 512:(hh + 1) * 512],
                        start=(vc == 0), stop=False))
            for hh in range(2):
                op("tensor", lambda E, hh=hh: E.matmul(
                    ps_d[:, hh * 512:(hh + 1) * 512], ones_rb[:],
                    bout_r[:, hh * 512:(hh + 1) * 512], start=False, stop=True),
                    sem=("stn" if hh == 1 else None))
            wait("vector", "stn", cnt["stn"])
            op("vector", lambda E, tt=tt: E.bn_stats(bnb[:, tt, 0:6], ps_d[:, 0:512]))
            op("vector", lambda E, tt=tt: E.bn_stats(bnb[:, tt, 6:12], ps_d[:, 512:1024]))
            tv = op("vector", lambda E, tt=tt: E.tensor_copy(zbuf[:, tt, :], ps_d[:, 0:1024]), sem="sv")
            wait("tensor", "sv", tv)
        statcombine(negmuz, rz_s, EPS)
        dump("zbuf", zbuf.rearrange("p t n -> p (t n)"))
        for tt in range(NTT):
            tl = dma("gpsimd", xtok_v[:, tt % 2, :], TOKT[sq * NTT + tt])
            op("vector", lambda E, tt=tt: E.tensor_scalar(
                rzgam[:], gamb[:], rz_s[:, tt:tt + 1], None, ALU.mult))
            op("vector", lambda E, tt=tt: E.scalar_tensor_tensor(
                t2_s[:], zbuf[:, tt, :], negmuz[:, tt:tt + 1], rzgam[:], ALU.add, ALU.mult))
            wait("vector", "gdma", tl)
            op("vector", lambda E, tt=tt: E.tensor_scalar(
                uTB_s[:], xtok_v[:, tt % 2, :], negmu0[:, tt:tt + 1], r0[:, tt:tt + 1],
                ALU.add, ALU.mult))
            op("vector", lambda E: E.tensor_tensor(t2_s[:], t2_s[:], uTB_s[:], ALU.add))
            op("vector", lambda E, tt=tt: E.tensor_tensor(sb_v[:, tt, :], t2_s[:], betb[:], ALU.add))
            op("vector", lambda E, tt=tt: E.bn_stats(bnb[:, tt, 0:6], sb_v[:, tt, 0:512]))
            tv = op("vector", lambda E, tt=tt: E.bn_stats(bnb[:, tt, 6:12], sb_v[:, tt, 512:1024]), sem="sv")
            wait("gpsimd", "sv", tv)
        statcombine(negmus, rs_s, EPS_S)
        dump("sbuf", sb_v.rearrange("p t n -> p (t n)"))
        for tt in range(NTT):
            tl = dma("gpsimd", xtok_v[:, tt % 2, :], TOKT[sq * NTT + tt])
            op("vector", lambda E, tt=tt: E.tensor_scalar(
                t2_s[:], sb_v[:, tt, :], negmus[:, tt:tt + 1], rs_s[:, tt:tt + 1],
                ALU.add, ALU.mult))
            wait("vector", "gdma", tl)
            op("vector", lambda E, tt=tt: E.tensor_tensor(
                x1buf[:, tt, :], xtok_v[:, tt % 2, :], t2_s[:], ALU.add))
            op("vector", lambda E, tt=tt: E.bn_stats(bnb[:, tt, 0:6], x1buf[:, tt, 0:512]))
            tv = op("vector", lambda E, tt=tt: E.bn_stats(bnb[:, tt, 6:12], x1buf[:, tt, 512:1024]), sem="sv")
            wait("gpsimd", "sv", tv)
        statcombine(negmu1, r1_s, EPS)
        dump("x1", x1buf.rearrange("p t n -> p (t n)"))
        for tt in range(NTT):
            tw = op("vector", lambda E, tt=tt: E.tensor_scalar(
                wu_s[:], x1buf[:, tt, :], negmu1[:, tt:tt + 1], r1_s[:, tt:tt + 1],
                ALU.add, ALU.mult), sem=("sv" if (tt == 0 and "w0" in dbg) else None))
            if tt == 0 and "w0" in dbg:
                wait("gpsimd", "sv", tw)
                dma("gpsimd", dbg["w0"], wu_s[:])
                wait("vector", "gdma", cnt["gdma"])
            tq = emit_quant(wu_s[:])
            wait("tensor", "sv", tq)
            if tt == 0 and "xd" in dbg:
                wait("gpsimd", "sv", tq)
                dma("gpsimd", dbg["xd"], au_s[:])
                wait("tensor", "gdma", cnt["gdma"])
            t = None
            for kc in range(8):
                t = op("tensor", lambda E, kc=kc: E.transpose(
                    ps_a_bf[:, 2048 + kc * 128:2048 + (kc + 1) * 128],
                    au_s[:, kc * 128:(kc + 1) * 128], identB[:]),
                    sem=("stn" if kc == 7 else None))
            wait("vector", "stn", t)
            tv = op("vector", lambda E: E.tensor_copy(
                xdt_v[:, :, :],
                ps_a_bf[:, 2048:3072].rearrange("p (kc n) -> p kc n", kc=8)), sem="sv")
            wait("tensor", "sv", tv)
            if tt == 0 and "xdt" in dbg:
                wait("gpsimd", "sv", tv)
                dma("gpsimd", dbg["xdt"], xdt_v[:].rearrange("p c n -> p (c n)"))
                wait("vector", "gdma", cnt["gdma"])
            for kc in range(8):
                for hh in range(2):
                    op("tensor", lambda E, kc=kc, hh=hh: E.matmul(
                        ps_d[:, hh * 512:(hh + 1) * 512],
                        xdt_v[:, kc, :],
                        wffnq[:, kc, hh * 512:(hh + 1) * 512],
                        start=(kc == 0), stop=False))
            for hh in range(2):
                op("tensor", lambda E, hh=hh: E.matmul(
                    ps_d[:, hh * 512:(hh + 1) * 512], ones_rb[:],
                    fbias_r[:, hh * 512:(hh + 1) * 512], start=False, stop=True),
                    sem=("stn" if hh == 1 else None))
            wait("vector", "stn", cnt["stn"])
            op("vector", lambda E, tt=tt: E.bn_stats(bnb[:, tt, 0:6], ps_d[:, 0:512]))
            op("vector", lambda E, tt=tt: E.bn_stats(bnb[:, tt, 6:12], ps_d[:, 512:1024]))
            tv = op("vector", lambda E, tt=tt: E.tensor_copy(fbuf[:, tt, :], ps_d[:, 0:1024]), sem="sv")
            wait("tensor", "sv", tv)
        statcombine(negmuf, rf_s, EPS)
        dump("fbuf", fbuf.rearrange("p t n -> p (t n)"))
        for tt in range(NTT):
            if tt >= 1:
                wait("vector", "sout", cnt["sout"])
            op("vector", lambda E, tt=tt: E.tensor_scalar(
                t2_s[:], fbuf[:, tt, :], negmuf[:, tt:tt + 1], rf_s[:, tt:tt + 1],
                ALU.add, ALU.mult))
            tv = op("vector", lambda E, tt=tt: E.tensor_tensor(
                outst[:], x1buf[:, tt, :], t2_s[:], ALU.add), sem="sv")
            wait("sync", "sv", tv)
            dma("sync", OUTT[sq * NTT + tt], outst[:], sem="sout")
        barrier()

    if cnt["sout"] == 0:
        dma("sync", OUTT[0], MISC[:, 0:1024], sem="sout")
    wait("sync", "sout", cnt["sout"])
    wait("gpsimd", "gdma", cnt["gdma"])

    with nc.Block() as blk:
        @blk.sync
        def _(E):
            for f in ops["sync"]:
                f(E)

        @blk.gpsimd
        def _(E):
            for f in ops["gpsimd"]:
                f(E)

        @blk.vector
        def _(E):
            for f in ops["vector"]:
                f(E)

        @blk.scalar
        def _(E):
            for f in ops["scalar"]:
                f(E)

        @blk.tensor
        def _(E):
            for f in ops["tensor"]:
                f(E)

    return nc, st


def _qk_perm():
    q = [h * 128 + 2 * d for h in range(H) for d in range(DH)]
    k = [h * 128 + 2 * d + 1 for h in range(H) for d in range(DH)]
    return np.array(q + k, dtype=np.int64)


def kernel(L0_attn_x, L0_attn_W_qk, L0_attn_b_qk, L0_attn_W_v, L0_attn_b_v,
           L0_attn_W_out, L0_attn_b_out, L0_attn_ln_gamma, L0_attn_ln_beta,
           L0_ffn_weight, L0_ffn_bias):
    x = np.ascontiguousarray(np.asarray(L0_attn_x, dtype=np.float32))
    perm = _qk_perm()
    wqk_p = np.ascontiguousarray(np.asarray(L0_attn_W_qk, np.float32)[:, perm])
    bqk_p = np.ascontiguousarray(np.asarray(L0_attn_b_qk, np.float32)[perm][None, :])
    wv_ = np.ascontiguousarray(np.asarray(L0_attn_W_v, np.float32))
    bv_ = np.ascontiguousarray(np.asarray(L0_attn_b_v, np.float32)[None, :])
    wout_ = np.ascontiguousarray(np.asarray(L0_attn_W_out, np.float32))
    bout_ = np.ascontiguousarray(np.asarray(L0_attn_b_out, np.float32)[None, :])
    gam_ = np.ascontiguousarray(np.asarray(L0_attn_ln_gamma, np.float32)[None, :])
    bet_ = np.ascontiguousarray(np.asarray(L0_attn_ln_beta, np.float32)[None, :])
    wffn_ = np.ascontiguousarray(np.asarray(L0_ffn_weight, np.float32))
    fbias_ = np.ascontiguousarray(np.asarray(L0_ffn_bias, np.float32)[None, :])

    nc, stack = build_program()
    try:
        in_maps = []
        for c in range(8):
            xs = x[c * NSEQ:(c + 1) * NSEQ]
            in_maps.append(dict(
                x_tok=np.ascontiguousarray(xs.reshape(NSEQ * S, D)),
                x_feat=np.ascontiguousarray(xs.transpose(0, 2, 1)),
                wqk=wqk_p, wv=wv_, wout=wout_, wffn=wffn_, bqk=bqk_p, bv=bv_,
                bout=bout_, gam=gam_, bet=bet_, fbias=fbias_))
        res = run_bass_kernel_spmd(nc, in_maps, core_ids=list(range(8)))
    finally:
        stack.close()
    outs = [res.results[c]["out"].reshape(NSEQ, S, D) for c in range(8)]
    return np.concatenate(outs, axis=0)


# version 07
# speedup vs baseline: 1.1831x; 1.1831x over previous
import numpy as np
import contextlib
import concourse.bass as bass
import concourse.mybir as mybir
from concourse.bass_utils import run_bass_kernel_spmd

F32 = mybir.dt.float32
BF16 = mybir.dt.bfloat16
I32 = mybir.dt.int32
ALU = mybir.AluOpType
AF = mybir.ActivationFunctionType
AX = mybir.AxisListType

NSEQ, S, D, H, DH, VD = 4, 2048, 1024, 8, 64, 512
NTT, NIC, NJT, NPAIR = 16, 4, 16, 4
EPS = 1e-5
EPS_S = 2e-5


def _finish(nc, ops, st):
    with nc.Block() as blk:
        @blk.sync
        def _(E):
            for f in ops["sync"]:
                f(E)

        @blk.gpsimd
        def _(E):
            for f in ops["gpsimd"]:
                f(E)

        @blk.vector
        def _(E):
            for f in ops["vector"]:
                f(E)

        @blk.scalar
        def _(E):
            for f in ops["scalar"]:
                f(E)

        @blk.tensor
        def _(E):
            for f in ops["tensor"]:
                f(E)
    return nc, st


def build_program(nseq=NSEQ, phases="ABCD", debug_dump=()):
    main_ph, _, sub_ph = phases.partition(".")
    nc = bass.Bass("TRN2", target_bir_lowering=False)

    x_tok = nc.dram_tensor("x_tok", [NSEQ * S, D], F32, kind="ExternalInput").ap()
    x_feat = nc.dram_tensor("x_feat", [NSEQ, D, S], F32, kind="ExternalInput").ap()
    wqk = nc.dram_tensor("wqk", [D, D], F32, kind="ExternalInput").ap()
    wv = nc.dram_tensor("wv", [D, VD], F32, kind="ExternalInput").ap()
    wout = nc.dram_tensor("wout", [VD, D], F32, kind="ExternalInput").ap()
    wffn = nc.dram_tensor("wffn", [D, D], F32, kind="ExternalInput").ap()
    bqk = nc.dram_tensor("bqk", [1, D], F32, kind="ExternalInput").ap()
    bv = nc.dram_tensor("bv", [1, VD], F32, kind="ExternalInput").ap()
    bout = nc.dram_tensor("bout", [1, D], F32, kind="ExternalInput").ap()
    gam = nc.dram_tensor("gam", [1, D], F32, kind="ExternalInput").ap()
    bet = nc.dram_tensor("bet", [1, D], F32, kind="ExternalInput").ap()
    fbias = nc.dram_tensor("fbias", [1, D], F32, kind="ExternalInput").ap()
    out = nc.dram_tensor("out", [NSEQ * S, D], F32, kind="ExternalOutput").ap()
    zscr = nc.dram_tensor("zscr", [NSEQ, 32 * 512], F32, kind="Internal").ap()
    rzscr = nc.dram_tensor("rzscr", [NSEQ, 32 * 512], F32, kind="Internal").ap()
    dbg = {}
    for name in debug_dump:
        shapes = {"q": [128, 4 * S], "k": [128, 4 * S], "v": [128, NJT * VD],
                  "obuf": [128, 4 * S], "zrow": [1, 32 * 512], "stats": [128, 6 * NTT],
                  "rows": [1, 4 * S], "rb": [128, S], "bnb": [128, NTT * 12], "zq": [32, 512], "rzq": [32, 512], "negm": [128, 4], "obun": [128, 4 * S],
                  "zbuf": [128, NTT * D], "sbuf": [128, NTT * D], "x1": [128, NTT * D],
                  "fbuf": [128, NTT * D], "xd": [128, D], "xdt": [128, D],
                  "wq": [128, 8 * D], "w0": [128, D]}
        dbg[name] = nc.dram_tensor("dbg_" + name, shapes[name], F32, kind="ExternalOutput").ap()

    OUTT0 = out.rearrange("(q p) n -> q p n", p=128)

    st = contextlib.ExitStack()
    e = st.enter_context

    wqk_s = e(nc.sbuf_tensor("wqk_s", [128, 8, D], BF16))
    wv_s = e(nc.sbuf_tensor("wv_s", [128, 8, VD], BF16))
    wout_s = e(nc.sbuf_tensor("wout_s", [128, 4, D], BF16))
    wffnq = e(nc.sbuf_tensor("wffnq", [128, 8, D], BF16))

    U1 = e(nc.sbuf_tensor("U1", [128, 16384], BF16))
    U2 = e(nc.sbuf_tensor("U2", [128, 16384], BF16))
    U3 = e(nc.sbuf_tensor("U3", [128, 16384], BF16))
    q_v = U1[:, 0:8192].rearrange("p (c n) -> p c n", c=4)
    k_v = U1[:, 8192:16384].rearrange("p (c n) -> p c n", c=4)
    zbuf = U1[:].rearrange("p (t n) -> p t n", t=NTT)
    fbuf = zbuf
    v_v = U2[:, 0:8192].rearrange("p (t n) -> p t n", t=NJT)
    obuf = U2[:, 8192:16384].rearrange("p (c n) -> p c n", c=4)
    x1buf = U2[:].rearrange("p (t n) -> p t n", t=NTT)
    xa_v = U3[:, 0:4096].bitcast(F32).rearrange("p (a n) -> p a n", a=2)
    wfst = U3[:, 0:4096].bitcast(F32).rearrange("p (a n) -> p a n", a=2)
    xfb = U3[:, 4096:12288].rearrange("p (a c n) -> p a c n", a=2, c=8)
    eg_v = U3[:, 12288:16384].rearrange("p (a n) -> p a n", a=2)
    sb_v = U3[:].rearrange("p (t n) -> p t n", t=NTT)

    identB = e(nc.sbuf_tensor("identB", [128, 128], BF16))
    identF = e(nc.sbuf_tensor("identF", [128, 128], F32))
    iota_a = U1[:, 0:256].bitcast(I32)
    iota_b = U1[:, 256:512].bitcast(I32)
    gamb = e(nc.sbuf_tensor("gamb", [128, D], BF16))
    betb = e(nc.sbuf_tensor("betb", [128, D], BF16))
    ones_col = e(nc.sbuf_tensor("ones_col", [128, 1], BF16))
    ones_rb = e(nc.sbuf_tensor("ones_rb", [1, 128], BF16))
    ones_rf = e(nc.sbuf_tensor("ones_rf", [1, 128], F32))
    oneblkA = e(nc.sbuf_tensor("oneblkA", [1, 128], F32))
    oneblkB = e(nc.sbuf_tensor("oneblkB", [1, 128], F32))

    bqk_r = e(nc.sbuf_tensor("bqk_r", [1, D], BF16))
    bv_r = e(nc.sbuf_tensor("bv_r", [1, VD], BF16))
    bout_r = e(nc.sbuf_tensor("bout_r", [1, D], BF16))
    fbias_r = e(nc.sbuf_tensor("fbias_r", [1, D], BF16))
    cwqk_r = e(nc.sbuf_tensor("cwqk_r", [1, D], BF16))
    cwv_r = e(nc.sbuf_tensor("cwv_r", [1, VD], BF16))

    murow = e(nc.sbuf_tensor("murow", [1, S], BF16))
    rinvrow = e(nc.sbuf_tensor("rinvrow", [1, S], BF16))
    RBP = e(nc.sbuf_tensor("RBP", [128, 2048], BF16))
    rb_v = RBP[:]
    msb_v = RBP[:, 0:1024].rearrange("p (a n) -> p a n", a=2)

    ZR = e(nc.sbuf_tensor("ZR", [1, 1024], F32))
    r0row = ZR[0:1, :].bitcast(BF16)[:, 0:2048]

    MISC = e(nc.sbuf_tensor("MISC", [128, 2048], F32))
    zgt = MISC[0:32, 0:512]
    rzg = MISC[0:32, 512:1024]
    xtok_v = MISC[:, 0:1024].bitcast(BF16).rearrange("p (a n) -> p a n", a=2)
    row_tmp = MISC[0:1, 1024:2048]
    xdt_v = MISC[:, 1024:1536].bitcast(BF16).rearrange("p (c n) -> p c n", c=8)
    xdt2_v = MISC[:, 1536:2048].bitcast(BF16).rearrange("p (c n) -> p c n", c=8)

    bnb = e(nc.sbuf_tensor("bnb", [128, NTT, 12], F32))
    negmu0 = e(nc.sbuf_tensor("negmu0", [128, NTT], F32))
    r0 = e(nc.sbuf_tensor("r0", [128, NTT], F32))
    rinv0 = e(nc.sbuf_tensor("rinv0", [128, NTT], F32))
    negmuz = e(nc.sbuf_tensor("negmuz", [128, NTT], F32))
    rz_s = e(nc.sbuf_tensor("rz_s", [128, NTT], F32))
    negmus = e(nc.sbuf_tensor("negmus", [128, NTT], F32))
    rs_s = e(nc.sbuf_tensor("rs_s", [128, NTT], F32))
    negmu1 = e(nc.sbuf_tensor("negmu1", [128, NTT], F32))
    r1_s = e(nc.sbuf_tensor("r1_s", [128, NTT], F32))
    negmuf = e(nc.sbuf_tensor("negmuf", [128, NTT], F32))
    rf_s = e(nc.sbuf_tensor("rf_s", [128, NTT], F32))
    stat_t1 = e(nc.sbuf_tensor("stat_t1", [128, NTT], F32))
    stat_t2 = e(nc.sbuf_tensor("stat_t2", [128, NTT], F32))
    stat_t3 = e(nc.sbuf_tensor("stat_t3", [128, NTT], F32))
    stat_t5 = e(nc.sbuf_tensor("stat_t5", [128, NTT], F32))
    stat_t6 = e(nc.sbuf_tensor("stat_t6", [128, NTT], F32))
    stat_t7 = e(nc.sbuf_tensor("stat_t7", [128, NTT], F32))
    stat_t8 = e(nc.sbuf_tensor("stat_t8", [128, NTT], F32))
    stat_t4 = e(nc.sbuf_tensor("stat_t4", [128, NTT, 4], F32))
    cm_s = e(nc.sbuf_tensor("cm_s", [128, 8], F32))
    cmp_s = e(nc.sbuf_tensor("cmp_s", [128, 4], F32))
    gm_s = e(nc.sbuf_tensor("gm_s", [4, 1], F32))
    negmp = e(nc.sbuf_tensor("negmp", [4, 1], F32))
    negmrow = e(nc.sbuf_tensor("negmrow", [1, 4], F32))
    negmb = e(nc.sbuf_tensor("negmb", [128, 4], F32))

    t2_s = e(nc.sbuf_tensor("t2_s", [128, D], BF16))
    uTB_s = RBP[:, 1024:2048]
    rzgam = RBP[:, 0:1024]
    wu_s = e(nc.sbuf_tensor("wu_s", [128, D], F32))
    qi_v = wu_s[:].bitcast(I32)
    au_s = e(nc.sbuf_tensor("au_s", [128, D], BF16))
    au2_s = e(nc.sbuf_tensor("au2_s", [128, D], BF16))
    mk1_s = e(nc.sbuf_tensor("mk1_s", [128, D], BF16))
    mk2_s = e(nc.sbuf_tensor("mk2_s", [128, D], BF16))
    junk_g = e(nc.sbuf_tensor("junk_g", [1, 8], F32))
    junk_v = e(nc.sbuf_tensor("junk_v", [1, 8], F32))
    junk_s = e(nc.sbuf_tensor("junk_s", [1, 8], F32))
    eps_c = e(nc.sbuf_tensor("eps_c", [128, 1], F32))
    eps2_c = e(nc.sbuf_tensor("eps2_c", [128, 1], F32))
    amax_s = e(nc.sbuf_tensor("amax_s", [128, 32], F32))
    ssc_s = e(nc.sbuf_tensor("ssc_s", [128, 32], F32))
    s8c_s = e(nc.sbuf_tensor("s8c_s", [128, 32], F32))
    rssc_s = e(nc.sbuf_tensor("rssc_s", [128, 32], F32))
    outst = U3[:, 0:4096].bitcast(F32)[:, 0:1024]

    ps_a = e(nc.psum_tensor("ps_a", [128, 2048], F32))
    ps_b = e(nc.psum_tensor("ps_b", [128, 512], F32))
    ps_c = e(nc.psum_tensor("ps_c", [128, 512], F32))
    ps_d = e(nc.psum_tensor("ps_d", [128, 1024], F32))
    ps_a_bf = ps_a[:].bitcast(BF16)

    sem_names = ["sdma", "gdma", "sv", "sa", "stn", "sgc", "ssy", "sout"]
    sems = {n: e(nc.semaphore(n)) for n in sem_names}
    cnt = {n: 0 for n in sem_names}
    ops = {k: [] for k in ["sync", "gpsimd", "vector", "scalar", "tensor"]}

    def inc(name, amt=1):
        cnt[name] += amt
        return cnt[name]

    def op(eng, fn, sem=None, amt=1):
        if sem is None:
            ops[eng].append(lambda E, f=fn: f(E))
            return None
        tgt = inc(sem, amt)
        semh = sems[sem]
        ops[eng].append(lambda E, f=fn, sh=semh, a=amt: f(E).then_inc(sh, a))
        return tgt

    def dma(eng, dst, src, sem=None):
        sem = sem or ("gdma" if eng == "gpsimd" else "sdma")
        return op(eng, lambda E, d=dst, s=src: E.dma_start(out=d, in_=s), sem=sem, amt=16)

    def wait(eng, sem, val):
        semh = sems[sem]
        ops[eng].append(lambda E, sh=semh, v=val: E.wait_ge(sh, v))

    ENG_SEM = {"sync": "ssy", "gpsimd": "sgc", "vector": "sv", "scalar": "sa", "tensor": "stn"}

    def barrier():
        wait("sync", "sdma", cnt["sdma"])
        wait("sync", "sout", cnt["sout"])
        wait("gpsimd", "gdma", cnt["gdma"])
        marks = {}
        marks["sync"] = op("sync", lambda E: E.nop(), sem="ssy")
        marks["gpsimd"] = op("gpsimd", lambda E: E.memset(junk_g[:], 0.0), sem="sgc")
        marks["vector"] = op("vector", lambda E: E.tensor_copy(junk_v[:], junk_g[:]), sem="sv")
        marks["scalar"] = op("scalar", lambda E: E.activation(junk_s[:], junk_v[:], AF.Square), sem="sa")
        marks["tensor"] = op("tensor", lambda E: E.matmul(
            ps_b[0:1, 0:1], ones_rf[0:1, 0:1], ones_rf[0:1, 0:1],
            start=True, stop=True), sem="stn")
        for eng in ENG_SEM:
            for other, sn in ENG_SEM.items():
                if other != eng:
                    wait(eng, sn, marks[other])

    for dst, src in [(bqk_r, bqk), (bv_r, bv), (bout_r, bout), (fbias_r, fbias)]:
        dma("gpsimd", dst[:], src)
    dma("gpsimd", wqk_s[:], wqk.rearrange("(kc p) n -> p kc n", p=128))
    dma("gpsimd", wv_s[:], wv.rearrange("(kc p) n -> p kc n", p=128))
    dma("gpsimd", wout_s[:], wout.rearrange("(kc p) n -> p kc n", p=128))
    op("gpsimd", lambda E: E.iota(iota_a[:], pattern=[[1, 128]], base=0, channel_multiplier=0))
    op("gpsimd", lambda E: E.iota(iota_b[:], pattern=[[0, 128]], base=0, channel_multiplier=1))
    op("gpsimd", lambda E: E.memset(ones_col[:], 1.0))
    op("gpsimd", lambda E: E.memset(ones_rb[:], 1.0))
    op("gpsimd", lambda E: E.memset(ones_rf[:], 1.0))
    op("gpsimd", lambda E: E.memset(oneblkA[:, 0:64], 1.0))
    op("gpsimd", lambda E: E.memset(oneblkA[:, 64:128], 0.0))
    op("gpsimd", lambda E: E.memset(oneblkB[:, 0:64], 0.0))
    op("gpsimd", lambda E: E.memset(oneblkB[:, 64:128], 1.0))
    op("gpsimd", lambda E: E.memset(eps_c[:], EPS))
    op("gpsimd", lambda E: E.memset(eps2_c[:], EPS_S))
    barrier()

    if main_ph == "S0":
        dma("sync", OUTT0[0], MISC[:, 0:1024], sem="sout")
        wait("sync", "sout", cnt["sout"])
        return _finish(nc, ops, st)

    op("vector", lambda E: E.tensor_tensor(identB[:], iota_a[:], iota_b[:], ALU.is_equal))
    op("vector", lambda E: E.tensor_tensor(identF[:], iota_a[:], iota_b[:], ALU.is_equal), sem="sv")
    if main_ph == "Sid":
        dma("sync", OUTT0[0], MISC[:, 0:1024], sem="sout")
        wait("sync", "sout", cnt["sout"])
        return _finish(nc, ops, st)

    for dst, src in [(gamb, gam), (betb, bet)]:
        tl = dma("sync", row_tmp, src)
        wait("tensor", "sdma", tl)
        t = None
        for hh in range(2):
            t = op("tensor", lambda E, hh=hh: E.matmul(
                ps_a[:, hh * 512:(hh + 1) * 512], ones_rf[:],
                row_tmp[:, hh * 512:(hh + 1) * 512], start=True, stop=True),
                sem=("stn" if hh == 1 else None))
        wait("vector", "stn", t)
        tv = op("vector", lambda E, d=dst: E.tensor_copy(d[:], ps_a[:, 0:1024]), sem="sv")
        wait("tensor", "sv", tv)
        wait("sync", "sv", tv)
    if main_ph == "Sgb":
        dma("sync", OUTT0[0], MISC[:, 0:1024], sem="sout")
        wait("sync", "sout", cnt["sout"])
        return _finish(nc, ops, st)

    def colsum(dst_row, w_tile, width):
        nh = width // 512
        t = None
        for hh in range(nh):
            for kc in range(8):
                last = (hh == nh - 1 and kc == 7)
                t = op("tensor", lambda E, hh=hh, kc=kc, w=w_tile: E.matmul(
                    ps_a[0:1, hh * 512:(hh + 1) * 512], ones_col[:],
                    w[:, kc, hh * 512:(hh + 1) * 512], start=(kc == 0), stop=(kc == 7)),
                    sem=("stn" if last else None))
        wait("vector", "stn", t)
        tv = op("vector", lambda E, d=dst_row, n=nh: E.tensor_copy(d[:], ps_a[0:1, 0:512 * n]), sem="sv")
        wait("tensor", "sv", tv)

    colsum(cwqk_r, wqk_s, D)
    colsum(cwv_r, wv_s, VD)
    if main_ph == "S1":
        dma("sync", OUTT0[0], MISC[:, 0:1024], sem="sout")
        wait("sync", "sout", cnt["sout"])
        return _finish(nc, ops, st)

    def emit_quant(win_ap, out_ap=None):
        if out_ap is None:
            out_ap = au_s[:]
        view = lambda t: t.rearrange("p (a b) -> p a b", b=32)
        op("vector", lambda E, w=win_ap: E.tensor_reduce(
            amax_s[:].unsqueeze(2), view(w), AX.X, ALU.max, apply_absolute_value=True))
        op("vector", lambda E: E.drain())
        op("vector", lambda E: E.tensor_scalar(ssc_s[:], amax_s[:], 1.0 / 7.5, 1e-30, ALU.mult, ALU.max))
        op("vector", lambda E: E.drain())
        op("vector", lambda E: E.tensor_scalar(s8c_s[:], ssc_s[:], 0.125, None, ALU.mult))
        op("vector", lambda E: E.reciprocal(rssc_s[:], ssc_s[:]))
        op("vector", lambda E: E.drain())
        rsb = rssc_s[:].unsqueeze(2).broadcast_to([128, 32, 32])
        s8b = s8c_s[:].unsqueeze(2).broadcast_to([128, 32, 32])
        op("vector", lambda E, w=win_ap: E.tensor_tensor(view(wu_s[:]), view(w), rsb, ALU.mult))
        op("vector", lambda E: E.scalar_tensor_tensor(au_s[:], wu_s[:], -1.0, wu_s[:], ALU.mult, ALU.max))
        op("vector", lambda E: E.tensor_scalar(mk1_s[:], au_s[:], 2.0, -4.0, ALU.is_ge, ALU.mult))
        op("vector", lambda E: E.tensor_scalar(mk2_s[:], au_s[:], 4.0, -2.0, ALU.is_ge, ALU.mult))
        op("vector", lambda E: E.scalar_tensor_tensor(mk1_s[:], mk1_s[:], 8.0, mk2_s[:], ALU.add, ALU.add))
        op("vector", lambda E: E.tensor_scalar(mk2_s[:], au_s[:], 4.0, 2.0, ALU.is_ge, ALU.mult))
        op("vector", lambda E: E.tensor_scalar(au_s[:], au_s[:], 2.0, None, ALU.is_ge))
        op("vector", lambda E: E.scalar_tensor_tensor(mk2_s[:], au_s[:], 1.0, mk2_s[:], ALU.add, ALU.add))
        op("vector", lambda E: E.tensor_tensor(qi_v, wu_s[:], mk1_s[:], ALU.mult))
        op("vector", lambda E: E.tensor_tensor(mk1_s[:], qi_v, mk2_s[:], ALU.mult))
        return op("vector", lambda E, o=out_ap: E.tensor_tensor(view(o), view(mk1_s[:]), s8b, ALU.mult), sem="sv")

    _tq_hist = []
    for jc in range(8):
        if jc >= 2:
            wait("sync", "sv", _tq_hist[jc - 2])
        tl = dma("sync", wfst[:, jc % 2, :], wffn.rearrange("(c p) n -> c p n", p=128)[jc])
        wait("vector", "sdma", tl)
        tq = emit_quant(wfst[:, jc % 2, :])
        _tq_hist.append(tq)
        wait("tensor", "sv", tq)
        t = None
        for kc in range(8):
            t = op("tensor", lambda E, kc=kc: E.transpose(
                ps_a_bf[:, kc * 128:(kc + 1) * 128],
                au_s[:, kc * 128:(kc + 1) * 128], identB[:]),
                sem=("stn" if kc == 7 else None))
        wait("vector", "stn", t)
        tv = op("vector", lambda E, jc=jc: E.tensor_copy(
            wffnq[:, :, jc * 128:(jc + 1) * 128],
            ps_a_bf[:, 0:1024].rearrange("p (kc n) -> p kc n", kc=8)), sem="sv")
        wait("tensor", "sv", tv)
    barrier()
    if "wq" in dbg:
        dma("gpsimd", dbg["wq"], wffnq[:].rearrange("p c n -> p (c n)"))
        wait("vector", "gdma", cnt["gdma"])

    def statcombine(negmu, r, eps, rinv=None):
        g = bnb[:].rearrange("p t (a k) -> p t a k", k=3)
        m4 = g[:, :, :, 1]
        v4 = g[:, :, :, 2]
        op("vector", lambda E: E.tensor_reduce(stat_t1[:].unsqueeze(2), m4, AX.X, ALU.add))
        op("vector", lambda E: E.tensor_reduce(stat_t2[:].unsqueeze(2), v4, AX.X, ALU.add))
        op("vector", lambda E: E.tensor_tensor(stat_t4[:], m4, m4, ALU.mult))
        op("vector", lambda E: E.tensor_reduce(stat_t3[:].unsqueeze(2), stat_t4[:], AX.X, ALU.add))
        op("vector", lambda E, n=negmu: E.tensor_scalar(n[:], stat_t1[:], -0.25, None, ALU.mult))
        op("vector", lambda E: E.tensor_scalar(stat_t5[:], stat_t2[:], 1.0 / 1024.0, None, ALU.mult))
        op("vector", lambda E: E.drain())
        op("vector", lambda E: E.scalar_tensor_tensor(stat_t6[:], stat_t3[:], 0.25, stat_t5[:], ALU.mult, ALU.add))
        op("vector", lambda E, n=negmu: E.tensor_tensor(stat_t7[:], n[:], n[:], ALU.mult))
        op("vector", lambda E: E.drain())
        tv = op("vector", lambda E: E.tensor_tensor(stat_t8[:], stat_t6[:], stat_t7[:], ALU.subtract), sem="sv")
        op("vector", lambda E: E.drain())
        wait("scalar", "sv", tv)
        dst_sd = rinv if rinv is not None else stat_t3
        eps_ap = eps_c[:] if eps == EPS else eps2_c[:]
        ta = op("scalar", lambda E, d=dst_sd, ep=eps_ap: E.activation(
            d[:], stat_t8[:], AF.Sqrt, bias=ep, scale=1.0), sem="sa")
        wait("vector", "sa", ta)
        rv = op("vector", lambda E, d=dst_sd, rr=r: E.reciprocal(rr[:], d[:]), sem="sv")
        op("vector", lambda E: E.drain())
        return rv

    def rowify(cols, row_tile):
        t = None
        for tt in range(NTT):
            t = op("tensor", lambda E, tt=tt, c=cols: E.transpose(
                ps_a[0:1, tt * 128:(tt + 1) * 128], c[:, tt:tt + 1], identF[:]),
                sem=("stn" if tt == NTT - 1 else None))
        wait("vector", "stn", t)
        tv = op("vector", lambda E, rt=row_tile: E.tensor_copy(rt, ps_a[0:1, 0:S]), sem="sv")
        wait("tensor", "sv", tv)

    TOKT = x_tok.rearrange("(q p) n -> q p n", p=128)
    OUTT = out.rearrange("(q p) n -> q p n", p=128)

    def dump(name, src_ap):
        if name in dbg:
            dma("gpsimd", dbg[name], src_ap)

    for sq in range(0 if main_ph == "S" else nseq):
        for tt in range(NTT):
            tl = dma("sync", xa_v[:, tt % 2, :], TOKT[sq * NTT + tt])
            wait("vector", "sdma", tl)
            op("vector", lambda E, tt=tt: E.bn_stats(bnb[:, tt, 0:6], xa_v[:, tt % 2, 0:512]))
            tv = op("vector", lambda E, tt=tt: E.bn_stats(bnb[:, tt, 6:12], xa_v[:, tt % 2, 512:1024]), sem="sv")
            wait("sync", "sv", tv)
        if main_ph == "A1":
            continue
        ret = statcombine(negmu0, r0, EPS, rinv=rinv0)
        wait("tensor", "sv", ret)
        if "bnb" in dbg:
            wait("gpsimd", "sv", ret)
            dma("gpsimd", dbg["bnb"], bnb[:].rearrange("p t n -> p (t n)"))
        if "stats" in dbg:
            wait("gpsimd", "sv", ret)
            dma("gpsimd", dbg["stats"].rearrange("p (a n) -> p a n", n=NTT)[:, 0, :], negmu0[:])
            dma("gpsimd", dbg["stats"].rearrange("p (a n) -> p a n", n=NTT)[:, 1, :], r0[:])
            dma("gpsimd", dbg["stats"].rearrange("p (a n) -> p a n", n=NTT)[:, 2, :], rinv0[:])
            dma("gpsimd", dbg["stats"].rearrange("p (a n) -> p a n", n=NTT)[:, 3, :], stat_t1[:])
            dma("gpsimd", dbg["stats"].rearrange("p (a n) -> p a n", n=NTT)[:, 4, :], stat_t2[:])
            dma("gpsimd", dbg["stats"].rearrange("p (a n) -> p a n", n=NTT)[:, 5, :], stat_t3[:])
        if main_ph == "A2":
            continue
        rowify(negmu0, murow[0:1, :])
        rowify(rinv0, rinvrow[0:1, :])
        rowify(r0, r0row)
        if main_ph == "A3":
            continue
        t = None
        for ic in range(NIC):
            t = op("tensor", lambda E, ic=ic: E.matmul(
                ps_a[:, ic * 512:(ic + 1) * 512], ones_rb[:],
                r0row[:, ic * 512:(ic + 1) * 512], start=True, stop=True),
                sem=("stn" if ic == NIC - 1 else None))
        wait("vector", "stn", t)
        tv = op("vector", lambda E: E.tensor_copy(rb_v, ps_a[:, 0:S]), sem="sv")
        wait("tensor", "sv", tv)
        if "rows" in dbg:
            wait("gpsimd", "sv", cnt["sv"])
            rv = dbg["rows"].rearrange("p (a n) -> p a n", n=S)
            dma("gpsimd", rv[:, 0, :], murow[0:1, :])
            dma("gpsimd", rv[:, 1, :], rinvrow[0:1, :])
            dma("gpsimd", rv[:, 2, :], r0row)
        if "rb" in dbg:
            dma("gpsimd", dbg["rb"], rb_v)
        barrier()
        if "B" not in main_ph:
            continue

        XF = x_feat[sq].rearrange("(kc p) n -> p kc n", p=128)
        for ic in range(NIC):
            if ic >= 1:
                wait("gpsimd", "stn", cnt["stn"])
            tl = dma("gpsimd", xfb[:, ic % 2, :, :], XF[:, :, ic * 512:(ic + 1) * 512])
            wait("tensor", "gdma", tl)
            for ct in range(8):
                pb = ps_b if (ct % 2 == 0) else ps_c
                for kc in range(8):
                    op("tensor", lambda E, kc=kc, ct=ct, ic=ic, pb=pb: E.matmul(
                        pb[:], wqk_s[:, kc, ct * 128:(ct + 1) * 128],
                        xfb[:, ic % 2, kc, :], start=(kc == 0), stop=False))
                op("tensor", lambda E, ct=ct, ic=ic, pb=pb: E.matmul(
                    pb[:], cwqk_r[:, ct * 128:(ct + 1) * 128],
                    murow[:, ic * 512:(ic + 1) * 512], start=False, stop=False))
                tmm = op("tensor", lambda E, ct=ct, ic=ic, pb=pb: E.matmul(
                    pb[:], bqk_r[:, ct * 128:(ct + 1) * 128],
                    rinvrow[:, ic * 512:(ic + 1) * 512], start=False, stop=True), sem="stn")
                wait("vector", "stn", tmm)
                dbuf = q_v if ct < 4 else k_v
                tv = op("vector", lambda E, d=dbuf, dc=ct % 4, ic=ic, pb=pb: E.tensor_tensor(
                    d[:, dc, ic * 512:(ic + 1) * 512], pb[:],
                    rb_v[:, ic * 512:(ic + 1) * 512], ALU.mult), sem="sv")
                wait("tensor", "sv", tv)
            for tl_ in range(4):
                tt = ic * 4 + tl_
                ph = tl_ % 2
                for kc in range(8):
                    op("tensor", lambda E, kc=kc, tl_=tl_, ph=ph, ic=ic: E.matmul(
                        ps_d[:, ph * 512:(ph + 1) * 512],
                        xfb[:, ic % 2, kc, tl_ * 128:(tl_ + 1) * 128],
                        wv_s[:, kc, :], start=(kc == 0), stop=False))
                op("tensor", lambda E, tt=tt, ph=ph: E.matmul(
                    ps_d[:, ph * 512:(ph + 1) * 512], murow[:, tt * 128:(tt + 1) * 128],
                    cwv_r[:], start=False, stop=False))
                tmm = op("tensor", lambda E, tt=tt, ph=ph: E.matmul(
                    ps_d[:, ph * 512:(ph + 1) * 512], rinvrow[:, tt * 128:(tt + 1) * 128],
                    bv_r[:], start=False, stop=True), sem="stn")
                wait("vector", "stn", tmm)
                tv = op("vector", lambda E, tt=tt, ph=ph: E.tensor_scalar(
                    v_v[:, tt, :], ps_d[:, ph * 512:(ph + 1) * 512],
                    r0[:, tt:tt + 1], None, ALU.mult), sem="sv")
                wait("tensor", "sv", tv)
        barrier()
        dump("q", q_v.rearrange("p c n -> p (c n)"))
        dump("k", k_v.rearrange("p c n -> p (c n)"))
        dump("v", v_v.rearrange("p c n -> p (c n)"))
        if "C" not in main_ph:
            continue

        for pr in range(NPAIR):
            tmm = None
            for hl in range(2):
                tmm = op("tensor", lambda E, pr=pr, hl=hl: E.matmul(
                    ps_d[:, hl * 512:(hl + 1) * 512],
                    q_v[hl * 64:(hl + 1) * 64, pr, 0:S:16],
                    k_v[hl * 64:(hl + 1) * 64, pr, 0:S:4],
                    start=True, stop=True, tile_position=(hl * 64, 0)),
                    sem=("stn" if hl == 1 else None))
            wait("vector", "stn", tmm)
            op("vector", lambda E, pr=pr: E.reduce_max(
                cm_s[:, 2 * pr:2 * pr + 1], ps_d[:, 0:512], AX.X))
            tv = op("vector", lambda E, pr=pr: E.reduce_max(
                cm_s[:, 2 * pr + 1:2 * pr + 2], ps_d[:, 512:1024], AX.X), sem="sv")
            wait("tensor", "sv", tv)
        evv = cm_s[:].rearrange("p (a b) -> p a b", b=2)
        op("vector", lambda E: E.drain())
        tv = op("vector", lambda E, ev=evv: E.tensor_tensor(
            cmp_s[:].unsqueeze(2), ev[:, :, 0:1], ev[:, :, 1:2], ALU.max), sem="sv")
        op("vector", lambda E: E.drain())
        wait("tensor", "sv", tv)
        t = op("tensor", lambda E: E.transpose(ps_a[0:4, 0:128], cmp_s[:], identF[:]), sem="stn")
        wait("vector", "stn", t)
        op("vector", lambda E: E.reduce_max(gm_s[:], ps_a[0:4, 0:128], AX.X))
        op("vector", lambda E: E.drain())
        tv = op("vector", lambda E: E.tensor_scalar(negmp[:], gm_s[:], -0.125, -2.0, ALU.mult, ALU.add), sem="sv")
        op("vector", lambda E: E.drain())
        wait("tensor", "sv", tv)
        t = op("tensor", lambda E: E.transpose(ps_a[0:1, 256:260], negmp[:], identF[0:4, 0:4]), sem="stn")
        wait("vector", "stn", t)
        tv = op("vector", lambda E: E.tensor_copy(negmrow[:], ps_a[0:1, 256:260]), sem="sv")
        op("vector", lambda E: E.drain())
        wait("tensor", "sv", tv)
        t = op("tensor", lambda E: E.matmul(ps_a[:, 512:516], ones_rf[:], negmrow[:],
                                            start=True, stop=True), sem="stn")
        wait("vector", "stn", t)
        tv = op("vector", lambda E: E.tensor_copy(negmb[:], ps_a[:, 512:516]), sem="sv")
        op("vector", lambda E: E.drain())
        wait("tensor", "sv", tv)
        wait("scalar", "sv", tv)
        if sub_ph == "C1":
            continue

        _nps = 1 if sub_ph == "C2" else NPAIR
        _nis = 1 if sub_ph == "C2" else NIC
        _lcap = 99
        if sub_ph.startswith("L") and sub_ph[1:].isdigit():
            _lcap = int(sub_ph[1:])
        if sub_ph.startswith("P"):
            _lcap = 8 if sub_ph.startswith(("PD", "PC", "PX", "PO", "PZ")) else 1
        _ldone = 0
        _iters = [(p_, i_) for p_ in range(_nps) for i_ in range(_nis)]
        if sub_ph == "LR":
            _iters = [(0, 0), (0, 0)]
            _lcap = 2
        if sub_ph == "LP2":
            _iters = [(2, 0), (2, 1), (2, 2), (2, 3)]
            _lcap = 4
        if sub_ph == "LP2a":
            _iters = [(2, 0)]
            _lcap = 1
        for pr, ic in _iters:
            if True:
                if _ldone >= _lcap:
                    continue
                if _ldone > 0 and _ldone % 4 == 0:
                    barrier()
                _ldone += 1
                t_av = None
                for g in range(8):
                    t_sh = [None, None]
                    for hl in range(2):
                        for jl in range(2):
                            jt = g * 2 + jl
                            sl = hl * 2 + jl
                            t_sh[hl] = op("tensor", lambda E, pr=pr, ic=ic, jt=jt, hl=hl, sl=sl: E.matmul(
                                ps_a[:, sl * 512:(sl + 1) * 512],
                                k_v[hl * 64:(hl + 1) * 64, pr, jt * 128:(jt + 1) * 128],
                                q_v[hl * 64:(hl + 1) * 64, pr, ic * 512:(ic + 1) * 512],
                                start=True, stop=True, tile_position=(hl * 64, 0)),
                                sem=("stn" if jl == 1 else None))
                    first = (g == 0)
                    lastg = (g == 7)
                    t_eh = [None, None]
                    for hl in range(2):
                        wait("scalar", "stn", t_sh[hl])
                        t_eh[hl] = op("scalar", lambda E, pr=pr, g=g, hl=hl: E.activation(
                            eg_v[:, g % 2, hl * 1024:(hl + 1) * 1024],
                            ps_a[:, hl * 1024:(hl + 1) * 1024], AF.Exp,
                            bias=negmb[:, pr:pr + 1], scale=0.125), sem="sa")
                    for hl in range(2):
                        wait("tensor", "sa", t_eh[hl])
                        for jl in range(2):
                            jt = g * 2 + jl
                            sl = hl * 2 + jl
                            h = pr * 2 + hl
                            st_ = (first and jl == 0)
                            sp_ = (lastg and jl == 1)
                            last_op = (hl == 1 and jl == 1)
                            ob_ap = ps_b[0:64, :] if hl == 0 else ps_b[64:128, :]
                            zb_ap = ps_c[0:1, :] if hl == 0 else ps_c[64:65, :]
                            op("tensor", lambda E, jt=jt, sl=sl, h=h, hl=hl, g=g,
                               st_=st_, sp_=sp_, ob=ob_ap: E.matmul(
                                ob, v_v[:, jt, h * 64:(h + 1) * 64],
                                eg_v[:, g % 2, sl * 512:(sl + 1) * 512],
                                start=st_, stop=sp_, tile_position=(0, hl * 64)))
                            t_av = op("tensor", lambda E, jt=jt, sl=sl, hl=hl, g=g,
                               st_=st_, sp_=sp_, zb=zb_ap: E.matmul(
                                zb, ones_col[:],
                                eg_v[:, g % 2, sl * 512:(sl + 1) * 512],
                                start=st_, stop=sp_, tile_position=(0, hl * 64)),
                                sem=("stn" if last_op else None))
                wait("scalar", "stn", t_av)
                idx1 = (pr * 2 + 0) * 4 + ic
                idx2 = (pr * 2 + 1) * 4 + ic
                wait("vector", "stn", cnt["stn"])
                tv = op("vector", lambda E, pr=pr, ic=ic: E.tensor_copy(
                    obuf[:, pr, ic * 512:(ic + 1) * 512], ps_b[:]), sem="sv")
                op("scalar", lambda E: E.copy(ZR[0:1, 0:512], ps_c[0:1, :]))
                ta = op("scalar", lambda E: E.copy(ZR[0:1, 512:1024], ps_c[64:65, :]), sem="sa")
                wait("tensor", "sv", tv)
                wait("tensor", "sa", ta)
                wait("sync", "sa", ta)
                wait("sync", "sv", tv)
                if sub_ph not in ("L2z", "L2n"):
                    wait("gpsimd", "sa", ta)
                    dma("gpsimd", zscr[sq:sq + 1, idx1 * 512:(idx1 + 1) * 512], ZR[0:1, 0:512])
                    tzd = dma("gpsimd", zscr[sq:sq + 1, idx2 * 512:(idx2 + 1) * 512], ZR[0:1, 512:1024])
                    wait("scalar", "gdma", tzd)
        if sub_ph in ("C2", "C3") or sub_ph.startswith("L") or sub_ph.startswith("P"):
            continue
        if "obun" in dbg:
            wait("gpsimd", "sv", cnt["sv"])
            dma("gpsimd", dbg["obun"], obuf.rearrange("p c n -> p (c n)"))
            wait("sync", "gdma", cnt["gdma"])
        wait("sync", "gdma", cnt["gdma"])
        t2_ = dma("sync", zgt, zscr[sq:sq + 1, :].rearrange("p (a b) -> (p a) b", a=32))
        wait("vector", "sdma", t2_)
        tv = op("vector", lambda E: E.reciprocal(rzg, zgt), sem="sv")
        wait("sync", "sv", tv)
        t3_ = dma("sync", rzscr[sq:sq + 1, :].rearrange("p (a b) -> (p a) b", a=32), rzg)
        wait("sync", "sdma", t3_)
        if "zq" in dbg:
            dma("sync", dbg["zq"], zgt)
            dma("sync", dbg["rzq"], rzg)
        if "negm" in dbg:
            dma("sync", dbg["negm"], negmb[:])
        if sub_ph.startswith("PZ"):
            for _pi in range(int(sub_ph[2:])):
                op("tensor", lambda E: E.matmul(
                    ps_c[64:65, :], ones_col[:], eg_v[:, 0, 0:512],
                    start=True, stop=True, tile_position=(0, 64)), sem="stn")
            wait("vector", "stn", cnt["stn"])
        elif sub_ph.startswith("PO"):
            for _pi in range(int(sub_ph[2:])):
                qo = (_pi * 64) % 1536
                op("tensor", lambda E, qo=qo: E.matmul(
                    ps_a[:, 0:512], k_v[0:64, 3, qo:qo + 128],
                    q_v[0:64, 3, qo + 64:qo + 576], start=True, stop=True,
                    tile_position=(0, 0)))
            op("tensor", lambda E: E.matmul(ps_b[0:1, 0:1], ones_rf[0:1, 0:1],
                                            ones_rf[0:1, 0:1], start=True, stop=True), sem="stn")
            wait("vector", "stn", cnt["stn"])
        elif sub_ph.startswith("PX"):
            for _pi in range(int(sub_ph[2:])):
                op("scalar", lambda E: E.activation(
                    eg_v[:, 0, :], ps_a[:, 0:2048], AF.Exp,
                    bias=negmb[:, 0:1], scale=0.125))
            op("scalar", lambda E: E.copy(junk_s[:], junk_v[:]), sem="sa")
        elif sub_ph.startswith("PC"):
            for _pi in range(int(sub_ph[2:])):
                op("scalar", lambda E: E.copy(ZR[0:1, 0:512], ps_c[0:1, :]))
                op("scalar", lambda E: E.copy(ZR[0:1, 512:1024], ps_c[64:65, :]))
            op("scalar", lambda E: E.copy(junk_s[:], junk_v[:]), sem="sa")
        elif sub_ph.startswith("PD"):
            for _pi in range(int(sub_ph[2:])):
                dma("gpsimd", junk_v[:], zscr[0:1, 0:8])
            wait("gpsimd", "gdma", cnt["gdma"])
        elif sub_ph.startswith("PS"):
            for _pi in range(int(sub_ph[2:])):
                op("scalar", lambda E: E.activation(junk_s[:], junk_v[:], AF.Square), sem="sa")
                op("tensor", lambda E: E.matmul(ps_b[0:1, 0:1], ones_rf[0:1, 0:1],
                                                ones_rf[0:1, 0:1], start=True, stop=True), sem="stn")
        elif sub_ph.startswith("PW"):
            for _pi in range(int(sub_ph[2:])):
                wait("tensor", "sv", cnt["sv"])
                wait("scalar", "stn", cnt["stn"])
                wait("vector", "sa", cnt["sa"])
        elif sub_ph.startswith("PT"):
            for _pi in range(int(sub_ph[2:])):
                op("tensor", lambda E: E.matmul(ps_b[0:1, 0:1], ones_rf[0:1, 0:1],
                                                ones_rf[0:1, 0:1], start=True, stop=True))
            op("tensor", lambda E: E.matmul(ps_b[0:1, 0:1], ones_rf[0:1, 0:1],
                                            ones_rf[0:1, 0:1], start=True, stop=True), sem="stn")
        elif sub_ph.startswith("PA"):
            for _pi in range(int(sub_ph[2:])):
                op("scalar", lambda E: E.activation(junk_s[:], junk_v[:], AF.Square))
            op("scalar", lambda E: E.activation(junk_s[:], junk_v[:], AF.Square), sem="sa")
        elif sub_ph.startswith("P"):
            for _pi in range(int(sub_ph[1:])):
                op("vector", lambda E: E.tensor_copy(junk_v[:], junk_g[:]))
            op("vector", lambda E: E.tensor_copy(junk_v[:], junk_g[:]), sem="sv")
        wait("gpsimd", "sdma", cnt["sdma"])
        _cap = 99
        if sub_ph.startswith("Cn"):
            _cap = int(sub_ph[2:])
        _done = 0
        for pr in range(0 if sub_ph == "C4" else NPAIR):
            for ic in range(NIC):
                if _done >= _cap:
                    continue
                _done += 1
                idx1 = (pr * 2 + 0) * 4 + ic
                idx2 = (pr * 2 + 1) * 4 + ic
                ph = (pr * NIC + ic) % 2
                rzv = rzscr[sq:sq + 1, :].rearrange("p (a b) -> (p a) b", a=32)
                dma("gpsimd", msb_v[:, ph, :][0:64, :],
                    rzv[idx1:idx1 + 1, :].broadcast_to([64, 512]))
                tg = dma("gpsimd", msb_v[:, ph, :][64:128, :],
                         rzv[idx2:idx2 + 1, :].broadcast_to([64, 512]))
                wait("vector", "gdma", tg)
                tv = op("vector", lambda E, pr=pr, ic=ic, ph=ph: E.tensor_tensor(
                    obuf[:, pr, ic * 512:(ic + 1) * 512],
                    obuf[:, pr, ic * 512:(ic + 1) * 512],
                    msb_v[:, ph, :], ALU.mult), sem="sv")
                wait("gpsimd", "sv", tv)
        barrier()
        dump("obuf", obuf.rearrange("p c n -> p (c n)"))
        if "D" not in main_ph:
            continue

        for tt in range(NTT):
            for vc in range(4):
                for hh in range(2):
                    op("tensor", lambda E, vc=vc, hh=hh, tt=tt: E.matmul(
                        ps_d[:, hh * 512:(hh + 1) * 512],
                        obuf[:, vc, tt * 128:(tt + 1) * 128],
                        wout_s[:, vc, hh * 512:(hh + 1) * 512],
                        start=(vc == 0), stop=False))
            for hh in range(2):
                op("tensor", lambda E, hh=hh: E.matmul(
                    ps_d[:, hh * 512:(hh + 1) * 512], ones_rb[:],
                    bout_r[:, hh * 512:(hh + 1) * 512], start=False, stop=True),
                    sem=("stn" if hh == 1 else None))
            wait("vector", "stn", cnt["stn"])
            op("vector", lambda E, tt=tt: E.bn_stats(bnb[:, tt, 0:6], ps_d[:, 0:512]))
            op("vector", lambda E, tt=tt: E.bn_stats(bnb[:, tt, 6:12], ps_d[:, 512:1024]))
            tv = op("vector", lambda E, tt=tt: E.tensor_copy(zbuf[:, tt, :], ps_d[:, 0:1024]), sem="sv")
            wait("tensor", "sv", tv)
        statcombine(negmuz, rz_s, EPS)
        dump("zbuf", zbuf.rearrange("p t n -> p (t n)"))
        for tt in range(NTT):
            tl = dma("gpsimd", xtok_v[:, tt % 2, :], TOKT[sq * NTT + tt])
            op("vector", lambda E, tt=tt: E.tensor_scalar(
                rzgam[:], gamb[:], rz_s[:, tt:tt + 1], None, ALU.mult))
            op("vector", lambda E, tt=tt: E.scalar_tensor_tensor(
                t2_s[:], zbuf[:, tt, :], negmuz[:, tt:tt + 1], rzgam[:], ALU.add, ALU.mult))
            wait("vector", "gdma", tl)
            op("vector", lambda E, tt=tt: E.tensor_scalar(
                uTB_s[:], xtok_v[:, tt % 2, :], negmu0[:, tt:tt + 1], r0[:, tt:tt + 1],
                ALU.add, ALU.mult))
            op("vector", lambda E: E.tensor_tensor(t2_s[:], t2_s[:], uTB_s[:], ALU.add))
            op("vector", lambda E, tt=tt: E.tensor_tensor(sb_v[:, tt, :], t2_s[:], betb[:], ALU.add))
            op("vector", lambda E, tt=tt: E.bn_stats(bnb[:, tt, 0:6], sb_v[:, tt, 0:512]))
            tv = op("vector", lambda E, tt=tt: E.bn_stats(bnb[:, tt, 6:12], sb_v[:, tt, 512:1024]), sem="sv")
            wait("gpsimd", "sv", tv)
        statcombine(negmus, rs_s, EPS_S)
        dump("sbuf", sb_v.rearrange("p t n -> p (t n)"))
        for tt in range(NTT):
            tl = dma("gpsimd", xtok_v[:, tt % 2, :], TOKT[sq * NTT + tt])
            op("vector", lambda E, tt=tt: E.tensor_scalar(
                t2_s[:], sb_v[:, tt, :], negmus[:, tt:tt + 1], rs_s[:, tt:tt + 1],
                ALU.add, ALU.mult))
            wait("vector", "gdma", tl)
            op("vector", lambda E, tt=tt: E.tensor_tensor(
                x1buf[:, tt, :], xtok_v[:, tt % 2, :], t2_s[:], ALU.add))
            op("vector", lambda E, tt=tt: E.bn_stats(bnb[:, tt, 0:6], x1buf[:, tt, 0:512]))
            tv = op("vector", lambda E, tt=tt: E.bn_stats(bnb[:, tt, 6:12], x1buf[:, tt, 512:1024]), sem="sv")
            wait("gpsimd", "sv", tv)
        statcombine(negmu1, r1_s, EPS)
        dump("x1", x1buf.rearrange("p t n -> p (t n)"))
        tq_p4 = [None] * NTT
        tbn_p4 = [None] * NTT
        tfc_p4 = [None] * NTT
        t_xc_prev = None
        for i in range(NTT + 1):
            if i > 0 and i % 4 == 0:
                barrier()
            if i < NTT:
                tw = op("vector", lambda E, tt=i: E.tensor_scalar(
                    wu_s[:], x1buf[:, tt, :], negmu1[:, tt:tt + 1], r1_s[:, tt:tt + 1],
                    ALU.add, ALU.mult), sem=("sv" if (i == 0 and "w0" in dbg) else None))
                if i == 0 and "w0" in dbg:
                    wait("gpsimd", "sv", tw)
                    dma("gpsimd", dbg["w0"], wu_s[:])
                    wait("vector", "gdma", cnt["gdma"])
                au_i = au_s if i % 2 == 0 else au2_s
                tq_p4[i] = emit_quant(wu_s[:], au_i[:])
                if i == 0 and "xd" in dbg:
                    wait("gpsimd", "sv", tq_p4[i])
                    dma("gpsimd", dbg["xd"], au_i[:])
                    wait("vector", "gdma", cnt["gdma"])
            if i >= 1:
                j = i - 1
                au_j = au_s if j % 2 == 0 else au2_s
                xdtf_j = MISC[:, 1024:1536].bitcast(BF16) if j % 2 == 0 else MISC[:, 1536:2048].bitcast(BF16)
                xdt_j = xdt_v if j % 2 == 0 else xdt2_v
                psd_j = ps_d if j % 2 == 0 else ps_a
                wait("tensor", "sv", tq_p4[j])
                if t_xc_prev is not None:
                    wait("tensor", "sv", t_xc_prev)
                t = None
                for kc in range(8):
                    t = op("tensor", lambda E, kc=kc, a=au_j: E.transpose(
                        ps_a_bf[:, 2048 + kc * 128:2048 + (kc + 1) * 128],
                        a[:, kc * 128:(kc + 1) * 128], identB[:]),
                        sem=("stn" if kc == 7 else None))
                wait("vector", "stn", t)
                t_xc = op("vector", lambda E, d=xdtf_j: E.tensor_copy(d, ps_a_bf[:, 2048:3072]), sem="sv")
                t_xc_prev = t_xc
                wait("tensor", "sv", t_xc)
                if j >= 2:
                    wait("tensor", "sv", tbn_p4[j - 2])
                    wait("tensor", "sv", tfc_p4[j - 2])
                for kc in range(8):
                    for hh in range(2):
                        op("tensor", lambda E, kc=kc, hh=hh, x=xdt_j, pd=psd_j: E.matmul(
                            pd[:, hh * 512:(hh + 1) * 512],
                            x[:, kc, :],
                            wffnq[:, kc, hh * 512:(hh + 1) * 512],
                            start=(kc == 0), stop=False))
                t_f = None
                for hh in range(2):
                    t_f = op("tensor", lambda E, hh=hh, pd=psd_j: E.matmul(
                        pd[:, hh * 512:(hh + 1) * 512], ones_rb[:],
                        fbias_r[:, hh * 512:(hh + 1) * 512], start=False, stop=True),
                        sem=("stn" if hh == 1 else None))
                wait("vector", "stn", t_f)
                op("vector", lambda E, tt=j, pd=psd_j: E.bn_stats(bnb[:, tt, 0:6], pd[:, 0:512]))
                tbn_p4[j] = op("vector", lambda E, tt=j, pd=psd_j: E.bn_stats(
                    bnb[:, tt, 6:12], pd[:, 512:1024]), sem="sv")
                tfc_p4[j] = op("vector", lambda E, tt=j, pd=psd_j: E.tensor_copy(
                    fbuf[:, tt, :], pd[:, 0:1024]), sem="sv")
        statcombine(negmuf, rf_s, EPS)
        dump("fbuf", fbuf.rearrange("p t n -> p (t n)"))
        for tt in range(NTT):
            if tt >= 1:
                wait("vector", "sout", cnt["sout"])
            op("vector", lambda E, tt=tt: E.tensor_scalar(
                t2_s[:], fbuf[:, tt, :], negmuf[:, tt:tt + 1], rf_s[:, tt:tt + 1],
                ALU.add, ALU.mult))
            tv = op("vector", lambda E, tt=tt: E.tensor_tensor(
                outst[:], x1buf[:, tt, :], t2_s[:], ALU.add), sem="sv")
            wait("sync", "sv", tv)
            dma("sync", OUTT[sq * NTT + tt], outst[:], sem="sout")
        barrier()

    if cnt["sout"] == 0:
        dma("sync", OUTT[0], MISC[:, 0:1024], sem="sout")
    wait("sync", "sout", cnt["sout"])
    wait("gpsimd", "gdma", cnt["gdma"])

    with nc.Block() as blk:
        @blk.sync
        def _(E):
            for f in ops["sync"]:
                f(E)

        @blk.gpsimd
        def _(E):
            for f in ops["gpsimd"]:
                f(E)

        @blk.vector
        def _(E):
            for f in ops["vector"]:
                f(E)

        @blk.scalar
        def _(E):
            for f in ops["scalar"]:
                f(E)

        @blk.tensor
        def _(E):
            for f in ops["tensor"]:
                f(E)

    return nc, st


def _qk_perm():
    q = [h * 128 + 2 * d for h in range(H) for d in range(DH)]
    k = [h * 128 + 2 * d + 1 for h in range(H) for d in range(DH)]
    return np.array(q + k, dtype=np.int64)


def kernel(L0_attn_x, L0_attn_W_qk, L0_attn_b_qk, L0_attn_W_v, L0_attn_b_v,
           L0_attn_W_out, L0_attn_b_out, L0_attn_ln_gamma, L0_attn_ln_beta,
           L0_ffn_weight, L0_ffn_bias):
    x = np.ascontiguousarray(np.asarray(L0_attn_x, dtype=np.float32))
    perm = _qk_perm()
    wqk_p = np.ascontiguousarray(np.asarray(L0_attn_W_qk, np.float32)[:, perm])
    bqk_p = np.ascontiguousarray(np.asarray(L0_attn_b_qk, np.float32)[perm][None, :])
    wv_ = np.ascontiguousarray(np.asarray(L0_attn_W_v, np.float32))
    bv_ = np.ascontiguousarray(np.asarray(L0_attn_b_v, np.float32)[None, :])
    wout_ = np.ascontiguousarray(np.asarray(L0_attn_W_out, np.float32))
    bout_ = np.ascontiguousarray(np.asarray(L0_attn_b_out, np.float32)[None, :])
    gam_ = np.ascontiguousarray(np.asarray(L0_attn_ln_gamma, np.float32)[None, :])
    bet_ = np.ascontiguousarray(np.asarray(L0_attn_ln_beta, np.float32)[None, :])
    wffn_ = np.ascontiguousarray(np.asarray(L0_ffn_weight, np.float32))
    fbias_ = np.ascontiguousarray(np.asarray(L0_ffn_bias, np.float32)[None, :])

    nc, stack = build_program()
    try:
        in_maps = []
        for c in range(8):
            xs = x[c * NSEQ:(c + 1) * NSEQ]
            in_maps.append(dict(
                x_tok=np.ascontiguousarray(xs.reshape(NSEQ * S, D)),
                x_feat=np.ascontiguousarray(xs.transpose(0, 2, 1)),
                wqk=wqk_p, wv=wv_, wout=wout_, wffn=wffn_, bqk=bqk_p, bv=bv_,
                bout=bout_, gam=gam_, bet=bet_, fbias=fbias_))
        res = run_bass_kernel_spmd(nc, in_maps, core_ids=list(range(8)))
    finally:
        stack.close()
    outs = [res.results[c]["out"].reshape(NSEQ, S, D) for c in range(8)]
    return np.concatenate(outs, axis=0)


# version 10
# speedup vs baseline: 1.2754x; 1.0780x over previous
import numpy as np
import contextlib
import concourse.bass as bass
import concourse.mybir as mybir
from concourse.bass_utils import run_bass_kernel_spmd

F32 = mybir.dt.float32
BF16 = mybir.dt.bfloat16
I32 = mybir.dt.int32
ALU = mybir.AluOpType
AF = mybir.ActivationFunctionType
AX = mybir.AxisListType

NSEQ, S, D, H, DH, VD = 4, 2048, 1024, 8, 64, 512
NTT, NIC, NJT, NPAIR = 16, 4, 16, 4
EPS = 1e-5
EPS_S = 2e-5


def _finish(nc, ops, st):
    with nc.Block() as blk:
        @blk.sync
        def _(E):
            for f in ops["sync"]:
                f(E)

        @blk.gpsimd
        def _(E):
            for f in ops["gpsimd"]:
                f(E)

        @blk.vector
        def _(E):
            for f in ops["vector"]:
                f(E)

        @blk.scalar
        def _(E):
            for f in ops["scalar"]:
                f(E)

        @blk.tensor
        def _(E):
            for f in ops["tensor"]:
                f(E)
    return nc, st


def build_program(nseq=NSEQ, phases="ABCD", debug_dump=()):
    main_ph, _, sub_ph = phases.partition(".")
    nc = bass.Bass("TRN2", target_bir_lowering=False)

    x_tok = nc.dram_tensor("x_tok", [NSEQ * S, D], F32, kind="ExternalInput").ap()
    x_feat = nc.dram_tensor("x_feat", [NSEQ, D, S], F32, kind="ExternalInput").ap()
    wqk = nc.dram_tensor("wqk", [D, D], F32, kind="ExternalInput").ap()
    wv = nc.dram_tensor("wv", [D, VD], F32, kind="ExternalInput").ap()
    wout = nc.dram_tensor("wout", [VD, D], F32, kind="ExternalInput").ap()
    wffn = nc.dram_tensor("wffn", [D, D], F32, kind="ExternalInput").ap()
    bqk = nc.dram_tensor("bqk", [1, D], F32, kind="ExternalInput").ap()
    bv = nc.dram_tensor("bv", [1, VD], F32, kind="ExternalInput").ap()
    bout = nc.dram_tensor("bout", [1, D], F32, kind="ExternalInput").ap()
    gam = nc.dram_tensor("gam", [1, D], F32, kind="ExternalInput").ap()
    bet = nc.dram_tensor("bet", [1, D], F32, kind="ExternalInput").ap()
    fbias = nc.dram_tensor("fbias", [1, D], F32, kind="ExternalInput").ap()
    out = nc.dram_tensor("out", [NSEQ * S, D], F32, kind="ExternalOutput").ap()
    zscr = nc.dram_tensor("zscr", [NSEQ, 32 * 512], F32, kind="Internal").ap()
    rzscr = nc.dram_tensor("rzscr", [NSEQ, 32 * 512], F32, kind="Internal").ap()
    dbg = {}
    for name in debug_dump:
        shapes = {"q": [128, 4 * S], "k": [128, 4 * S], "v": [128, NJT * VD],
                  "obuf": [128, 4 * S], "zrow": [1, 32 * 512], "stats": [128, 6 * NTT],
                  "rows": [1, 4 * S], "rb": [128, S], "bnb": [128, NTT * 12], "zq": [32, 512], "rzq": [32, 512], "negm": [128, 4], "obun": [128, 4 * S],
                  "zbuf": [128, NTT * D], "sbuf": [128, NTT * D], "x1": [128, NTT * D],
                  "fbuf": [128, NTT * D], "xd": [128, D], "xdt": [128, D],
                  "wq": [128, 8 * D], "w0": [128, D]}
        dbg[name] = nc.dram_tensor("dbg_" + name, shapes[name], F32, kind="ExternalOutput").ap()

    OUTT0 = out.rearrange("(q p) n -> q p n", p=128)

    st = contextlib.ExitStack()
    e = st.enter_context

    wqk_s = e(nc.sbuf_tensor("wqk_s", [128, 8, D], BF16))
    wv_s = e(nc.sbuf_tensor("wv_s", [128, 8, VD], BF16))
    wout_s = e(nc.sbuf_tensor("wout_s", [128, 4, D], BF16))
    wffnq = e(nc.sbuf_tensor("wffnq", [128, 8, D], BF16))

    U1 = e(nc.sbuf_tensor("U1", [128, 16384], BF16))
    U2 = e(nc.sbuf_tensor("U2", [128, 16384], BF16))
    U3 = e(nc.sbuf_tensor("U3", [128, 16384], BF16))
    q_v = U1[:, 0:8192].rearrange("p (c n) -> p c n", c=4)
    k_v = U1[:, 8192:16384].rearrange("p (c n) -> p c n", c=4)
    zbuf = U1[:].rearrange("p (t n) -> p t n", t=NTT)
    fbuf = zbuf
    v_v = U2[:, 0:8192].rearrange("p (t n) -> p t n", t=NJT)
    obuf = U2[:, 8192:16384].rearrange("p (c n) -> p c n", c=4)
    x1buf = U2[:].rearrange("p (t n) -> p t n", t=NTT)
    xa_v = U3[:, 0:4096].bitcast(F32).rearrange("p (a n) -> p a n", a=2)
    wfst = U3[:, 0:4096].bitcast(F32).rearrange("p (a n) -> p a n", a=2)
    xfb = U3[:, 4096:12288].rearrange("p (a c n) -> p a c n", a=2, c=8)
    eg_v = U3[:, 12288:16384].rearrange("p (a n) -> p a n", a=2)
    sb_v = U3[:].rearrange("p (t n) -> p t n", t=NTT)

    identB = e(nc.sbuf_tensor("identB", [128, 128], BF16))
    identF = e(nc.sbuf_tensor("identF", [128, 128], F32))
    iota_a = U1[:, 0:256].bitcast(I32)
    iota_b = U1[:, 256:512].bitcast(I32)
    gamb = e(nc.sbuf_tensor("gamb", [128, D], BF16))
    betb = e(nc.sbuf_tensor("betb", [128, D], BF16))
    ones_col = e(nc.sbuf_tensor("ones_col", [128, 1], BF16))
    ones_rb = e(nc.sbuf_tensor("ones_rb", [1, 128], BF16))
    ones_rf = e(nc.sbuf_tensor("ones_rf", [1, 128], F32))
    oneblkA = e(nc.sbuf_tensor("oneblkA", [1, 128], F32))
    oneblkB = e(nc.sbuf_tensor("oneblkB", [1, 128], F32))

    bqk_r = e(nc.sbuf_tensor("bqk_r", [1, D], BF16))
    bv_r = e(nc.sbuf_tensor("bv_r", [1, VD], BF16))
    bout_r = e(nc.sbuf_tensor("bout_r", [1, D], BF16))
    fbias_r = e(nc.sbuf_tensor("fbias_r", [1, D], BF16))
    cwqk_r = e(nc.sbuf_tensor("cwqk_r", [1, D], BF16))
    cwv_r = e(nc.sbuf_tensor("cwv_r", [1, VD], BF16))

    murow = e(nc.sbuf_tensor("murow", [1, S], BF16))
    rinvrow = e(nc.sbuf_tensor("rinvrow", [1, S], BF16))
    RBP = e(nc.sbuf_tensor("RBP", [128, 2048], BF16))
    rb_v = RBP[:]
    msb_v = RBP[:, 0:1024].rearrange("p (a n) -> p a n", a=2)

    ZR = e(nc.sbuf_tensor("ZR", [1, 2048], F32))
    r0row = ZR[0:1, :].bitcast(BF16)[:, 0:2048]

    MISC = e(nc.sbuf_tensor("MISC", [128, 2048], F32))
    zgt = MISC[0:32, 0:512]
    rzg = MISC[0:32, 512:1024]
    xtok_v = MISC[:, 0:1024].bitcast(BF16).rearrange("p (a n) -> p a n", a=2)
    row_tmp = MISC[0:1, 1024:2048]
    xdt_v = MISC[:, 1024:1536].bitcast(BF16).rearrange("p (c n) -> p c n", c=8)
    xdt2_v = MISC[:, 1536:2048].bitcast(BF16).rearrange("p (c n) -> p c n", c=8)

    bnb = e(nc.sbuf_tensor("bnb", [128, NTT, 12], F32))
    negmu0 = e(nc.sbuf_tensor("negmu0", [128, NTT], F32))
    r0 = e(nc.sbuf_tensor("r0", [128, NTT], F32))
    rinv0 = e(nc.sbuf_tensor("rinv0", [128, NTT], F32))
    negmuz = e(nc.sbuf_tensor("negmuz", [128, NTT], F32))
    rz_s = e(nc.sbuf_tensor("rz_s", [128, NTT], F32))
    negmus = e(nc.sbuf_tensor("negmus", [128, NTT], F32))
    rs_s = e(nc.sbuf_tensor("rs_s", [128, NTT], F32))
    negmu1 = e(nc.sbuf_tensor("negmu1", [128, NTT], F32))
    r1_s = e(nc.sbuf_tensor("r1_s", [128, NTT], F32))
    negmuf = e(nc.sbuf_tensor("negmuf", [128, NTT], F32))
    rf_s = e(nc.sbuf_tensor("rf_s", [128, NTT], F32))
    stat_t1 = e(nc.sbuf_tensor("stat_t1", [128, NTT], F32))
    stat_t2 = e(nc.sbuf_tensor("stat_t2", [128, NTT], F32))
    stat_t3 = e(nc.sbuf_tensor("stat_t3", [128, NTT], F32))
    stat_t5 = e(nc.sbuf_tensor("stat_t5", [128, NTT], F32))
    stat_t6 = e(nc.sbuf_tensor("stat_t6", [128, NTT], F32))
    stat_t7 = e(nc.sbuf_tensor("stat_t7", [128, NTT], F32))
    stat_t8 = e(nc.sbuf_tensor("stat_t8", [128, NTT], F32))
    stat_t4 = e(nc.sbuf_tensor("stat_t4", [128, NTT, 4], F32))
    cm_s = e(nc.sbuf_tensor("cm_s", [128, 8], F32))
    cmp_s = e(nc.sbuf_tensor("cmp_s", [128, 4], F32))
    gm_s = e(nc.sbuf_tensor("gm_s", [4, 1], F32))
    negmp = e(nc.sbuf_tensor("negmp", [4, 1], F32))
    negmrow = e(nc.sbuf_tensor("negmrow", [1, 4], F32))
    negmb = e(nc.sbuf_tensor("negmb", [128, 4], F32))

    t2_s = e(nc.sbuf_tensor("t2_s", [128, D], BF16))
    uTB_s = RBP[:, 1024:2048]
    rzgam = RBP[:, 0:1024]
    wu_s = e(nc.sbuf_tensor("wu_s", [128, D], F32))
    qi_v = wu_s[:].bitcast(I32)
    au_s = e(nc.sbuf_tensor("au_s", [128, D], BF16))
    au2_s = e(nc.sbuf_tensor("au2_s", [128, D], BF16))
    mk1_s = e(nc.sbuf_tensor("mk1_s", [128, D], BF16))
    mk2_s = e(nc.sbuf_tensor("mk2_s", [128, D], BF16))
    junk_g = e(nc.sbuf_tensor("junk_g", [1, 8], F32))
    junk_v = e(nc.sbuf_tensor("junk_v", [1, 8], F32))
    junk_s = e(nc.sbuf_tensor("junk_s", [1, 8], F32))
    eps_c = e(nc.sbuf_tensor("eps_c", [128, 1], F32))
    eps2_c = e(nc.sbuf_tensor("eps2_c", [128, 1], F32))
    amax_s = e(nc.sbuf_tensor("amax_s", [128, 32], F32))
    ssc_s = e(nc.sbuf_tensor("ssc_s", [128, 32], F32))
    s8c_s = e(nc.sbuf_tensor("s8c_s", [128, 32], F32))
    rssc_s = e(nc.sbuf_tensor("rssc_s", [128, 32], F32))
    outst = U3[:, 0:4096].bitcast(F32)[:, 0:1024]

    ps_a = e(nc.psum_tensor("ps_a", [128, 2048], F32))
    ps_b = e(nc.psum_tensor("ps_b", [128, 512], F32))
    ps_c = e(nc.psum_tensor("ps_c", [128, 512], F32))
    ps_d = e(nc.psum_tensor("ps_d", [128, 1024], F32))
    ps_a_bf = ps_a[:].bitcast(BF16)

    sem_names = ["sdma", "gdma", "sv", "sa", "stn", "sgc", "ssy", "sout"]
    sems = {n: e(nc.semaphore(n)) for n in sem_names}
    cnt = {n: 0 for n in sem_names}
    ops = {k: [] for k in ["sync", "gpsimd", "vector", "scalar", "tensor"]}

    def inc(name, amt=1):
        cnt[name] += amt
        return cnt[name]

    def op(eng, fn, sem=None, amt=1):
        if sem is None:
            ops[eng].append(lambda E, f=fn: f(E))
            return None
        tgt = inc(sem, amt)
        semh = sems[sem]
        ops[eng].append(lambda E, f=fn, sh=semh, a=amt: f(E).then_inc(sh, a))
        return tgt

    def dma(eng, dst, src, sem=None):
        sem = sem or ("gdma" if eng == "gpsimd" else "sdma")
        return op(eng, lambda E, d=dst, s=src: E.dma_start(out=d, in_=s), sem=sem, amt=16)

    def wait(eng, sem, val):
        semh = sems[sem]
        ops[eng].append(lambda E, sh=semh, v=val: E.wait_ge(sh, v))

    ENG_SEM = {"sync": "ssy", "gpsimd": "sgc", "vector": "sv", "scalar": "sa", "tensor": "stn"}

    def barrier():
        wait("sync", "sdma", cnt["sdma"])
        wait("sync", "sout", cnt["sout"])
        wait("gpsimd", "gdma", cnt["gdma"])
        marks = {}
        marks["sync"] = op("sync", lambda E: E.nop(), sem="ssy")
        marks["gpsimd"] = op("gpsimd", lambda E: E.memset(junk_g[:], 0.0), sem="sgc")
        marks["vector"] = op("vector", lambda E: E.tensor_copy(junk_v[:], junk_g[:]), sem="sv")
        marks["scalar"] = op("scalar", lambda E: E.activation(junk_s[:], junk_v[:], AF.Square), sem="sa")
        marks["tensor"] = op("tensor", lambda E: E.matmul(
            ps_b[0:1, 0:1], ones_rf[0:1, 0:1], ones_rf[0:1, 0:1],
            start=True, stop=True), sem="stn")
        for eng in ENG_SEM:
            for other, sn in ENG_SEM.items():
                if other != eng:
                    wait(eng, sn, marks[other])

    for dst, src in [(bqk_r, bqk), (bv_r, bv), (bout_r, bout), (fbias_r, fbias)]:
        dma("gpsimd", dst[:], src)
    dma("gpsimd", wqk_s[:], wqk.rearrange("(kc p) n -> p kc n", p=128))
    dma("gpsimd", wv_s[:], wv.rearrange("(kc p) n -> p kc n", p=128))
    dma("gpsimd", wout_s[:], wout.rearrange("(kc p) n -> p kc n", p=128))
    op("gpsimd", lambda E: E.iota(iota_a[:], pattern=[[1, 128]], base=0, channel_multiplier=0))
    op("gpsimd", lambda E: E.iota(iota_b[:], pattern=[[0, 128]], base=0, channel_multiplier=1))
    op("gpsimd", lambda E: E.memset(ones_col[:], 1.0))
    op("gpsimd", lambda E: E.memset(ones_rb[:], 1.0))
    op("gpsimd", lambda E: E.memset(ones_rf[:], 1.0))
    op("gpsimd", lambda E: E.memset(oneblkA[:, 0:64], 1.0))
    op("gpsimd", lambda E: E.memset(oneblkA[:, 64:128], 0.0))
    op("gpsimd", lambda E: E.memset(oneblkB[:, 0:64], 0.0))
    op("gpsimd", lambda E: E.memset(oneblkB[:, 64:128], 1.0))
    op("gpsimd", lambda E: E.memset(eps_c[:], EPS))
    op("gpsimd", lambda E: E.memset(eps2_c[:], EPS_S))
    barrier()

    if main_ph == "S0":
        dma("sync", OUTT0[0], MISC[:, 0:1024], sem="sout")
        wait("sync", "sout", cnt["sout"])
        return _finish(nc, ops, st)

    op("vector", lambda E: E.tensor_tensor(identB[:], iota_a[:], iota_b[:], ALU.is_equal))
    op("vector", lambda E: E.tensor_tensor(identF[:], iota_a[:], iota_b[:], ALU.is_equal), sem="sv")
    if main_ph == "Sid":
        dma("sync", OUTT0[0], MISC[:, 0:1024], sem="sout")
        wait("sync", "sout", cnt["sout"])
        return _finish(nc, ops, st)

    for dst, src in [(gamb, gam), (betb, bet)]:
        tl = dma("sync", row_tmp, src)
        wait("tensor", "sdma", tl)
        t = None
        for hh in range(2):
            t = op("tensor", lambda E, hh=hh: E.matmul(
                ps_a[:, hh * 512:(hh + 1) * 512], ones_rf[:],
                row_tmp[:, hh * 512:(hh + 1) * 512], start=True, stop=True),
                sem=("stn" if hh == 1 else None))
        wait("vector", "stn", t)
        tv = op("vector", lambda E, d=dst: E.tensor_copy(d[:], ps_a[:, 0:1024]), sem="sv")
        wait("tensor", "sv", tv)
        wait("sync", "sv", tv)
    if main_ph == "Sgb":
        dma("sync", OUTT0[0], MISC[:, 0:1024], sem="sout")
        wait("sync", "sout", cnt["sout"])
        return _finish(nc, ops, st)

    def colsum(dst_row, w_tile, width):
        nh = width // 512
        t = None
        for hh in range(nh):
            for kc in range(8):
                last = (hh == nh - 1 and kc == 7)
                t = op("tensor", lambda E, hh=hh, kc=kc, w=w_tile: E.matmul(
                    ps_a[0:1, hh * 512:(hh + 1) * 512], ones_col[:],
                    w[:, kc, hh * 512:(hh + 1) * 512], start=(kc == 0), stop=(kc == 7)),
                    sem=("stn" if last else None))
        wait("vector", "stn", t)
        tv = op("vector", lambda E, d=dst_row, n=nh: E.tensor_copy(d[:], ps_a[0:1, 0:512 * n]), sem="sv")
        wait("tensor", "sv", tv)

    colsum(cwqk_r, wqk_s, D)
    colsum(cwv_r, wv_s, VD)
    if main_ph == "S1":
        dma("sync", OUTT0[0], MISC[:, 0:1024], sem="sout")
        wait("sync", "sout", cnt["sout"])
        return _finish(nc, ops, st)

    def emit_quant(win_ap, out_ap=None):
        if out_ap is None:
            out_ap = au_s[:]
        view = lambda t: t.rearrange("p (a b) -> p a b", b=32)
        op("vector", lambda E, w=win_ap: E.tensor_reduce(
            amax_s[:].unsqueeze(2), view(w), AX.X, ALU.max, apply_absolute_value=True))
        op("vector", lambda E: E.drain())
        op("vector", lambda E: E.tensor_scalar(ssc_s[:], amax_s[:], 1.0 / 7.5, 1e-30, ALU.mult, ALU.max))
        op("vector", lambda E: E.drain())
        op("vector", lambda E: E.tensor_scalar(s8c_s[:], ssc_s[:], 0.125, None, ALU.mult))
        op("vector", lambda E: E.reciprocal(rssc_s[:], ssc_s[:]))
        op("vector", lambda E: E.drain())
        rsb = rssc_s[:].unsqueeze(2).broadcast_to([128, 32, 32])
        s8b = s8c_s[:].unsqueeze(2).broadcast_to([128, 32, 32])
        op("vector", lambda E, w=win_ap: E.tensor_tensor(view(wu_s[:]), view(w), rsb, ALU.mult))
        op("vector", lambda E: E.scalar_tensor_tensor(au_s[:], wu_s[:], -1.0, wu_s[:], ALU.mult, ALU.max))
        op("vector", lambda E: E.tensor_scalar(mk1_s[:], au_s[:], 2.0, -4.0, ALU.is_ge, ALU.mult))
        op("vector", lambda E: E.tensor_scalar(mk2_s[:], au_s[:], 4.0, -2.0, ALU.is_ge, ALU.mult))
        op("vector", lambda E: E.scalar_tensor_tensor(mk1_s[:], mk1_s[:], 8.0, mk2_s[:], ALU.add, ALU.add))
        op("vector", lambda E: E.tensor_scalar(mk2_s[:], au_s[:], 4.0, 2.0, ALU.is_ge, ALU.mult))
        op("vector", lambda E: E.tensor_scalar(au_s[:], au_s[:], 2.0, None, ALU.is_ge))
        op("vector", lambda E: E.scalar_tensor_tensor(mk2_s[:], au_s[:], 1.0, mk2_s[:], ALU.add, ALU.add))
        op("vector", lambda E: E.tensor_tensor(qi_v, wu_s[:], mk1_s[:], ALU.mult))
        op("vector", lambda E: E.tensor_tensor(mk1_s[:], qi_v, mk2_s[:], ALU.mult))
        return op("vector", lambda E, o=out_ap: E.tensor_tensor(view(o), view(mk1_s[:]), s8b, ALU.mult), sem="sv")

    _tq_hist = []
    for jc in range(8):
        if jc >= 2:
            wait("sync", "sv", _tq_hist[jc - 2])
        tl = dma("sync", wfst[:, jc % 2, :], wffn.rearrange("(c p) n -> c p n", p=128)[jc])
        wait("vector", "sdma", tl)
        tq = emit_quant(wfst[:, jc % 2, :])
        _tq_hist.append(tq)
        wait("tensor", "sv", tq)
        t = None
        for kc in range(8):
            t = op("tensor", lambda E, kc=kc: E.transpose(
                ps_a_bf[:, kc * 128:(kc + 1) * 128],
                au_s[:, kc * 128:(kc + 1) * 128], identB[:]),
                sem=("stn" if kc == 7 else None))
        wait("vector", "stn", t)
        tv = op("vector", lambda E, jc=jc: E.tensor_copy(
            wffnq[:, :, jc * 128:(jc + 1) * 128],
            ps_a_bf[:, 0:1024].rearrange("p (kc n) -> p kc n", kc=8)), sem="sv")
        wait("tensor", "sv", tv)
    barrier()
    if "wq" in dbg:
        dma("gpsimd", dbg["wq"], wffnq[:].rearrange("p c n -> p (c n)"))
        wait("vector", "gdma", cnt["gdma"])

    def statcombine(negmu, r, eps, rinv=None):
        g = bnb[:].rearrange("p t (a k) -> p t a k", k=3)
        m4 = g[:, :, :, 1]
        v4 = g[:, :, :, 2]
        op("vector", lambda E: E.tensor_reduce(stat_t1[:].unsqueeze(2), m4, AX.X, ALU.add))
        op("vector", lambda E: E.tensor_reduce(stat_t2[:].unsqueeze(2), v4, AX.X, ALU.add))
        op("vector", lambda E: E.tensor_tensor(stat_t4[:], m4, m4, ALU.mult))
        op("vector", lambda E: E.tensor_reduce(stat_t3[:].unsqueeze(2), stat_t4[:], AX.X, ALU.add))
        op("vector", lambda E, n=negmu: E.tensor_scalar(n[:], stat_t1[:], -0.25, None, ALU.mult))
        op("vector", lambda E: E.tensor_scalar(stat_t5[:], stat_t2[:], 1.0 / 1024.0, None, ALU.mult))
        op("vector", lambda E: E.drain())
        op("vector", lambda E: E.scalar_tensor_tensor(stat_t6[:], stat_t3[:], 0.25, stat_t5[:], ALU.mult, ALU.add))
        op("vector", lambda E, n=negmu: E.tensor_tensor(stat_t7[:], n[:], n[:], ALU.mult))
        op("vector", lambda E: E.drain())
        tv = op("vector", lambda E: E.tensor_tensor(stat_t8[:], stat_t6[:], stat_t7[:], ALU.subtract), sem="sv")
        op("vector", lambda E: E.drain())
        wait("scalar", "sv", tv)
        dst_sd = rinv if rinv is not None else stat_t3
        eps_ap = eps_c[:] if eps == EPS else eps2_c[:]
        ta = op("scalar", lambda E, d=dst_sd, ep=eps_ap: E.activation(
            d[:], stat_t8[:], AF.Sqrt, bias=ep, scale=1.0), sem="sa")
        wait("vector", "sa", ta)
        rv = op("vector", lambda E, d=dst_sd, rr=r: E.reciprocal(rr[:], d[:]), sem="sv")
        op("vector", lambda E: E.drain())
        return rv

    def rowify(cols, row_tile):
        t = None
        for tt in range(NTT):
            t = op("tensor", lambda E, tt=tt, c=cols: E.transpose(
                ps_a[0:1, tt * 128:(tt + 1) * 128], c[:, tt:tt + 1], identF[:]),
                sem=("stn" if tt == NTT - 1 else None))
        wait("vector", "stn", t)
        tv = op("vector", lambda E, rt=row_tile: E.tensor_copy(rt, ps_a[0:1, 0:S]), sem="sv")
        wait("tensor", "sv", tv)

    TOKT = x_tok.rearrange("(q p) n -> q p n", p=128)
    OUTT = out.rearrange("(q p) n -> q p n", p=128)

    def dump(name, src_ap):
        if name in dbg:
            dma("gpsimd", dbg[name], src_ap)

    for sq in range(0 if main_ph == "S" else nseq):
        for tt in range(NTT):
            tl = dma("sync", xa_v[:, tt % 2, :], TOKT[sq * NTT + tt])
            wait("vector", "sdma", tl)
            op("vector", lambda E, tt=tt: E.bn_stats(bnb[:, tt, 0:6], xa_v[:, tt % 2, 0:512]))
            tv = op("vector", lambda E, tt=tt: E.bn_stats(bnb[:, tt, 6:12], xa_v[:, tt % 2, 512:1024]), sem="sv")
            wait("sync", "sv", tv)
        if main_ph == "A1":
            continue
        ret = statcombine(negmu0, r0, EPS, rinv=rinv0)
        wait("tensor", "sv", ret)
        if "bnb" in dbg:
            wait("gpsimd", "sv", ret)
            dma("gpsimd", dbg["bnb"], bnb[:].rearrange("p t n -> p (t n)"))
        if "stats" in dbg:
            wait("gpsimd", "sv", ret)
            dma("gpsimd", dbg["stats"].rearrange("p (a n) -> p a n", n=NTT)[:, 0, :], negmu0[:])
            dma("gpsimd", dbg["stats"].rearrange("p (a n) -> p a n", n=NTT)[:, 1, :], r0[:])
            dma("gpsimd", dbg["stats"].rearrange("p (a n) -> p a n", n=NTT)[:, 2, :], rinv0[:])
            dma("gpsimd", dbg["stats"].rearrange("p (a n) -> p a n", n=NTT)[:, 3, :], stat_t1[:])
            dma("gpsimd", dbg["stats"].rearrange("p (a n) -> p a n", n=NTT)[:, 4, :], stat_t2[:])
            dma("gpsimd", dbg["stats"].rearrange("p (a n) -> p a n", n=NTT)[:, 5, :], stat_t3[:])
        if main_ph == "A2":
            continue
        rowify(negmu0, murow[0:1, :])
        rowify(rinv0, rinvrow[0:1, :])
        rowify(r0, r0row)
        if main_ph == "A3":
            continue
        t = None
        for ic in range(NIC):
            t = op("tensor", lambda E, ic=ic: E.matmul(
                ps_a[:, ic * 512:(ic + 1) * 512], ones_rb[:],
                r0row[:, ic * 512:(ic + 1) * 512], start=True, stop=True),
                sem=("stn" if ic == NIC - 1 else None))
        wait("vector", "stn", t)
        tv = op("vector", lambda E: E.tensor_copy(rb_v, ps_a[:, 0:S]), sem="sv")
        wait("tensor", "sv", tv)
        if "rows" in dbg:
            wait("gpsimd", "sv", cnt["sv"])
            rv = dbg["rows"].rearrange("p (a n) -> p a n", n=S)
            dma("gpsimd", rv[:, 0, :], murow[0:1, :])
            dma("gpsimd", rv[:, 1, :], rinvrow[0:1, :])
            dma("gpsimd", rv[:, 2, :], r0row)
        if "rb" in dbg:
            dma("gpsimd", dbg["rb"], rb_v)
        barrier()
        if "B" not in main_ph:
            continue

        XF = x_feat[sq].rearrange("(kc p) n -> p kc n", p=128)
        for ic in range(NIC):
            if ic >= 1:
                wait("gpsimd", "stn", cnt["stn"])
            tl = dma("gpsimd", xfb[:, ic % 2, :, :], XF[:, :, ic * 512:(ic + 1) * 512])
            wait("tensor", "gdma", tl)
            for ct in range(8):
                pb = ps_b if (ct % 2 == 0) else ps_c
                for kc in range(8):
                    op("tensor", lambda E, kc=kc, ct=ct, ic=ic, pb=pb: E.matmul(
                        pb[:], wqk_s[:, kc, ct * 128:(ct + 1) * 128],
                        xfb[:, ic % 2, kc, :], start=(kc == 0), stop=False))
                op("tensor", lambda E, ct=ct, ic=ic, pb=pb: E.matmul(
                    pb[:], cwqk_r[:, ct * 128:(ct + 1) * 128],
                    murow[:, ic * 512:(ic + 1) * 512], start=False, stop=False))
                tmm = op("tensor", lambda E, ct=ct, ic=ic, pb=pb: E.matmul(
                    pb[:], bqk_r[:, ct * 128:(ct + 1) * 128],
                    rinvrow[:, ic * 512:(ic + 1) * 512], start=False, stop=True), sem="stn")
                wait("vector", "stn", tmm)
                dbuf = q_v if ct < 4 else k_v
                tv = op("vector", lambda E, d=dbuf, dc=ct % 4, ic=ic, pb=pb: E.tensor_tensor(
                    d[:, dc, ic * 512:(ic + 1) * 512], pb[:],
                    rb_v[:, ic * 512:(ic + 1) * 512], ALU.mult), sem="sv")
                wait("tensor", "sv", tv)
            for tl_ in range(4):
                tt = ic * 4 + tl_
                ph = tl_ % 2
                for kc in range(8):
                    op("tensor", lambda E, kc=kc, tl_=tl_, ph=ph, ic=ic: E.matmul(
                        ps_d[:, ph * 512:(ph + 1) * 512],
                        xfb[:, ic % 2, kc, tl_ * 128:(tl_ + 1) * 128],
                        wv_s[:, kc, :], start=(kc == 0), stop=False))
                op("tensor", lambda E, tt=tt, ph=ph: E.matmul(
                    ps_d[:, ph * 512:(ph + 1) * 512], murow[:, tt * 128:(tt + 1) * 128],
                    cwv_r[:], start=False, stop=False))
                tmm = op("tensor", lambda E, tt=tt, ph=ph: E.matmul(
                    ps_d[:, ph * 512:(ph + 1) * 512], rinvrow[:, tt * 128:(tt + 1) * 128],
                    bv_r[:], start=False, stop=True), sem="stn")
                wait("vector", "stn", tmm)
                tv = op("vector", lambda E, tt=tt, ph=ph: E.tensor_scalar(
                    v_v[:, tt, :], ps_d[:, ph * 512:(ph + 1) * 512],
                    r0[:, tt:tt + 1], None, ALU.mult), sem="sv")
                wait("tensor", "sv", tv)
        barrier()
        dump("q", q_v.rearrange("p c n -> p (c n)"))
        dump("k", k_v.rearrange("p c n -> p (c n)"))
        dump("v", v_v.rearrange("p c n -> p (c n)"))
        if "C" not in main_ph:
            continue

        for pr in range(NPAIR):
            tmm = None
            for hl in range(2):
                tmm = op("tensor", lambda E, pr=pr, hl=hl: E.matmul(
                    ps_d[:, hl * 512:(hl + 1) * 512],
                    q_v[hl * 64:(hl + 1) * 64, pr, 0:S:16],
                    k_v[hl * 64:(hl + 1) * 64, pr, 0:S:4],
                    start=True, stop=True, tile_position=(hl * 64, 0)),
                    sem=("stn" if hl == 1 else None))
            wait("vector", "stn", tmm)
            op("vector", lambda E, pr=pr: E.reduce_max(
                cm_s[:, 2 * pr:2 * pr + 1], ps_d[:, 0:512], AX.X))
            tv = op("vector", lambda E, pr=pr: E.reduce_max(
                cm_s[:, 2 * pr + 1:2 * pr + 2], ps_d[:, 512:1024], AX.X), sem="sv")
            wait("tensor", "sv", tv)
        evv = cm_s[:].rearrange("p (a b) -> p a b", b=2)
        op("vector", lambda E: E.drain())
        tv = op("vector", lambda E, ev=evv: E.tensor_tensor(
            cmp_s[:].unsqueeze(2), ev[:, :, 0:1], ev[:, :, 1:2], ALU.max), sem="sv")
        op("vector", lambda E: E.drain())
        wait("tensor", "sv", tv)
        t = op("tensor", lambda E: E.transpose(ps_a[0:4, 0:128], cmp_s[:], identF[:]), sem="stn")
        wait("vector", "stn", t)
        op("vector", lambda E: E.reduce_max(gm_s[:], ps_a[0:4, 0:128], AX.X))
        op("vector", lambda E: E.drain())
        tv = op("vector", lambda E: E.tensor_scalar(negmp[:], gm_s[:], -0.125, -2.0, ALU.mult, ALU.add), sem="sv")
        op("vector", lambda E: E.drain())
        wait("tensor", "sv", tv)
        t = op("tensor", lambda E: E.transpose(ps_a[0:1, 256:260], negmp[:], identF[0:4, 0:4]), sem="stn")
        wait("vector", "stn", t)
        tv = op("vector", lambda E: E.tensor_copy(negmrow[:], ps_a[0:1, 256:260]), sem="sv")
        op("vector", lambda E: E.drain())
        wait("tensor", "sv", tv)
        t = op("tensor", lambda E: E.matmul(ps_a[:, 512:516], ones_rf[:], negmrow[:],
                                            start=True, stop=True), sem="stn")
        wait("vector", "stn", t)
        tv = op("vector", lambda E: E.tensor_copy(negmb[:], ps_a[:, 512:516]), sem="sv")
        op("vector", lambda E: E.drain())
        wait("tensor", "sv", tv)
        wait("scalar", "sv", tv)
        if sub_ph == "C1":
            continue

        _nps = 1 if sub_ph == "C2" else NPAIR
        _nis = 1 if sub_ph == "C2" else NIC
        _lcap = 99
        if sub_ph.startswith("L") and sub_ph[1:].isdigit():
            _lcap = int(sub_ph[1:])
        if sub_ph.startswith("P"):
            _lcap = 8 if sub_ph.startswith(("PD", "PC", "PX", "PO", "PZ")) else 1
        _ldone = 0
        _iters = [(p_, i_) for p_ in range(_nps) for i_ in range(_nis)]
        if sub_ph == "LR":
            _iters = [(0, 0), (0, 0)]
            _lcap = 2
        if sub_ph == "LP2":
            _iters = [(2, 0), (2, 1), (2, 2), (2, 3)]
            _lcap = 4
        if sub_ph == "LP2a":
            _iters = [(2, 0)]
            _lcap = 1
        _tzd_hist = []
        for pr, ic in _iters:
            if True:
                if _ldone >= _lcap:
                    continue
                if _ldone > 0 and _ldone % 4 == 0:
                    barrier()
                _ldone += 1
                t_av = None
                for g in range(8):
                    t_sh = [None, None]
                    for hl in range(2):
                        for jl in range(2):
                            jt = g * 2 + jl
                            sl = hl * 2 + jl
                            t_sh[hl] = op("tensor", lambda E, pr=pr, ic=ic, jt=jt, hl=hl, sl=sl: E.matmul(
                                ps_a[:, sl * 512:(sl + 1) * 512],
                                k_v[hl * 64:(hl + 1) * 64, pr, jt * 128:(jt + 1) * 128],
                                q_v[hl * 64:(hl + 1) * 64, pr, ic * 512:(ic + 1) * 512],
                                start=True, stop=True, tile_position=(hl * 64, 0)),
                                sem=("stn" if jl == 1 else None))
                    first = (g == 0)
                    lastg = (g == 7)
                    t_eh = [None, None]
                    for hl in range(2):
                        wait("scalar", "stn", t_sh[hl])
                        t_eh[hl] = op("scalar", lambda E, pr=pr, g=g, hl=hl: E.activation(
                            eg_v[:, g % 2, hl * 1024:(hl + 1) * 1024],
                            ps_a[:, hl * 1024:(hl + 1) * 1024], AF.Exp,
                            bias=negmb[:, pr:pr + 1], scale=0.125), sem="sa")
                    for hl in range(2):
                        wait("tensor", "sa", t_eh[hl])
                        for jl in range(2):
                            jt = g * 2 + jl
                            sl = hl * 2 + jl
                            h = pr * 2 + hl
                            st_ = (first and jl == 0)
                            sp_ = (lastg and jl == 1)
                            last_op = (hl == 1 and jl == 1)
                            ob_ap = ps_b[0:64, :] if hl == 0 else ps_b[64:128, :]
                            zb_ap = ps_c[0:1, :] if hl == 0 else ps_c[64:65, :]
                            op("tensor", lambda E, jt=jt, sl=sl, h=h, hl=hl, g=g,
                               st_=st_, sp_=sp_, ob=ob_ap: E.matmul(
                                ob, v_v[:, jt, h * 64:(h + 1) * 64],
                                eg_v[:, g % 2, sl * 512:(sl + 1) * 512],
                                start=st_, stop=sp_, tile_position=(0, hl * 64)))
                            t_av = op("tensor", lambda E, jt=jt, sl=sl, hl=hl, g=g,
                               st_=st_, sp_=sp_, zb=zb_ap: E.matmul(
                                zb, ones_col[:],
                                eg_v[:, g % 2, sl * 512:(sl + 1) * 512],
                                start=st_, stop=sp_, tile_position=(0, hl * 64)),
                                sem=("stn" if last_op else None))
                wait("scalar", "stn", t_av)
                idx1 = (pr * 2 + 0) * 4 + ic
                idx2 = (pr * 2 + 1) * 4 + ic
                _zsl = ((pr * NIC + ic) % 2) * 1024
                wait("vector", "stn", cnt["stn"])
                tv = op("vector", lambda E, pr=pr, ic=ic: E.tensor_copy(
                    obuf[:, pr, ic * 512:(ic + 1) * 512], ps_b[:]), sem="sv")
                if len(_tzd_hist) >= 2:
                    wait("scalar", "gdma", _tzd_hist[-2])
                op("scalar", lambda E, z=_zsl: E.copy(ZR[0:1, z:z + 512], ps_c[0:1, :]))
                ta = op("scalar", lambda E, z=_zsl: E.copy(
                    ZR[0:1, z + 512:z + 1024], ps_c[64:65, :]), sem="sa")
                wait("tensor", "sv", tv)
                wait("tensor", "sa", ta)
                if sub_ph not in ("L2z", "L2n"):
                    wait("gpsimd", "sa", ta)
                    dma("gpsimd", zscr[sq:sq + 1, idx1 * 512:(idx1 + 1) * 512],
                        ZR[0:1, _zsl:_zsl + 512])
                    tzd = dma("gpsimd", zscr[sq:sq + 1, idx2 * 512:(idx2 + 1) * 512],
                              ZR[0:1, _zsl + 512:_zsl + 1024])
                    _tzd_hist.append(tzd)
        if sub_ph in ("C2", "C3") or sub_ph.startswith("L") or sub_ph.startswith("P"):
            continue
        if "obun" in dbg:
            wait("gpsimd", "sv", cnt["sv"])
            dma("gpsimd", dbg["obun"], obuf.rearrange("p c n -> p (c n)"))
            wait("sync", "gdma", cnt["gdma"])
        wait("sync", "gdma", cnt["gdma"])
        t2_ = dma("sync", zgt, zscr[sq:sq + 1, :].rearrange("p (a b) -> (p a) b", a=32))
        wait("vector", "sdma", t2_)
        tv = op("vector", lambda E: E.reciprocal(rzg, zgt), sem="sv")
        wait("sync", "sv", tv)
        t3_ = dma("sync", rzscr[sq:sq + 1, :].rearrange("p (a b) -> (p a) b", a=32), rzg)
        wait("sync", "sdma", t3_)
        if "zq" in dbg:
            dma("sync", dbg["zq"], zgt)
            dma("sync", dbg["rzq"], rzg)
        if "negm" in dbg:
            dma("sync", dbg["negm"], negmb[:])
        if sub_ph.startswith("PZ"):
            for _pi in range(int(sub_ph[2:])):
                op("tensor", lambda E: E.matmul(
                    ps_c[64:65, :], ones_col[:], eg_v[:, 0, 0:512],
                    start=True, stop=True, tile_position=(0, 64)), sem="stn")
            wait("vector", "stn", cnt["stn"])
        elif sub_ph.startswith("PO"):
            for _pi in range(int(sub_ph[2:])):
                qo = (_pi * 64) % 1536
                op("tensor", lambda E, qo=qo: E.matmul(
                    ps_a[:, 0:512], k_v[0:64, 3, qo:qo + 128],
                    q_v[0:64, 3, qo + 64:qo + 576], start=True, stop=True,
                    tile_position=(0, 0)))
            op("tensor", lambda E: E.matmul(ps_b[0:1, 0:1], ones_rf[0:1, 0:1],
                                            ones_rf[0:1, 0:1], start=True, stop=True), sem="stn")
            wait("vector", "stn", cnt["stn"])
        elif sub_ph.startswith("PX"):
            for _pi in range(int(sub_ph[2:])):
                op("scalar", lambda E: E.activation(
                    eg_v[:, 0, :], ps_a[:, 0:2048], AF.Exp,
                    bias=negmb[:, 0:1], scale=0.125))
            op("scalar", lambda E: E.copy(junk_s[:], junk_v[:]), sem="sa")
        elif sub_ph.startswith("PC"):
            for _pi in range(int(sub_ph[2:])):
                op("scalar", lambda E: E.copy(ZR[0:1, 0:512], ps_c[0:1, :]))
                op("scalar", lambda E: E.copy(ZR[0:1, 512:1024], ps_c[64:65, :]))
            op("scalar", lambda E: E.copy(junk_s[:], junk_v[:]), sem="sa")
        elif sub_ph.startswith("PD"):
            for _pi in range(int(sub_ph[2:])):
                dma("gpsimd", junk_v[:], zscr[0:1, 0:8])
            wait("gpsimd", "gdma", cnt["gdma"])
        elif sub_ph.startswith("PS"):
            for _pi in range(int(sub_ph[2:])):
                op("scalar", lambda E: E.activation(junk_s[:], junk_v[:], AF.Square), sem="sa")
                op("tensor", lambda E: E.matmul(ps_b[0:1, 0:1], ones_rf[0:1, 0:1],
                                                ones_rf[0:1, 0:1], start=True, stop=True), sem="stn")
        elif sub_ph.startswith("PW"):
            for _pi in range(int(sub_ph[2:])):
                wait("tensor", "sv", cnt["sv"])
                wait("scalar", "stn", cnt["stn"])
                wait("vector", "sa", cnt["sa"])
        elif sub_ph.startswith("PT"):
            for _pi in range(int(sub_ph[2:])):
                op("tensor", lambda E: E.matmul(ps_b[0:1, 0:1], ones_rf[0:1, 0:1],
                                                ones_rf[0:1, 0:1], start=True, stop=True))
            op("tensor", lambda E: E.matmul(ps_b[0:1, 0:1], ones_rf[0:1, 0:1],
                                            ones_rf[0:1, 0:1], start=True, stop=True), sem="stn")
        elif sub_ph.startswith("PA"):
            for _pi in range(int(sub_ph[2:])):
                op("scalar", lambda E: E.activation(junk_s[:], junk_v[:], AF.Square))
            op("scalar", lambda E: E.activation(junk_s[:], junk_v[:], AF.Square), sem="sa")
        elif sub_ph.startswith("P"):
            for _pi in range(int(sub_ph[1:])):
                op("vector", lambda E: E.tensor_copy(junk_v[:], junk_g[:]))
            op("vector", lambda E: E.tensor_copy(junk_v[:], junk_g[:]), sem="sv")
        wait("gpsimd", "sdma", cnt["sdma"])
        _cap = 99
        if sub_ph.startswith("Cn"):
            _cap = int(sub_ph[2:])
        _done = 0
        _tvn_hist = []
        for pr in range(0 if sub_ph == "C4" else NPAIR):
            for ic in range(NIC):
                if _done >= _cap:
                    continue
                _done += 1
                idx1 = (pr * 2 + 0) * 4 + ic
                idx2 = (pr * 2 + 1) * 4 + ic
                ph = (pr * NIC + ic) % 2
                rzv = rzscr[sq:sq + 1, :].rearrange("p (a b) -> (p a) b", a=32)
                if len(_tvn_hist) >= 2:
                    wait("gpsimd", "sv", _tvn_hist[-2])
                dma("gpsimd", msb_v[:, ph, :][0:64, :],
                    rzv[idx1:idx1 + 1, :].broadcast_to([64, 512]))
                tg = dma("gpsimd", msb_v[:, ph, :][64:128, :],
                         rzv[idx2:idx2 + 1, :].broadcast_to([64, 512]))
                wait("vector", "gdma", tg)
                tv = op("vector", lambda E, pr=pr, ic=ic, ph=ph: E.tensor_tensor(
                    obuf[:, pr, ic * 512:(ic + 1) * 512],
                    obuf[:, pr, ic * 512:(ic + 1) * 512],
                    msb_v[:, ph, :], ALU.mult), sem="sv")
                _tvn_hist.append(tv)
        barrier()
        dump("obuf", obuf.rearrange("p c n -> p (c n)"))
        if "D" not in main_ph:
            continue

        _tz_hist = []
        for tt in range(NTT):
            pd1 = ps_d if tt % 2 == 0 else ps_a
            if tt >= 2:
                wait("tensor", "sv", _tz_hist[tt - 2])
            for vc in range(4):
                for hh in range(2):
                    op("tensor", lambda E, vc=vc, hh=hh, tt=tt, pd=pd1: E.matmul(
                        pd[:, hh * 512:(hh + 1) * 512],
                        obuf[:, vc, tt * 128:(tt + 1) * 128],
                        wout_s[:, vc, hh * 512:(hh + 1) * 512],
                        start=(vc == 0), stop=False))
            t_z = None
            for hh in range(2):
                t_z = op("tensor", lambda E, hh=hh, pd=pd1: E.matmul(
                    pd[:, hh * 512:(hh + 1) * 512], ones_rb[:],
                    bout_r[:, hh * 512:(hh + 1) * 512], start=False, stop=True),
                    sem=("stn" if hh == 1 else None))
            wait("vector", "stn", t_z)
            op("vector", lambda E, tt=tt, pd=pd1: E.bn_stats(bnb[:, tt, 0:6], pd[:, 0:512]))
            op("vector", lambda E, tt=tt, pd=pd1: E.bn_stats(bnb[:, tt, 6:12], pd[:, 512:1024]))
            tv = op("vector", lambda E, tt=tt, pd=pd1: E.tensor_copy(
                zbuf[:, tt, :], pd[:, 0:1024]), sem="sv")
            _tz_hist.append(tv)
        statcombine(negmuz, rz_s, EPS)
        dump("zbuf", zbuf.rearrange("p t n -> p (t n)"))
        for tt in range(NTT):
            tl = dma("gpsimd", xtok_v[:, tt % 2, :], TOKT[sq * NTT + tt])
            op("vector", lambda E, tt=tt: E.tensor_scalar(
                rzgam[:], gamb[:], rz_s[:, tt:tt + 1], None, ALU.mult))
            op("vector", lambda E, tt=tt: E.scalar_tensor_tensor(
                t2_s[:], zbuf[:, tt, :], negmuz[:, tt:tt + 1], rzgam[:], ALU.add, ALU.mult))
            wait("vector", "gdma", tl)
            op("vector", lambda E, tt=tt: E.tensor_scalar(
                uTB_s[:], xtok_v[:, tt % 2, :], negmu0[:, tt:tt + 1], r0[:, tt:tt + 1],
                ALU.add, ALU.mult))
            op("vector", lambda E: E.tensor_tensor(t2_s[:], t2_s[:], uTB_s[:], ALU.add))
            op("vector", lambda E, tt=tt: E.tensor_tensor(sb_v[:, tt, :], t2_s[:], betb[:], ALU.add))
            op("vector", lambda E, tt=tt: E.bn_stats(bnb[:, tt, 0:6], sb_v[:, tt, 0:512]))
            tv = op("vector", lambda E, tt=tt: E.bn_stats(bnb[:, tt, 6:12], sb_v[:, tt, 512:1024]), sem="sv")
            wait("gpsimd", "sv", tv)
        statcombine(negmus, rs_s, EPS_S)
        dump("sbuf", sb_v.rearrange("p t n -> p (t n)"))
        for tt in range(NTT):
            tl = dma("gpsimd", xtok_v[:, tt % 2, :], TOKT[sq * NTT + tt])
            op("vector", lambda E, tt=tt: E.tensor_scalar(
                t2_s[:], sb_v[:, tt, :], negmus[:, tt:tt + 1], rs_s[:, tt:tt + 1],
                ALU.add, ALU.mult))
            wait("vector", "gdma", tl)
            op("vector", lambda E, tt=tt: E.tensor_tensor(
                x1buf[:, tt, :], xtok_v[:, tt % 2, :], t2_s[:], ALU.add))
            op("vector", lambda E, tt=tt: E.bn_stats(bnb[:, tt, 0:6], x1buf[:, tt, 0:512]))
            tv = op("vector", lambda E, tt=tt: E.bn_stats(bnb[:, tt, 6:12], x1buf[:, tt, 512:1024]), sem="sv")
            wait("gpsimd", "sv", tv)
        statcombine(negmu1, r1_s, EPS)
        dump("x1", x1buf.rearrange("p t n -> p (t n)"))
        tq_p4 = [None] * NTT
        tbn_p4 = [None] * NTT
        tfc_p4 = [None] * NTT
        t_xc_prev = None
        for i in range(NTT + 1):
            if i > 0 and i % 4 == 0:
                barrier()
            if i < NTT:
                tw = op("vector", lambda E, tt=i: E.tensor_scalar(
                    wu_s[:], x1buf[:, tt, :], negmu1[:, tt:tt + 1], r1_s[:, tt:tt + 1],
                    ALU.add, ALU.mult), sem=("sv" if (i == 0 and "w0" in dbg) else None))
                if i == 0 and "w0" in dbg:
                    wait("gpsimd", "sv", tw)
                    dma("gpsimd", dbg["w0"], wu_s[:])
                    wait("vector", "gdma", cnt["gdma"])
                au_i = au_s if i % 2 == 0 else au2_s
                tq_p4[i] = emit_quant(wu_s[:], au_i[:])
                if i == 0 and "xd" in dbg:
                    wait("gpsimd", "sv", tq_p4[i])
                    dma("gpsimd", dbg["xd"], au_i[:])
                    wait("vector", "gdma", cnt["gdma"])
            if i >= 1:
                j = i - 1
                au_j = au_s if j % 2 == 0 else au2_s
                xdtf_j = MISC[:, 1024:1536].bitcast(BF16) if j % 2 == 0 else MISC[:, 1536:2048].bitcast(BF16)
                xdt_j = xdt_v if j % 2 == 0 else xdt2_v
                psd_j = ps_d if j % 2 == 0 else ps_a
                wait("tensor", "sv", tq_p4[j])
                if t_xc_prev is not None:
                    wait("tensor", "sv", t_xc_prev)
                t = None
                for kc in range(8):
                    t = op("tensor", lambda E, kc=kc, a=au_j: E.transpose(
                        ps_a_bf[:, 2048 + kc * 128:2048 + (kc + 1) * 128],
                        a[:, kc * 128:(kc + 1) * 128], identB[:]),
                        sem=("stn" if kc == 7 else None))
                wait("vector", "stn", t)
                t_xc = op("vector", lambda E, d=xdtf_j: E.tensor_copy(d, ps_a_bf[:, 2048:3072]), sem="sv")
                t_xc_prev = t_xc
                wait("tensor", "sv", t_xc)
                if j >= 2:
                    wait("tensor", "sv", tbn_p4[j - 2])
                    wait("tensor", "sv", tfc_p4[j - 2])
                for kc in range(8):
                    for hh in range(2):
                        op("tensor", lambda E, kc=kc, hh=hh, x=xdt_j, pd=psd_j: E.matmul(
                            pd[:, hh * 512:(hh + 1) * 512],
                            x[:, kc, :],
                            wffnq[:, kc, hh * 512:(hh + 1) * 512],
                            start=(kc == 0), stop=False))
                t_f = None
                for hh in range(2):
                    t_f = op("tensor", lambda E, hh=hh, pd=psd_j: E.matmul(
                        pd[:, hh * 512:(hh + 1) * 512], ones_rb[:],
                        fbias_r[:, hh * 512:(hh + 1) * 512], start=False, stop=True),
                        sem=("stn" if hh == 1 else None))
                wait("vector", "stn", t_f)
                op("vector", lambda E, tt=j, pd=psd_j: E.bn_stats(bnb[:, tt, 0:6], pd[:, 0:512]))
                tbn_p4[j] = op("vector", lambda E, tt=j, pd=psd_j: E.bn_stats(
                    bnb[:, tt, 6:12], pd[:, 512:1024]), sem="sv")
                tfc_p4[j] = op("vector", lambda E, tt=j, pd=psd_j: E.tensor_copy(
                    fbuf[:, tt, :], pd[:, 0:1024]), sem="sv")
        statcombine(negmuf, rf_s, EPS)
        dump("fbuf", fbuf.rearrange("p t n -> p (t n)"))
        sout_hist = []
        for tt in range(NTT):
            if tt >= 2:
                wait("vector", "sout", sout_hist[tt - 2])
            op("vector", lambda E, tt=tt: E.tensor_scalar(
                t2_s[:], fbuf[:, tt, :], negmuf[:, tt:tt + 1], rf_s[:, tt:tt + 1],
                ALU.add, ALU.mult))
            tv = op("vector", lambda E, tt=tt: E.tensor_tensor(
                wfst[:, tt % 2, :], x1buf[:, tt, :], t2_s[:], ALU.add), sem="sv")
            wait("sync", "sv", tv)
            sout_hist.append(dma("sync", OUTT[sq * NTT + tt], wfst[:, tt % 2, :], sem="sout"))
        barrier()

    if cnt["sout"] == 0:
        dma("sync", OUTT[0], MISC[:, 0:1024], sem="sout")
    wait("sync", "sout", cnt["sout"])
    wait("gpsimd", "gdma", cnt["gdma"])

    with nc.Block() as blk:
        @blk.sync
        def _(E):
            for f in ops["sync"]:
                f(E)

        @blk.gpsimd
        def _(E):
            for f in ops["gpsimd"]:
                f(E)

        @blk.vector
        def _(E):
            for f in ops["vector"]:
                f(E)

        @blk.scalar
        def _(E):
            for f in ops["scalar"]:
                f(E)

        @blk.tensor
        def _(E):
            for f in ops["tensor"]:
                f(E)

    return nc, st


def _qk_perm():
    q = [h * 128 + 2 * d for h in range(H) for d in range(DH)]
    k = [h * 128 + 2 * d + 1 for h in range(H) for d in range(DH)]
    return np.array(q + k, dtype=np.int64)


def kernel(L0_attn_x, L0_attn_W_qk, L0_attn_b_qk, L0_attn_W_v, L0_attn_b_v,
           L0_attn_W_out, L0_attn_b_out, L0_attn_ln_gamma, L0_attn_ln_beta,
           L0_ffn_weight, L0_ffn_bias):
    x = np.ascontiguousarray(np.asarray(L0_attn_x, dtype=np.float32))
    perm = _qk_perm()
    wqk_p = np.ascontiguousarray(np.asarray(L0_attn_W_qk, np.float32)[:, perm])
    bqk_p = np.ascontiguousarray(np.asarray(L0_attn_b_qk, np.float32)[perm][None, :])
    wv_ = np.ascontiguousarray(np.asarray(L0_attn_W_v, np.float32))
    bv_ = np.ascontiguousarray(np.asarray(L0_attn_b_v, np.float32)[None, :])
    wout_ = np.ascontiguousarray(np.asarray(L0_attn_W_out, np.float32))
    bout_ = np.ascontiguousarray(np.asarray(L0_attn_b_out, np.float32)[None, :])
    gam_ = np.ascontiguousarray(np.asarray(L0_attn_ln_gamma, np.float32)[None, :])
    bet_ = np.ascontiguousarray(np.asarray(L0_attn_ln_beta, np.float32)[None, :])
    wffn_ = np.ascontiguousarray(np.asarray(L0_ffn_weight, np.float32))
    fbias_ = np.ascontiguousarray(np.asarray(L0_ffn_bias, np.float32)[None, :])

    nc, stack = build_program()
    try:
        in_maps = []
        for c in range(8):
            xs = x[c * NSEQ:(c + 1) * NSEQ]
            in_maps.append(dict(
                x_tok=np.ascontiguousarray(xs.reshape(NSEQ * S, D)),
                x_feat=np.ascontiguousarray(xs.transpose(0, 2, 1)),
                wqk=wqk_p, wv=wv_, wout=wout_, wffn=wffn_, bqk=bqk_p, bv=bv_,
                bout=bout_, gam=gam_, bet=bet_, fbias=fbias_))
        res = run_bass_kernel_spmd(nc, in_maps, core_ids=list(range(8)))
    finally:
        stack.close()
    outs = [res.results[c]["out"].reshape(NSEQ, S, D) for c in range(8)]
    return np.concatenate(outs, axis=0)


# version 11
# speedup vs baseline: 1.4111x; 1.1064x over previous
import numpy as np
import contextlib
import concourse.bass as bass
import concourse.mybir as mybir
from concourse.bass_utils import run_bass_kernel_spmd

F32 = mybir.dt.float32
BF16 = mybir.dt.bfloat16
I32 = mybir.dt.int32
ALU = mybir.AluOpType
AF = mybir.ActivationFunctionType
AX = mybir.AxisListType

NSEQ, S, D, H, DH, VD = 4, 2048, 1024, 8, 64, 512
NTT, NIC, NJT, NPAIR = 16, 4, 16, 4
EPS = 1e-5
EPS_S = 2e-5


def _finish(nc, ops, st):
    with nc.Block() as blk:
        @blk.sync
        def _(E):
            for f in ops["sync"]:
                f(E)

        @blk.gpsimd
        def _(E):
            for f in ops["gpsimd"]:
                f(E)

        @blk.vector
        def _(E):
            for f in ops["vector"]:
                f(E)

        @blk.scalar
        def _(E):
            for f in ops["scalar"]:
                f(E)

        @blk.tensor
        def _(E):
            for f in ops["tensor"]:
                f(E)
    return nc, st


def build_program(nseq=NSEQ, phases="ABCD", debug_dump=()):
    main_ph, _, sub_ph = phases.partition(".")
    nc = bass.Bass("TRN2", target_bir_lowering=False)

    x_tok = nc.dram_tensor("x_tok", [NSEQ * S, D], F32, kind="ExternalInput").ap()
    x_feat = nc.dram_tensor("x_feat", [NSEQ, D, S], F32, kind="ExternalInput").ap()
    wqk = nc.dram_tensor("wqk", [D, D], F32, kind="ExternalInput").ap()
    wv = nc.dram_tensor("wv", [D, VD], F32, kind="ExternalInput").ap()
    wout = nc.dram_tensor("wout", [VD, D], F32, kind="ExternalInput").ap()
    wffn = nc.dram_tensor("wffn", [D, D], F32, kind="ExternalInput").ap()
    bqk = nc.dram_tensor("bqk", [1, D], F32, kind="ExternalInput").ap()
    bv = nc.dram_tensor("bv", [1, VD], F32, kind="ExternalInput").ap()
    bout = nc.dram_tensor("bout", [1, D], F32, kind="ExternalInput").ap()
    gam = nc.dram_tensor("gam", [1, D], F32, kind="ExternalInput").ap()
    bet = nc.dram_tensor("bet", [1, D], F32, kind="ExternalInput").ap()
    fbias = nc.dram_tensor("fbias", [1, D], F32, kind="ExternalInput").ap()
    out = nc.dram_tensor("out", [NSEQ * S, D], F32, kind="ExternalOutput").ap()
    zscr = nc.dram_tensor("zscr", [NSEQ, 32 * 512], F32, kind="Internal").ap()
    rzscr = nc.dram_tensor("rzscr", [NSEQ, 32 * 512], F32, kind="Internal").ap()
    dbg = {}
    for name in debug_dump:
        shapes = {"q": [128, 4 * S], "k": [128, 4 * S], "v": [128, NJT * 520],
                  "obuf": [128, 4 * S], "zrow": [1, 32 * 512], "stats": [128, 6 * NTT],
                  "rows": [1, 4 * S], "rb": [128, S], "bnb": [128, NTT * 12], "zq": [32, 512], "rzq": [32, 512], "negm": [128, 4], "obun": [128, 4 * S],
                  "zbuf": [128, NTT * D], "sbuf": [128, NTT * D], "x1": [128, NTT * D],
                  "fbuf": [128, NTT * D], "xd": [128, D], "xdt": [128, D],
                  "wq": [128, 8 * D], "w0": [128, D]}
        dbg[name] = nc.dram_tensor("dbg_" + name, shapes[name], F32, kind="ExternalOutput").ap()

    OUTT0 = out.rearrange("(q p) n -> q p n", p=128)

    st = contextlib.ExitStack()
    e = st.enter_context

    wqk_s = e(nc.sbuf_tensor("wqk_s", [128, 8, D], BF16))
    wv_s = e(nc.sbuf_tensor("wv_s", [128, 8, VD], BF16))
    wout_s = e(nc.sbuf_tensor("wout_s", [128, 4, D], BF16))
    wffnq = e(nc.sbuf_tensor("wffnq", [128, 8, D], BF16))

    U1 = e(nc.sbuf_tensor("U1", [128, 16384], BF16))
    U2 = e(nc.sbuf_tensor("U2", [128, 16512], BF16))
    U3 = e(nc.sbuf_tensor("U3", [128, 16384], BF16))
    q_v = U1[:, 0:8192].rearrange("p (c n) -> p c n", c=4)
    k_v = U1[:, 8192:16384].rearrange("p (c n) -> p c n", c=4)
    zbuf = U1[:].rearrange("p (t n) -> p t n", t=NTT)
    fbuf = zbuf
    v_v = U2[:, 0:8320].rearrange("p (t n) -> p t n", t=NJT)
    v_ones = U2[:, 0:8320].rearrange("p (a e) -> p a e", e=65)[:, :, 64:65]
    obuf = U2[:, 8320:16512].rearrange("p (c n) -> p c n", c=4)
    x1buf = U2[:, 0:16384].rearrange("p (t n) -> p t n", t=NTT)
    xa_v = U3[:, 0:4096].bitcast(F32).rearrange("p (a n) -> p a n", a=2)
    wfst = U3[:, 0:4096].bitcast(F32).rearrange("p (a n) -> p a n", a=2)
    xfb = U3[:, 4096:12288].rearrange("p (a c n) -> p a c n", a=2, c=8)
    eg_v = U3[:, 12288:16384].rearrange("p (a n) -> p a n", a=2)
    sb_v = U3[:].rearrange("p (t n) -> p t n", t=NTT)

    identB = e(nc.sbuf_tensor("identB", [128, 128], BF16))
    identF = e(nc.sbuf_tensor("identF", [128, 128], F32))
    iota_a = U1[:, 0:256].bitcast(I32)
    iota_b = U1[:, 256:512].bitcast(I32)
    gamb = e(nc.sbuf_tensor("gamb", [128, D], BF16))
    betb = e(nc.sbuf_tensor("betb", [128, D], BF16))
    ones_col = e(nc.sbuf_tensor("ones_col", [128, 1], BF16))
    ones_rb = e(nc.sbuf_tensor("ones_rb", [1, 128], BF16))
    ones_rf = e(nc.sbuf_tensor("ones_rf", [1, 128], F32))
    oneblkA = e(nc.sbuf_tensor("oneblkA", [1, 128], F32))
    oneblkB = e(nc.sbuf_tensor("oneblkB", [1, 128], F32))

    bqk_r = e(nc.sbuf_tensor("bqk_r", [1, D], BF16))
    bv_r = e(nc.sbuf_tensor("bv_r", [1, VD], BF16))
    bout_r = e(nc.sbuf_tensor("bout_r", [1, D], BF16))
    fbias_r = e(nc.sbuf_tensor("fbias_r", [1, D], BF16))
    cwqk_r = e(nc.sbuf_tensor("cwqk_r", [1, D], BF16))
    cwv_r = e(nc.sbuf_tensor("cwv_r", [1, VD], BF16))

    murow = e(nc.sbuf_tensor("murow", [1, S], BF16))
    rinvrow = e(nc.sbuf_tensor("rinvrow", [1, S], BF16))
    RBP = e(nc.sbuf_tensor("RBP", [128, 2048], BF16))
    rb_v = RBP[:]
    msb_v = RBP[:, 0:1024].rearrange("p (a n) -> p a n", a=2)

    ZR = e(nc.sbuf_tensor("ZR", [1, 2048], F32))
    r0row = ZR[0:1, :].bitcast(BF16)[:, 0:2048]

    MISC = e(nc.sbuf_tensor("MISC", [128, 2048], F32))
    zgt = MISC[0:32, 0:512]
    rzg = MISC[0:32, 512:1024]
    xtok_v = MISC[:, 0:1024].bitcast(BF16).rearrange("p (a n) -> p a n", a=2)
    row_tmp = MISC[0:1, 1024:2048]
    xdt_v = MISC[:, 1024:1536].bitcast(BF16).rearrange("p (c n) -> p c n", c=8)
    xdt2_v = MISC[:, 1536:2048].bitcast(BF16).rearrange("p (c n) -> p c n", c=8)

    bnb = e(nc.sbuf_tensor("bnb", [128, NTT, 12], F32))
    negmu0 = e(nc.sbuf_tensor("negmu0", [128, NTT], F32))
    r0 = e(nc.sbuf_tensor("r0", [128, NTT], F32))
    rinv0 = e(nc.sbuf_tensor("rinv0", [128, NTT], F32))
    negmuz = e(nc.sbuf_tensor("negmuz", [128, NTT], F32))
    rz_s = e(nc.sbuf_tensor("rz_s", [128, NTT], F32))
    negmus = e(nc.sbuf_tensor("negmus", [128, NTT], F32))
    rs_s = e(nc.sbuf_tensor("rs_s", [128, NTT], F32))
    negmu1 = e(nc.sbuf_tensor("negmu1", [128, NTT], F32))
    r1_s = e(nc.sbuf_tensor("r1_s", [128, NTT], F32))
    negmuf = e(nc.sbuf_tensor("negmuf", [128, NTT], F32))
    rf_s = e(nc.sbuf_tensor("rf_s", [128, NTT], F32))
    stat_t1 = e(nc.sbuf_tensor("stat_t1", [128, NTT], F32))
    stat_t2 = e(nc.sbuf_tensor("stat_t2", [128, NTT], F32))
    stat_t3 = e(nc.sbuf_tensor("stat_t3", [128, NTT], F32))
    stat_t5 = e(nc.sbuf_tensor("stat_t5", [128, NTT], F32))
    stat_t6 = e(nc.sbuf_tensor("stat_t6", [128, NTT], F32))
    stat_t7 = e(nc.sbuf_tensor("stat_t7", [128, NTT], F32))
    stat_t8 = e(nc.sbuf_tensor("stat_t8", [128, NTT], F32))
    stat_t4 = e(nc.sbuf_tensor("stat_t4", [128, NTT, 4], F32))
    cm_s = e(nc.sbuf_tensor("cm_s", [128, 8], F32))
    cmp_s = e(nc.sbuf_tensor("cmp_s", [128, 4], F32))
    gm_s = e(nc.sbuf_tensor("gm_s", [4, 1], F32))
    negmp = e(nc.sbuf_tensor("negmp", [4, 1], F32))
    negmrow = e(nc.sbuf_tensor("negmrow", [1, 4], F32))
    negmb = e(nc.sbuf_tensor("negmb", [128, 4], F32))

    t2_s = e(nc.sbuf_tensor("t2_s", [128, D], BF16))
    uTB_s = RBP[:, 1024:2048]
    rzgam = RBP[:, 0:1024]
    wu_s = e(nc.sbuf_tensor("wu_s", [128, D], F32))
    qi_v = wu_s[:].bitcast(I32)
    au_s = e(nc.sbuf_tensor("au_s", [128, D], BF16))
    au2_s = e(nc.sbuf_tensor("au2_s", [128, D], BF16))
    mk1_s = e(nc.sbuf_tensor("mk1_s", [128, D], BF16))
    mk2_s = e(nc.sbuf_tensor("mk2_s", [128, D], BF16))
    junk_g = e(nc.sbuf_tensor("junk_g", [1, 8], F32))
    junk_v = e(nc.sbuf_tensor("junk_v", [1, 8], F32))
    junk_s = e(nc.sbuf_tensor("junk_s", [1, 8], F32))
    eps_c = e(nc.sbuf_tensor("eps_c", [128, 1], F32))
    eps2_c = e(nc.sbuf_tensor("eps2_c", [128, 1], F32))
    amax_s = e(nc.sbuf_tensor("amax_s", [128, 32], F32))
    ssc_s = e(nc.sbuf_tensor("ssc_s", [128, 32], F32))
    s8c_s = e(nc.sbuf_tensor("s8c_s", [128, 32], F32))
    rssc_s = e(nc.sbuf_tensor("rssc_s", [128, 32], F32))
    outst = U3[:, 0:4096].bitcast(F32)[:, 0:1024]

    ps_a = e(nc.psum_tensor("ps_a", [128, 2048], F32))
    ps_b = e(nc.psum_tensor("ps_b", [128, 512], F32))
    ps_c = e(nc.psum_tensor("ps_c", [128, 512], F32))
    ps_d = e(nc.psum_tensor("ps_d", [128, 1024], F32))
    ps_a_bf = ps_a[:].bitcast(BF16)

    sem_names = ["sdma", "gdma", "sv", "sa", "stn", "sgc", "ssy", "sout"]
    sems = {n: e(nc.semaphore(n)) for n in sem_names}
    cnt = {n: 0 for n in sem_names}
    ops = {k: [] for k in ["sync", "gpsimd", "vector", "scalar", "tensor"]}

    def inc(name, amt=1):
        cnt[name] += amt
        return cnt[name]

    def op(eng, fn, sem=None, amt=1):
        if sem is None:
            ops[eng].append(lambda E, f=fn: f(E))
            return None
        tgt = inc(sem, amt)
        semh = sems[sem]
        ops[eng].append(lambda E, f=fn, sh=semh, a=amt: f(E).then_inc(sh, a))
        return tgt

    def dma(eng, dst, src, sem=None):
        sem = sem or ("gdma" if eng == "gpsimd" else "sdma")
        return op(eng, lambda E, d=dst, s=src: E.dma_start(out=d, in_=s), sem=sem, amt=16)

    def wait(eng, sem, val):
        semh = sems[sem]
        ops[eng].append(lambda E, sh=semh, v=val: E.wait_ge(sh, v))

    ENG_SEM = {"sync": "ssy", "gpsimd": "sgc", "vector": "sv", "scalar": "sa", "tensor": "stn"}

    def barrier():
        wait("sync", "sdma", cnt["sdma"])
        wait("sync", "sout", cnt["sout"])
        wait("gpsimd", "gdma", cnt["gdma"])
        marks = {}
        marks["sync"] = op("sync", lambda E: E.nop(), sem="ssy")
        marks["gpsimd"] = op("gpsimd", lambda E: E.memset(junk_g[:], 0.0), sem="sgc")
        marks["vector"] = op("vector", lambda E: E.tensor_copy(junk_v[:], junk_g[:]), sem="sv")
        marks["scalar"] = op("scalar", lambda E: E.activation(junk_s[:], junk_v[:], AF.Square), sem="sa")
        marks["tensor"] = op("tensor", lambda E: E.matmul(
            ps_b[0:1, 0:1], ones_rf[0:1, 0:1], ones_rf[0:1, 0:1],
            start=True, stop=True), sem="stn")
        for eng in ENG_SEM:
            for other, sn in ENG_SEM.items():
                if other != eng:
                    wait(eng, sn, marks[other])

    for dst, src in [(bqk_r, bqk), (bv_r, bv), (bout_r, bout), (fbias_r, fbias)]:
        dma("gpsimd", dst[:], src)
    dma("gpsimd", wqk_s[:], wqk.rearrange("(kc p) n -> p kc n", p=128))
    dma("gpsimd", wv_s[:], wv.rearrange("(kc p) n -> p kc n", p=128))
    dma("gpsimd", wout_s[:], wout.rearrange("(kc p) n -> p kc n", p=128))
    op("gpsimd", lambda E: E.iota(iota_a[:], pattern=[[1, 128]], base=0, channel_multiplier=0))
    op("gpsimd", lambda E: E.iota(iota_b[:], pattern=[[0, 128]], base=0, channel_multiplier=1))
    op("gpsimd", lambda E: E.memset(ones_col[:], 1.0))
    op("gpsimd", lambda E: E.memset(ones_rb[:], 1.0))
    op("gpsimd", lambda E: E.memset(ones_rf[:], 1.0))
    op("gpsimd", lambda E: E.memset(oneblkA[:, 0:64], 1.0))
    op("gpsimd", lambda E: E.memset(oneblkA[:, 64:128], 0.0))
    op("gpsimd", lambda E: E.memset(oneblkB[:, 0:64], 0.0))
    op("gpsimd", lambda E: E.memset(oneblkB[:, 64:128], 1.0))
    op("gpsimd", lambda E: E.memset(eps_c[:], EPS))
    op("gpsimd", lambda E: E.memset(eps2_c[:], EPS_S))
    barrier()

    if main_ph == "S0":
        dma("sync", OUTT0[0], MISC[:, 0:1024], sem="sout")
        wait("sync", "sout", cnt["sout"])
        return _finish(nc, ops, st)

    op("vector", lambda E: E.tensor_tensor(identB[:], iota_a[:], iota_b[:], ALU.is_equal))
    op("vector", lambda E: E.tensor_tensor(identF[:], iota_a[:], iota_b[:], ALU.is_equal), sem="sv")
    if main_ph == "Sid":
        dma("sync", OUTT0[0], MISC[:, 0:1024], sem="sout")
        wait("sync", "sout", cnt["sout"])
        return _finish(nc, ops, st)

    for dst, src in [(gamb, gam), (betb, bet)]:
        tl = dma("sync", row_tmp, src)
        wait("tensor", "sdma", tl)
        t = None
        for hh in range(2):
            t = op("tensor", lambda E, hh=hh: E.matmul(
                ps_a[:, hh * 512:(hh + 1) * 512], ones_rf[:],
                row_tmp[:, hh * 512:(hh + 1) * 512], start=True, stop=True),
                sem=("stn" if hh == 1 else None))
        wait("vector", "stn", t)
        tv = op("vector", lambda E, d=dst: E.tensor_copy(d[:], ps_a[:, 0:1024]), sem="sv")
        wait("tensor", "sv", tv)
        wait("sync", "sv", tv)
    if main_ph == "Sgb":
        dma("sync", OUTT0[0], MISC[:, 0:1024], sem="sout")
        wait("sync", "sout", cnt["sout"])
        return _finish(nc, ops, st)

    def colsum(dst_row, w_tile, width):
        nh = width // 512
        t = None
        for hh in range(nh):
            for kc in range(8):
                last = (hh == nh - 1 and kc == 7)
                t = op("tensor", lambda E, hh=hh, kc=kc, w=w_tile: E.matmul(
                    ps_a[0:1, hh * 512:(hh + 1) * 512], ones_col[:],
                    w[:, kc, hh * 512:(hh + 1) * 512], start=(kc == 0), stop=(kc == 7)),
                    sem=("stn" if last else None))
        wait("vector", "stn", t)
        tv = op("vector", lambda E, d=dst_row, n=nh: E.tensor_copy(d[:], ps_a[0:1, 0:512 * n]), sem="sv")
        wait("tensor", "sv", tv)

    colsum(cwqk_r, wqk_s, D)
    colsum(cwv_r, wv_s, VD)
    if main_ph == "S1":
        dma("sync", OUTT0[0], MISC[:, 0:1024], sem="sout")
        wait("sync", "sout", cnt["sout"])
        return _finish(nc, ops, st)

    def emit_quant(win_ap, out_ap=None):
        if out_ap is None:
            out_ap = au_s[:]
        view = lambda t: t.rearrange("p (a b) -> p a b", b=32)
        op("vector", lambda E, w=win_ap: E.tensor_reduce(
            amax_s[:].unsqueeze(2), view(w), AX.X, ALU.max, apply_absolute_value=True))
        op("vector", lambda E: E.drain())
        op("vector", lambda E: E.tensor_scalar(ssc_s[:], amax_s[:], 1.0 / 7.5, 1e-30, ALU.mult, ALU.max))
        op("vector", lambda E: E.drain())
        op("vector", lambda E: E.tensor_scalar(s8c_s[:], ssc_s[:], 0.125, None, ALU.mult))
        op("vector", lambda E: E.reciprocal(rssc_s[:], ssc_s[:]))
        op("vector", lambda E: E.drain())
        rsb = rssc_s[:].unsqueeze(2).broadcast_to([128, 32, 32])
        s8b = s8c_s[:].unsqueeze(2).broadcast_to([128, 32, 32])
        op("vector", lambda E, w=win_ap: E.tensor_tensor(view(wu_s[:]), view(w), rsb, ALU.mult))
        op("vector", lambda E: E.scalar_tensor_tensor(au_s[:], wu_s[:], -1.0, wu_s[:], ALU.mult, ALU.max))
        op("vector", lambda E: E.tensor_scalar(mk1_s[:], au_s[:], 2.0, -4.0, ALU.is_ge, ALU.mult))
        op("vector", lambda E: E.tensor_scalar(mk2_s[:], au_s[:], 4.0, -2.0, ALU.is_ge, ALU.mult))
        op("vector", lambda E: E.scalar_tensor_tensor(mk1_s[:], mk1_s[:], 8.0, mk2_s[:], ALU.add, ALU.add))
        op("vector", lambda E: E.tensor_scalar(mk2_s[:], au_s[:], 4.0, 2.0, ALU.is_ge, ALU.mult))
        op("vector", lambda E: E.tensor_scalar(au_s[:], au_s[:], 2.0, None, ALU.is_ge))
        op("vector", lambda E: E.scalar_tensor_tensor(mk2_s[:], au_s[:], 1.0, mk2_s[:], ALU.add, ALU.add))
        op("vector", lambda E: E.tensor_tensor(qi_v, wu_s[:], mk1_s[:], ALU.mult))
        op("vector", lambda E: E.tensor_tensor(mk1_s[:], qi_v, mk2_s[:], ALU.mult))
        return op("vector", lambda E, o=out_ap: E.tensor_tensor(view(o), view(mk1_s[:]), s8b, ALU.mult), sem="sv")

    _tq_hist = []
    for jc in range(8):
        if jc >= 2:
            wait("sync", "sv", _tq_hist[jc - 2])
        tl = dma("sync", wfst[:, jc % 2, :], wffn.rearrange("(c p) n -> c p n", p=128)[jc])
        wait("vector", "sdma", tl)
        tq = emit_quant(wfst[:, jc % 2, :])
        _tq_hist.append(tq)
        wait("tensor", "sv", tq)
        t = None
        for kc in range(8):
            t = op("tensor", lambda E, kc=kc: E.transpose(
                ps_a_bf[:, kc * 128:(kc + 1) * 128],
                au_s[:, kc * 128:(kc + 1) * 128], identB[:]),
                sem=("stn" if kc == 7 else None))
        wait("vector", "stn", t)
        tv = op("vector", lambda E, jc=jc: E.tensor_copy(
            wffnq[:, :, jc * 128:(jc + 1) * 128],
            ps_a_bf[:, 0:1024].rearrange("p (kc n) -> p kc n", kc=8)), sem="sv")
        wait("tensor", "sv", tv)
    barrier()
    if "wq" in dbg:
        dma("gpsimd", dbg["wq"], wffnq[:].rearrange("p c n -> p (c n)"))
        wait("vector", "gdma", cnt["gdma"])

    def statcombine(negmu, r, eps, rinv=None):
        g = bnb[:].rearrange("p t (a k) -> p t a k", k=3)
        m4 = g[:, :, :, 1]
        v4 = g[:, :, :, 2]
        op("vector", lambda E: E.tensor_reduce(stat_t1[:].unsqueeze(2), m4, AX.X, ALU.add))
        op("vector", lambda E: E.tensor_reduce(stat_t2[:].unsqueeze(2), v4, AX.X, ALU.add))
        op("vector", lambda E: E.tensor_tensor(stat_t4[:], m4, m4, ALU.mult))
        op("vector", lambda E: E.tensor_reduce(stat_t3[:].unsqueeze(2), stat_t4[:], AX.X, ALU.add))
        op("vector", lambda E, n=negmu: E.tensor_scalar(n[:], stat_t1[:], -0.25, None, ALU.mult))
        op("vector", lambda E: E.tensor_scalar(stat_t5[:], stat_t2[:], 1.0 / 1024.0, None, ALU.mult))
        op("vector", lambda E: E.drain())
        op("vector", lambda E: E.scalar_tensor_tensor(stat_t6[:], stat_t3[:], 0.25, stat_t5[:], ALU.mult, ALU.add))
        op("vector", lambda E, n=negmu: E.tensor_tensor(stat_t7[:], n[:], n[:], ALU.mult))
        op("vector", lambda E: E.drain())
        tv = op("vector", lambda E: E.tensor_tensor(stat_t8[:], stat_t6[:], stat_t7[:], ALU.subtract), sem="sv")
        op("vector", lambda E: E.drain())
        wait("scalar", "sv", tv)
        dst_sd = rinv if rinv is not None else stat_t3
        eps_ap = eps_c[:] if eps == EPS else eps2_c[:]
        ta = op("scalar", lambda E, d=dst_sd, ep=eps_ap: E.activation(
            d[:], stat_t8[:], AF.Sqrt, bias=ep, scale=1.0), sem="sa")
        wait("vector", "sa", ta)
        rv = op("vector", lambda E, d=dst_sd, rr=r: E.reciprocal(rr[:], d[:]), sem="sv")
        op("vector", lambda E: E.drain())
        return rv

    def rowify(cols, row_tile):
        t = None
        for tt in range(NTT):
            t = op("tensor", lambda E, tt=tt, c=cols: E.transpose(
                ps_a[0:1, tt * 128:(tt + 1) * 128], c[:, tt:tt + 1], identF[:]),
                sem=("stn" if tt == NTT - 1 else None))
        wait("vector", "stn", t)
        tv = op("vector", lambda E, rt=row_tile: E.tensor_copy(rt, ps_a[0:1, 0:S]), sem="sv")
        wait("tensor", "sv", tv)

    TOKT = x_tok.rearrange("(q p) n -> q p n", p=128)
    OUTT = out.rearrange("(q p) n -> q p n", p=128)

    def dump(name, src_ap):
        if name in dbg:
            dma("gpsimd", dbg[name], src_ap)

    for sq in range(0 if main_ph == "S" else nseq):
        for tt in range(NTT):
            tl = dma("sync", xa_v[:, tt % 2, :], TOKT[sq * NTT + tt])
            wait("vector", "sdma", tl)
            op("vector", lambda E, tt=tt: E.bn_stats(bnb[:, tt, 0:6], xa_v[:, tt % 2, 0:512]))
            tv = op("vector", lambda E, tt=tt: E.bn_stats(bnb[:, tt, 6:12], xa_v[:, tt % 2, 512:1024]), sem="sv")
            wait("sync", "sv", tv)
        if main_ph == "A1":
            continue
        ret = statcombine(negmu0, r0, EPS, rinv=rinv0)
        wait("tensor", "sv", ret)
        if "bnb" in dbg:
            wait("gpsimd", "sv", ret)
            dma("gpsimd", dbg["bnb"], bnb[:].rearrange("p t n -> p (t n)"))
        if "stats" in dbg:
            wait("gpsimd", "sv", ret)
            dma("gpsimd", dbg["stats"].rearrange("p (a n) -> p a n", n=NTT)[:, 0, :], negmu0[:])
            dma("gpsimd", dbg["stats"].rearrange("p (a n) -> p a n", n=NTT)[:, 1, :], r0[:])
            dma("gpsimd", dbg["stats"].rearrange("p (a n) -> p a n", n=NTT)[:, 2, :], rinv0[:])
            dma("gpsimd", dbg["stats"].rearrange("p (a n) -> p a n", n=NTT)[:, 3, :], stat_t1[:])
            dma("gpsimd", dbg["stats"].rearrange("p (a n) -> p a n", n=NTT)[:, 4, :], stat_t2[:])
            dma("gpsimd", dbg["stats"].rearrange("p (a n) -> p a n", n=NTT)[:, 5, :], stat_t3[:])
        if main_ph == "A2":
            continue
        rowify(negmu0, murow[0:1, :])
        rowify(rinv0, rinvrow[0:1, :])
        rowify(r0, r0row)
        if main_ph == "A3":
            continue
        t = None
        for ic in range(NIC):
            t = op("tensor", lambda E, ic=ic: E.matmul(
                ps_a[:, ic * 512:(ic + 1) * 512], ones_rb[:],
                r0row[:, ic * 512:(ic + 1) * 512], start=True, stop=True),
                sem=("stn" if ic == NIC - 1 else None))
        wait("vector", "stn", t)
        tv = op("vector", lambda E: E.tensor_copy(rb_v, ps_a[:, 0:S]), sem="sv")
        wait("tensor", "sv", tv)
        if "rows" in dbg:
            wait("gpsimd", "sv", cnt["sv"])
            rv = dbg["rows"].rearrange("p (a n) -> p a n", n=S)
            dma("gpsimd", rv[:, 0, :], murow[0:1, :])
            dma("gpsimd", rv[:, 1, :], rinvrow[0:1, :])
            dma("gpsimd", rv[:, 2, :], r0row)
        if "rb" in dbg:
            dma("gpsimd", dbg["rb"], rb_v)
        barrier()
        if "B" not in main_ph:
            continue

        XF = x_feat[sq].rearrange("(kc p) n -> p kc n", p=128)
        for ic in range(NIC):
            if ic >= 1:
                wait("gpsimd", "stn", cnt["stn"])
            tl = dma("gpsimd", xfb[:, ic % 2, :, :], XF[:, :, ic * 512:(ic + 1) * 512])
            wait("tensor", "gdma", tl)
            for ct in range(8):
                pb = ps_b if (ct % 2 == 0) else ps_c
                for kc in range(8):
                    op("tensor", lambda E, kc=kc, ct=ct, ic=ic, pb=pb: E.matmul(
                        pb[:], wqk_s[:, kc, ct * 128:(ct + 1) * 128],
                        xfb[:, ic % 2, kc, :], start=(kc == 0), stop=False))
                op("tensor", lambda E, ct=ct, ic=ic, pb=pb: E.matmul(
                    pb[:], cwqk_r[:, ct * 128:(ct + 1) * 128],
                    murow[:, ic * 512:(ic + 1) * 512], start=False, stop=False))
                tmm = op("tensor", lambda E, ct=ct, ic=ic, pb=pb: E.matmul(
                    pb[:], bqk_r[:, ct * 128:(ct + 1) * 128],
                    rinvrow[:, ic * 512:(ic + 1) * 512], start=False, stop=True), sem="stn")
                wait("vector", "stn", tmm)
                dbuf = q_v if ct < 4 else k_v
                tv = op("vector", lambda E, d=dbuf, dc=ct % 4, ic=ic, pb=pb: E.tensor_tensor(
                    d[:, dc, ic * 512:(ic + 1) * 512], pb[:],
                    rb_v[:, ic * 512:(ic + 1) * 512], ALU.mult), sem="sv")
                wait("tensor", "sv", tv)
            for tl_ in range(4):
                tt = ic * 4 + tl_
                ph = tl_ % 2
                for kc in range(8):
                    op("tensor", lambda E, kc=kc, tl_=tl_, ph=ph, ic=ic: E.matmul(
                        ps_d[:, ph * 512:(ph + 1) * 512],
                        xfb[:, ic % 2, kc, tl_ * 128:(tl_ + 1) * 128],
                        wv_s[:, kc, :], start=(kc == 0), stop=False))
                op("tensor", lambda E, tt=tt, ph=ph: E.matmul(
                    ps_d[:, ph * 512:(ph + 1) * 512], murow[:, tt * 128:(tt + 1) * 128],
                    cwv_r[:], start=False, stop=False))
                tmm = op("tensor", lambda E, tt=tt, ph=ph: E.matmul(
                    ps_d[:, ph * 512:(ph + 1) * 512], rinvrow[:, tt * 128:(tt + 1) * 128],
                    bv_r[:], start=False, stop=True), sem="stn")
                wait("vector", "stn", tmm)
                tv = op("vector", lambda E, tt=tt, ph=ph: E.tensor_scalar(
                    v_v[:, tt, :].rearrange("p (h e) -> p h e", e=65)[:, :, 0:64],
                    ps_d[:, ph * 512:(ph + 1) * 512].rearrange("p (h d) -> p h d", d=64),
                    r0[:, tt:tt + 1], None, ALU.mult), sem="sv")
                wait("tensor", "sv", tv)
        op("gpsimd", lambda E: E.memset(v_ones, 1.0))
        barrier()
        dump("q", q_v.rearrange("p c n -> p (c n)"))
        dump("k", k_v.rearrange("p c n -> p (c n)"))
        dump("v", v_v.rearrange("p c n -> p (c n)"))
        if "C" not in main_ph:
            continue

        for pr in range(NPAIR):
            tmm = None
            for hl in range(2):
                tmm = op("tensor", lambda E, pr=pr, hl=hl: E.matmul(
                    ps_d[:, hl * 512:(hl + 1) * 512],
                    q_v[hl * 64:(hl + 1) * 64, pr, 0:S:16],
                    k_v[hl * 64:(hl + 1) * 64, pr, 0:S:4],
                    start=True, stop=True, tile_position=(hl * 64, 0)),
                    sem=("stn" if hl == 1 else None))
            wait("vector", "stn", tmm)
            op("vector", lambda E, pr=pr: E.reduce_max(
                cm_s[:, 2 * pr:2 * pr + 1], ps_d[:, 0:512], AX.X))
            tv = op("vector", lambda E, pr=pr: E.reduce_max(
                cm_s[:, 2 * pr + 1:2 * pr + 2], ps_d[:, 512:1024], AX.X), sem="sv")
            wait("tensor", "sv", tv)
        evv = cm_s[:].rearrange("p (a b) -> p a b", b=2)
        op("vector", lambda E: E.drain())
        tv = op("vector", lambda E, ev=evv: E.tensor_tensor(
            cmp_s[:].unsqueeze(2), ev[:, :, 0:1], ev[:, :, 1:2], ALU.max), sem="sv")
        op("vector", lambda E: E.drain())
        wait("tensor", "sv", tv)
        t = op("tensor", lambda E: E.transpose(ps_a[0:4, 0:128], cmp_s[:], identF[:]), sem="stn")
        wait("vector", "stn", t)
        op("vector", lambda E: E.reduce_max(gm_s[:], ps_a[0:4, 0:128], AX.X))
        op("vector", lambda E: E.drain())
        tv = op("vector", lambda E: E.tensor_scalar(negmp[:], gm_s[:], -0.125, -2.0, ALU.mult, ALU.add), sem="sv")
        op("vector", lambda E: E.drain())
        wait("tensor", "sv", tv)
        t = op("tensor", lambda E: E.transpose(ps_a[0:1, 256:260], negmp[:], identF[0:4, 0:4]), sem="stn")
        wait("vector", "stn", t)
        tv = op("vector", lambda E: E.tensor_copy(negmrow[:], ps_a[0:1, 256:260]), sem="sv")
        op("vector", lambda E: E.drain())
        wait("tensor", "sv", tv)
        t = op("tensor", lambda E: E.matmul(ps_a[:, 512:516], ones_rf[:], negmrow[:],
                                            start=True, stop=True), sem="stn")
        wait("vector", "stn", t)
        tv = op("vector", lambda E: E.tensor_copy(negmb[:], ps_a[:, 512:516]), sem="sv")
        op("vector", lambda E: E.drain())
        wait("tensor", "sv", tv)
        wait("scalar", "sv", tv)
        if sub_ph == "C1":
            continue

        _nps = 1 if sub_ph == "C2" else NPAIR
        _nis = 1 if sub_ph == "C2" else NIC
        _lcap = 99
        if sub_ph.startswith("L") and sub_ph[1:].isdigit():
            _lcap = int(sub_ph[1:])
        if sub_ph.startswith("P"):
            _lcap = 8 if sub_ph.startswith(("PD", "PC", "PX", "PO", "PZ")) else 1
        _ldone = 0
        _iters = [(p_, i_) for p_ in range(_nps) for i_ in range(_nis)]
        if sub_ph == "LR":
            _iters = [(0, 0), (0, 0)]
            _lcap = 2
        if sub_ph == "LP2":
            _iters = [(2, 0), (2, 1), (2, 2), (2, 3)]
            _lcap = 4
        if sub_ph == "LP2a":
            _iters = [(2, 0)]
            _lcap = 1
        _tzd_hist = []
        for pr, ic in _iters:
            if True:
                if _ldone >= _lcap:
                    continue
                if _ldone > 0 and _ldone % 4 == 0:
                    barrier()
                _ldone += 1
                t_av = None
                for g in range(8):
                    t_sh = [None, None]
                    for hl in range(2):
                        for jl in range(2):
                            jt = g * 2 + jl
                            sl = hl * 2 + jl
                            t_sh[hl] = op("tensor", lambda E, pr=pr, ic=ic, jt=jt, hl=hl, sl=sl: E.matmul(
                                ps_a[:, sl * 512:(sl + 1) * 512],
                                k_v[hl * 64:(hl + 1) * 64, pr, jt * 128:(jt + 1) * 128],
                                q_v[hl * 64:(hl + 1) * 64, pr, ic * 512:(ic + 1) * 512],
                                start=True, stop=True, tile_position=(hl * 64, 0)),
                                sem=("stn" if jl == 1 else None))
                    first = (g == 0)
                    lastg = (g == 7)
                    t_eh = [None, None]
                    for hl in range(2):
                        wait("scalar", "stn", t_sh[hl])
                        t_eh[hl] = op("scalar", lambda E, pr=pr, g=g, hl=hl: E.activation(
                            eg_v[:, g % 2, hl * 1024:(hl + 1) * 1024],
                            ps_a[:, hl * 1024:(hl + 1) * 1024], AF.Exp,
                            bias=negmb[:, pr:pr + 1], scale=0.125), sem="sa")
                    for hl in range(2):
                        wait("tensor", "sa", t_eh[hl])
                        for jl in range(2):
                            jt = g * 2 + jl
                            sl = hl * 2 + jl
                            h = pr * 2 + hl
                            st_ = (first and jl == 0)
                            sp_ = (lastg and jl == 1)
                            last_op = (hl == 1 and jl == 1)
                            ob_ap = ps_b[0:65, :] if hl == 0 else ps_c[0:65, :]
                            t_av = op("tensor", lambda E, jt=jt, sl=sl, h=h, g=g,
                               st_=st_, sp_=sp_, ob=ob_ap: E.matmul(
                                ob, v_v[:, jt, h * 65:(h + 1) * 65],
                                eg_v[:, g % 2, sl * 512:(sl + 1) * 512],
                                start=st_, stop=sp_),
                                sem=("stn" if last_op else None))
                wait("scalar", "stn", t_av)
                idx1 = (pr * 2 + 0) * 4 + ic
                idx2 = (pr * 2 + 1) * 4 + ic
                _zsl = ((pr * NIC + ic) % 2) * 1024
                wait("vector", "stn", cnt["stn"])
                op("vector", lambda E, pr=pr, ic=ic: E.tensor_copy(
                    obuf[0:64, pr, ic * 512:(ic + 1) * 512], ps_b[0:64, :]))
                tv = op("vector", lambda E, pr=pr, ic=ic: E.tensor_copy(
                    obuf[64:128, pr, ic * 512:(ic + 1) * 512], ps_c[0:64, :]), sem="sv")
                if len(_tzd_hist) >= 2:
                    wait("scalar", "gdma", _tzd_hist[-2])
                op("scalar", lambda E, z=_zsl: E.copy(ZR[0:1, z:z + 512], ps_b[64:65, :]))
                ta = op("scalar", lambda E, z=_zsl: E.copy(
                    ZR[0:1, z + 512:z + 1024], ps_c[64:65, :]), sem="sa")
                wait("tensor", "sv", tv)
                wait("tensor", "sa", ta)
                if sub_ph not in ("L2z", "L2n"):
                    wait("gpsimd", "sa", ta)
                    dma("gpsimd", zscr[sq:sq + 1, idx1 * 512:(idx1 + 1) * 512],
                        ZR[0:1, _zsl:_zsl + 512])
                    tzd = dma("gpsimd", zscr[sq:sq + 1, idx2 * 512:(idx2 + 1) * 512],
                              ZR[0:1, _zsl + 512:_zsl + 1024])
                    _tzd_hist.append(tzd)
        if sub_ph in ("C2", "C3") or sub_ph.startswith("L") or sub_ph.startswith("P"):
            continue
        if "obun" in dbg:
            wait("gpsimd", "sv", cnt["sv"])
            dma("gpsimd", dbg["obun"], obuf.rearrange("p c n -> p (c n)"))
            wait("sync", "gdma", cnt["gdma"])
        wait("sync", "gdma", cnt["gdma"])
        t2_ = dma("sync", zgt, zscr[sq:sq + 1, :].rearrange("p (a b) -> (p a) b", a=32))
        wait("vector", "sdma", t2_)
        tv = op("vector", lambda E: E.reciprocal(rzg, zgt), sem="sv")
        wait("sync", "sv", tv)
        t3_ = dma("sync", rzscr[sq:sq + 1, :].rearrange("p (a b) -> (p a) b", a=32), rzg)
        wait("sync", "sdma", t3_)
        if "zq" in dbg:
            dma("sync", dbg["zq"], zgt)
            dma("sync", dbg["rzq"], rzg)
        if "negm" in dbg:
            dma("sync", dbg["negm"], negmb[:])
        if sub_ph.startswith("PZ"):
            for _pi in range(int(sub_ph[2:])):
                op("tensor", lambda E: E.matmul(
                    ps_c[64:65, :], ones_col[:], eg_v[:, 0, 0:512],
                    start=True, stop=True, tile_position=(0, 64)), sem="stn")
            wait("vector", "stn", cnt["stn"])
        elif sub_ph.startswith("PO"):
            for _pi in range(int(sub_ph[2:])):
                qo = (_pi * 64) % 1536
                op("tensor", lambda E, qo=qo: E.matmul(
                    ps_a[:, 0:512], k_v[0:64, 3, qo:qo + 128],
                    q_v[0:64, 3, qo + 64:qo + 576], start=True, stop=True,
                    tile_position=(0, 0)))
            op("tensor", lambda E: E.matmul(ps_b[0:1, 0:1], ones_rf[0:1, 0:1],
                                            ones_rf[0:1, 0:1], start=True, stop=True), sem="stn")
            wait("vector", "stn", cnt["stn"])
        elif sub_ph.startswith("PX"):
            for _pi in range(int(sub_ph[2:])):
                op("scalar", lambda E: E.activation(
                    eg_v[:, 0, :], ps_a[:, 0:2048], AF.Exp,
                    bias=negmb[:, 0:1], scale=0.125))
            op("scalar", lambda E: E.copy(junk_s[:], junk_v[:]), sem="sa")
        elif sub_ph.startswith("PC"):
            for _pi in range(int(sub_ph[2:])):
                op("scalar", lambda E: E.copy(ZR[0:1, 0:512], ps_c[0:1, :]))
                op("scalar", lambda E: E.copy(ZR[0:1, 512:1024], ps_c[64:65, :]))
            op("scalar", lambda E: E.copy(junk_s[:], junk_v[:]), sem="sa")
        elif sub_ph.startswith("PD"):
            for _pi in range(int(sub_ph[2:])):
                dma("gpsimd", junk_v[:], zscr[0:1, 0:8])
            wait("gpsimd", "gdma", cnt["gdma"])
        elif sub_ph.startswith("PS"):
            for _pi in range(int(sub_ph[2:])):
                op("scalar", lambda E: E.activation(junk_s[:], junk_v[:], AF.Square), sem="sa")
                op("tensor", lambda E: E.matmul(ps_b[0:1, 0:1], ones_rf[0:1, 0:1],
                                                ones_rf[0:1, 0:1], start=True, stop=True), sem="stn")
        elif sub_ph.startswith("PW"):
            for _pi in range(int(sub_ph[2:])):
                wait("tensor", "sv", cnt["sv"])
                wait("scalar", "stn", cnt["stn"])
                wait("vector", "sa", cnt["sa"])
        elif sub_ph.startswith("PT"):
            for _pi in range(int(sub_ph[2:])):
                op("tensor", lambda E: E.matmul(ps_b[0:1, 0:1], ones_rf[0:1, 0:1],
                                                ones_rf[0:1, 0:1], start=True, stop=True))
            op("tensor", lambda E: E.matmul(ps_b[0:1, 0:1], ones_rf[0:1, 0:1],
                                            ones_rf[0:1, 0:1], start=True, stop=True), sem="stn")
        elif sub_ph.startswith("PA"):
            for _pi in range(int(sub_ph[2:])):
                op("scalar", lambda E: E.activation(junk_s[:], junk_v[:], AF.Square))
            op("scalar", lambda E: E.activation(junk_s[:], junk_v[:], AF.Square), sem="sa")
        elif sub_ph.startswith("P"):
            for _pi in range(int(sub_ph[1:])):
                op("vector", lambda E: E.tensor_copy(junk_v[:], junk_g[:]))
            op("vector", lambda E: E.tensor_copy(junk_v[:], junk_g[:]), sem="sv")
        wait("gpsimd", "sdma", cnt["sdma"])
        _cap = 99
        if sub_ph.startswith("Cn"):
            _cap = int(sub_ph[2:])
        _done = 0
        _tvn_hist = []
        for pr in range(0 if sub_ph == "C4" else NPAIR):
            for ic in range(NIC):
                if _done >= _cap:
                    continue
                _done += 1
                idx1 = (pr * 2 + 0) * 4 + ic
                idx2 = (pr * 2 + 1) * 4 + ic
                ph = (pr * NIC + ic) % 2
                rzv = rzscr[sq:sq + 1, :].rearrange("p (a b) -> (p a) b", a=32)
                if len(_tvn_hist) >= 2:
                    wait("gpsimd", "sv", _tvn_hist[-2])
                dma("gpsimd", msb_v[:, ph, :][0:64, :],
                    rzv[idx1:idx1 + 1, :].broadcast_to([64, 512]))
                tg = dma("gpsimd", msb_v[:, ph, :][64:128, :],
                         rzv[idx2:idx2 + 1, :].broadcast_to([64, 512]))
                wait("vector", "gdma", tg)
                tv = op("vector", lambda E, pr=pr, ic=ic, ph=ph: E.tensor_tensor(
                    obuf[:, pr, ic * 512:(ic + 1) * 512],
                    obuf[:, pr, ic * 512:(ic + 1) * 512],
                    msb_v[:, ph, :], ALU.mult), sem="sv")
                _tvn_hist.append(tv)
        barrier()
        dump("obuf", obuf.rearrange("p c n -> p (c n)"))
        if "D" not in main_ph:
            continue

        _tz_hist = []
        for tt in range(NTT):
            pd1 = ps_d if tt % 2 == 0 else ps_a
            if tt >= 2:
                wait("tensor", "sv", _tz_hist[tt - 2])
            for vc in range(4):
                for hh in range(2):
                    op("tensor", lambda E, vc=vc, hh=hh, tt=tt, pd=pd1: E.matmul(
                        pd[:, hh * 512:(hh + 1) * 512],
                        obuf[:, vc, tt * 128:(tt + 1) * 128],
                        wout_s[:, vc, hh * 512:(hh + 1) * 512],
                        start=(vc == 0), stop=False))
            t_z = None
            for hh in range(2):
                t_z = op("tensor", lambda E, hh=hh, pd=pd1: E.matmul(
                    pd[:, hh * 512:(hh + 1) * 512], ones_rb[:],
                    bout_r[:, hh * 512:(hh + 1) * 512], start=False, stop=True),
                    sem=("stn" if hh == 1 else None))
            wait("vector", "stn", t_z)
            op("vector", lambda E, tt=tt, pd=pd1: E.bn_stats(bnb[:, tt, 0:6], pd[:, 0:512]))
            op("vector", lambda E, tt=tt, pd=pd1: E.bn_stats(bnb[:, tt, 6:12], pd[:, 512:1024]))
            tv = op("vector", lambda E, tt=tt, pd=pd1: E.tensor_copy(
                zbuf[:, tt, :], pd[:, 0:1024]), sem="sv")
            _tz_hist.append(tv)
        statcombine(negmuz, rz_s, EPS)
        dump("zbuf", zbuf.rearrange("p t n -> p (t n)"))
        for tt in range(NTT):
            tl = dma("gpsimd", xtok_v[:, tt % 2, :], TOKT[sq * NTT + tt])
            op("vector", lambda E, tt=tt: E.tensor_scalar(
                rzgam[:], gamb[:], rz_s[:, tt:tt + 1], None, ALU.mult))
            op("vector", lambda E, tt=tt: E.scalar_tensor_tensor(
                t2_s[:], zbuf[:, tt, :], negmuz[:, tt:tt + 1], rzgam[:], ALU.add, ALU.mult))
            wait("vector", "gdma", tl)
            op("vector", lambda E, tt=tt: E.tensor_scalar(
                uTB_s[:], xtok_v[:, tt % 2, :], negmu0[:, tt:tt + 1], r0[:, tt:tt + 1],
                ALU.add, ALU.mult))
            op("vector", lambda E: E.tensor_tensor(t2_s[:], t2_s[:], uTB_s[:], ALU.add))
            op("vector", lambda E, tt=tt: E.tensor_tensor(sb_v[:, tt, :], t2_s[:], betb[:], ALU.add))
            op("vector", lambda E, tt=tt: E.bn_stats(bnb[:, tt, 0:6], sb_v[:, tt, 0:512]))
            tv = op("vector", lambda E, tt=tt: E.bn_stats(bnb[:, tt, 6:12], sb_v[:, tt, 512:1024]), sem="sv")
            wait("gpsimd", "sv", tv)
        statcombine(negmus, rs_s, EPS_S)
        dump("sbuf", sb_v.rearrange("p t n -> p (t n)"))
        for tt in range(NTT):
            tl = dma("gpsimd", xtok_v[:, tt % 2, :], TOKT[sq * NTT + tt])
            op("vector", lambda E, tt=tt: E.tensor_scalar(
                t2_s[:], sb_v[:, tt, :], negmus[:, tt:tt + 1], rs_s[:, tt:tt + 1],
                ALU.add, ALU.mult))
            wait("vector", "gdma", tl)
            op("vector", lambda E, tt=tt: E.tensor_tensor(
                x1buf[:, tt, :], xtok_v[:, tt % 2, :], t2_s[:], ALU.add))
            op("vector", lambda E, tt=tt: E.bn_stats(bnb[:, tt, 0:6], x1buf[:, tt, 0:512]))
            tv = op("vector", lambda E, tt=tt: E.bn_stats(bnb[:, tt, 6:12], x1buf[:, tt, 512:1024]), sem="sv")
            wait("gpsimd", "sv", tv)
        statcombine(negmu1, r1_s, EPS)
        dump("x1", x1buf.rearrange("p t n -> p (t n)"))
        tq_p4 = [None] * NTT
        tbn_p4 = [None] * NTT
        tfc_p4 = [None] * NTT
        t_xc_prev = None
        for i in range(NTT + 1):
            if i > 0 and i % 4 == 0:
                barrier()
            if i < NTT:
                tw = op("vector", lambda E, tt=i: E.tensor_scalar(
                    wu_s[:], x1buf[:, tt, :], negmu1[:, tt:tt + 1], r1_s[:, tt:tt + 1],
                    ALU.add, ALU.mult), sem=("sv" if (i == 0 and "w0" in dbg) else None))
                if i == 0 and "w0" in dbg:
                    wait("gpsimd", "sv", tw)
                    dma("gpsimd", dbg["w0"], wu_s[:])
                    wait("vector", "gdma", cnt["gdma"])
                au_i = au_s if i % 2 == 0 else au2_s
                tq_p4[i] = emit_quant(wu_s[:], au_i[:])
                if i == 0 and "xd" in dbg:
                    wait("gpsimd", "sv", tq_p4[i])
                    dma("gpsimd", dbg["xd"], au_i[:])
                    wait("vector", "gdma", cnt["gdma"])
            if i >= 1:
                j = i - 1
                au_j = au_s if j % 2 == 0 else au2_s
                xdtf_j = MISC[:, 1024:1536].bitcast(BF16) if j % 2 == 0 else MISC[:, 1536:2048].bitcast(BF16)
                xdt_j = xdt_v if j % 2 == 0 else xdt2_v
                psd_j = ps_d if j % 2 == 0 else ps_a
                wait("tensor", "sv", tq_p4[j])
                if t_xc_prev is not None:
                    wait("tensor", "sv", t_xc_prev)
                t = None
                for kc in range(8):
                    t = op("tensor", lambda E, kc=kc, a=au_j: E.transpose(
                        ps_a_bf[:, 2048 + kc * 128:2048 + (kc + 1) * 128],
                        a[:, kc * 128:(kc + 1) * 128], identB[:]),
                        sem=("stn" if kc == 7 else None))
                wait("vector", "stn", t)
                t_xc = op("vector", lambda E, d=xdtf_j: E.tensor_copy(d, ps_a_bf[:, 2048:3072]), sem="sv")
                t_xc_prev = t_xc
                wait("tensor", "sv", t_xc)
                if j >= 2:
                    wait("tensor", "sv", tbn_p4[j - 2])
                    wait("tensor", "sv", tfc_p4[j - 2])
                for kc in range(8):
                    for hh in range(2):
                        op("tensor", lambda E, kc=kc, hh=hh, x=xdt_j, pd=psd_j: E.matmul(
                            pd[:, hh * 512:(hh + 1) * 512],
                            x[:, kc, :],
                            wffnq[:, kc, hh * 512:(hh + 1) * 512],
                            start=(kc == 0), stop=False))
                t_f = None
                for hh in range(2):
                    t_f = op("tensor", lambda E, hh=hh, pd=psd_j: E.matmul(
                        pd[:, hh * 512:(hh + 1) * 512], ones_rb[:],
                        fbias_r[:, hh * 512:(hh + 1) * 512], start=False, stop=True),
                        sem=("stn" if hh == 1 else None))
                wait("vector", "stn", t_f)
                op("vector", lambda E, tt=j, pd=psd_j: E.bn_stats(bnb[:, tt, 0:6], pd[:, 0:512]))
                tbn_p4[j] = op("vector", lambda E, tt=j, pd=psd_j: E.bn_stats(
                    bnb[:, tt, 6:12], pd[:, 512:1024]), sem="sv")
                tfc_p4[j] = op("vector", lambda E, tt=j, pd=psd_j: E.tensor_copy(
                    fbuf[:, tt, :], pd[:, 0:1024]), sem="sv")
        statcombine(negmuf, rf_s, EPS)
        dump("fbuf", fbuf.rearrange("p t n -> p (t n)"))
        sout_hist = []
        for tt in range(NTT):
            if tt >= 2:
                wait("vector", "sout", sout_hist[tt - 2])
            op("vector", lambda E, tt=tt: E.tensor_scalar(
                t2_s[:], fbuf[:, tt, :], negmuf[:, tt:tt + 1], rf_s[:, tt:tt + 1],
                ALU.add, ALU.mult))
            tv = op("vector", lambda E, tt=tt: E.tensor_tensor(
                wfst[:, tt % 2, :], x1buf[:, tt, :], t2_s[:], ALU.add), sem="sv")
            wait("sync", "sv", tv)
            sout_hist.append(dma("sync", OUTT[sq * NTT + tt], wfst[:, tt % 2, :], sem="sout"))
        barrier()

    if cnt["sout"] == 0:
        dma("sync", OUTT[0], MISC[:, 0:1024], sem="sout")
    wait("sync", "sout", cnt["sout"])
    wait("gpsimd", "gdma", cnt["gdma"])

    with nc.Block() as blk:
        @blk.sync
        def _(E):
            for f in ops["sync"]:
                f(E)

        @blk.gpsimd
        def _(E):
            for f in ops["gpsimd"]:
                f(E)

        @blk.vector
        def _(E):
            for f in ops["vector"]:
                f(E)

        @blk.scalar
        def _(E):
            for f in ops["scalar"]:
                f(E)

        @blk.tensor
        def _(E):
            for f in ops["tensor"]:
                f(E)

    return nc, st


def _qk_perm():
    q = [h * 128 + 2 * d for h in range(H) for d in range(DH)]
    k = [h * 128 + 2 * d + 1 for h in range(H) for d in range(DH)]
    return np.array(q + k, dtype=np.int64)


def kernel(L0_attn_x, L0_attn_W_qk, L0_attn_b_qk, L0_attn_W_v, L0_attn_b_v,
           L0_attn_W_out, L0_attn_b_out, L0_attn_ln_gamma, L0_attn_ln_beta,
           L0_ffn_weight, L0_ffn_bias):
    x = np.ascontiguousarray(np.asarray(L0_attn_x, dtype=np.float32))
    perm = _qk_perm()
    wqk_p = np.ascontiguousarray(np.asarray(L0_attn_W_qk, np.float32)[:, perm])
    bqk_p = np.ascontiguousarray(np.asarray(L0_attn_b_qk, np.float32)[perm][None, :])
    wv_ = np.ascontiguousarray(np.asarray(L0_attn_W_v, np.float32))
    bv_ = np.ascontiguousarray(np.asarray(L0_attn_b_v, np.float32)[None, :])
    wout_ = np.ascontiguousarray(np.asarray(L0_attn_W_out, np.float32))
    bout_ = np.ascontiguousarray(np.asarray(L0_attn_b_out, np.float32)[None, :])
    gam_ = np.ascontiguousarray(np.asarray(L0_attn_ln_gamma, np.float32)[None, :])
    bet_ = np.ascontiguousarray(np.asarray(L0_attn_ln_beta, np.float32)[None, :])
    wffn_ = np.ascontiguousarray(np.asarray(L0_ffn_weight, np.float32))
    fbias_ = np.ascontiguousarray(np.asarray(L0_ffn_bias, np.float32)[None, :])

    nc, stack = build_program()
    try:
        in_maps = []
        for c in range(8):
            xs = x[c * NSEQ:(c + 1) * NSEQ]
            in_maps.append(dict(
                x_tok=np.ascontiguousarray(xs.reshape(NSEQ * S, D)),
                x_feat=np.ascontiguousarray(xs.transpose(0, 2, 1)),
                wqk=wqk_p, wv=wv_, wout=wout_, wffn=wffn_, bqk=bqk_p, bv=bv_,
                bout=bout_, gam=gam_, bet=bet_, fbias=fbias_))
        res = run_bass_kernel_spmd(nc, in_maps, core_ids=list(range(8)))
    finally:
        stack.close()
    outs = [res.results[c]["out"].reshape(NSEQ, S, D) for c in range(8)]
    return np.concatenate(outs, axis=0)
